# version 05
# speedup vs baseline: 1.0236x; 1.0213x over previous
import numpy as np
import concourse.bass as bass
import concourse.mybir as mybir
from concourse.bass import IndirectOffsetOnAxis
from concourse.bass_utils import run_bass_kernel_spmd

F32 = mybir.dt.float32
BF16 = mybir.dt.bfloat16
I32 = mybir.dt.int32
AF = mybir.ActivationFunctionType
ALU = mybir.AluOpType
AX = mybir.AxisListType

NCORES = 8
B = 4
S = 4096
D = 1024
SLOC = S // NCORES
T = B * SLOC
NDT = D // 128
MAGIC = 12582912.0

ENG_ATTR = {"pe": "tensor", "act": "scalar", "dve": "vector",
            "pool": "gpsimd", "sp": "sync"}


class Prog:

    def __init__(self, nc):
        self.nc = nc
        self.eng = {k: getattr(nc, v) for k, v in ENG_ATTR.items()}
        self.sems = {}
        self.counts = {}
        self.waited = {k: {} for k in ENG_ATTR}
        self._psn = 0
        for name in ("pe", "act", "dve", "pool", "sp_dma", "pool_dma",
                     "act_dma"):
            self.sems[name] = nc.ctx.enter_context(nc.semaphore(f"S_{name}"))
            self.counts[name] = 0

    def sbuf(self, name, shape, dtype):
        return self.nc.ctx.enter_context(self.nc.sbuf_tensor(name, shape, dtype))

    def psum(self, shape, dtype=F32, name=None):
        if name is None:
            name = f"ps{self._psn}"
            self._psn += 1
        return self.nc.ctx.enter_context(
            self.nc.psum_tensor(name, shape, dtype))

    def _wait(self, eng, tok):
        if tok is None:
            return
        sem_name, val = tok
        if val <= 0:
            return
        if sem_name == "pe" and eng == "pe":
            return
        w = self.waited[eng]
        if w.get(sem_name, 0) >= val:
            return
        self.eng[eng].wait_ge(self.sems[sem_name], val)
        w[sem_name] = val

    def emit(self, eng, method, *args, deps=(), inc=True, dma=False, **kw):
        for d in deps:
            self._wait(eng, d)
        ins = getattr(self.eng[eng], method)(*args, **kw)
        stream = eng + "_dma" if dma else eng
        if inc:
            step = 16 if dma else 1
            self.counts[stream] += step
            ins.then_inc(self.sems[stream], step)
            return (stream, self.counts[stream])
        return None

    def _dma_stream(self, out, queue):
        t = out
        while hasattr(t, "tensor"):
            t = t.tensor
        name = "dma." + queue + ":" + t.name
        if name not in self.sems:
            self.sems[name] = self.nc.ctx.enter_context(
                self.nc.semaphore(name))
            self.counts[name] = 0
        return name

    def dma(self, queue, out, in_, deps=(), key=None, **kw):
        for d in deps:
            self._wait(queue, d)
        ins = getattr(self.eng[queue], "dma_start")(out=out, in_=in_, **kw)
        if key is not None:
            stream = "dma." + queue + ":" + key
            if stream not in self.sems:
                self.sems[stream] = self.nc.ctx.enter_context(
                    self.nc.semaphore(stream))
                self.counts[stream] = 0
        else:
            stream = self._dma_stream(out, queue)
        self.counts[stream] += 16
        ins.then_inc(self.sems[stream], 16)
        return (stream, self.counts[stream])

    def idma(self, out, out_offset, in_, in_offset, deps=(), key="idma",
             **kw):
        for d in deps:
            self._wait("pool", d)
        ins = self.eng["pool"].indirect_dma_start(out, out_offset, in_,
                                                  in_offset, **kw)
        stream = "dma.pool:" + key
        if stream not in self.sems:
            self.sems[stream] = self.nc.ctx.enter_context(
                self.nc.semaphore(stream))
            self.counts[stream] = 0
        self.counts[stream] += 16
        ins.then_inc(self.sems[stream], 16)
        return (stream, self.counts[stream])

    def tok(self, stream):
        return (stream, self.counts[stream])


def merge_tok(toks):
    best = {}
    for t in toks:
        if t is None:
            continue
        s, v = t
        if best.get(s, -1) < v:
            best[s] = v
    return [(s, v) for s, v in best.items()]


class Buf:
    __slots__ = ("h", "w", "r")

    def __init__(self, h):
        self.h = h
        self.w = {}
        self.r = {}

    def rdep(self):
        return [(s, v) for s, v in self.w.items()]

    def wdep(self):
        d = dict(self.w)
        for s, v in self.r.items():
            if d.get(s, -1) < v:
                d[s] = v
        return [(s, v) for s, v in d.items()]

    def wrote(self, tok):
        if tok is not None:
            s, v = tok
            if self.w.get(s, -1) < v:
                self.w[s] = v

    def read(self, tok):
        if tok is not None:
            s, v = tok
            if self.r.get(s, -1) < v:
                self.r[s] = v


def ceil_div(a, b):
    return -(-a // b)


TC = 512


class Region:
    __slots__ = ("ap_full", "name")

    def __init__(self, ap_full, name):
        self.ap_full = ap_full
        self.name = name

    def __getitem__(self, idx):
        return self.ap_full[idx]


class Ctx:
    def __init__(self, p: Prog):
        self.p = p
        self.X = [Buf(p.sbuf(f"X{i}", [128, T], F32)) for i in range(NDT)]
        self.XN = [Buf(p.sbuf(f"XN{i}", [128, T + 16], BF16)) for i in range(NDT)]
        self.XC = [Buf(p.sbuf(f"XC{i}", [128, TC], F32)) for i in range(NDT)]
        self.WB = [Buf(p.sbuf(f"WB{i}", [128, 2048], BF16)) for i in range(4)]
        self.WF = [Buf(p.sbuf(f"WF{i}", [128, 1024], F32)) for i in range(2)]
        self.HT = [Buf(p.sbuf(f"HT{i}", [128, T], BF16)) for i in range(3)]
        self.SC = [Buf(p.sbuf(f"SCR{i}", [128, T], F32)) for i in range(2)]
        self.SH = [Buf(p.sbuf(f"SH{i}", [128, 512], F32)) for i in range(2)]
        self.ROW = [Buf(p.sbuf(f"ROW{i}", [4, T], F32)) for i in range(3)]
        self.PERS = Buf(p.sbuf("PERS", [1, 64], F32))
        self._pers_n = 0
        self.ARENA = p.sbuf("ARENA", [128, 3072], F32)
        self._arena_off = 0
        self._arena_bufs = []
        self._arena_fence = []
        self._arena_n = 0
        self.PS = [Buf(p.psum([128, 512])) for _ in range(8)]
        self.ones_bf = p.sbuf("ones_bf", [128, 1], BF16)
        self.ones_f1 = p.sbuf("ones_f1", [1, 128], F32)
        self.ones_row_bf = p.sbuf("ones_row_bf", [1, 512], BF16)
        self.ident_f = p.sbuf("ident_f", [128, 128], F32)
        self.ident_b = p.sbuf("ident_b", [128, 128], BF16)
        self.c11 = {}
        self.c128 = {}
        self._c11_pool = p.sbuf("c11pool", [1, 48], F32)
        self._c128_pool = p.sbuf("c128pool", [128, 16], F32)
        self._c11_n = 0
        self._c128_n = 0
        self.setup_tok = None
        self.drow = Buf(p.nc.dram_tensor("c_drow", [2048], F32,
                                         kind="Internal").ap())

    def scratch(self, shape, dtype=F32):
        pdim, w = shape
        if dtype == F32:
            nf = w
        elif dtype in (BF16, I32):
            nf = ceil_div(w, 2) if dtype == BF16 else w
        else:
            raise ValueError(dtype)
        off = self._arena_off
        self._arena_off += nf + (nf & 1)
        assert self._arena_off <= 3072, "arena overflow"
        ap = self.ARENA[:, off:off + nf]
        if dtype == BF16:
            ap = ap.bitcast(BF16)
        elif dtype == I32:
            ap = ap.bitcast(I32)
        self._arena_n += 1
        b = Buf(Region(ap, f"ARENA_{self._arena_n}"))
        for t in self._arena_fence:
            b.read(t)
            b.wrote(t)
        self._arena_bufs.append(b)
        return b

    def arena_reset(self):
        toks = []
        for b in self._arena_bufs:
            toks += [(s, v) for s, v in b.w.items()]
            toks += [(s, v) for s, v in b.r.items()]
        self._arena_fence = merge_tok(toks + self._arena_fence)
        self._arena_bufs = []
        self._arena_off = 0

    def pers_slot(self):
        i = self._pers_n
        self._pers_n += 1
        assert i < 64
        return self.PERS.h[0:1, i:i + 1]

    def c11t(self, val):
        key = float(val)
        if key not in self.c11:
            i = self._c11_n
            self._c11_n += 1
            assert i < 48
            ap = self._c11_pool[0:1, i:i + 1]
            t = self.p.emit("pool", "memset", ap, key)
            self.c11[key] = (ap, t)
        return self.c11[key]

    def c128t(self, val):
        key = float(val)
        if key not in self.c128:
            i = self._c128_n
            self._c128_n += 1
            assert i < 16
            ap = self._c128_pool[:, i:i + 1]
            t = self.p.emit("pool", "memset", ap, key)
            self.c128[key] = (ap, t)
        return self.c128[key]

    def setup(self):
        p = self.p
        t1 = p.emit("pool", "memset", self.ones_bf[:], 1.0)
        t2 = p.emit("pool", "memset", self.ones_f1[:], 1.0)
        t6 = p.emit("pool", "memset", self.ones_row_bf[:], 1.0)
        t3 = p.emit("pool", "memset", self.ident_f[:], 1.0)
        t4 = p.emit("pool", "affine_select", self.ident_f[:], self.ident_f[:],
                    pattern=[[-1, 128]], compare_op=ALU.is_equal, fill=0.0,
                    base=0, channel_multiplier=1, deps=[t3])
        t5 = p.emit("dve", "tensor_copy", self.ident_b[:], self.ident_f[:],
                    deps=[t4])
        self.setup_tok = merge_tok([t1, t2, t4, t5, t6])


def ln_stats(p, c, src_tiles, n_dt, Tn, eps=1e-5, cast_tiles=None):
    m_row, r_row = c.ROW[0], c.ROW[1]
    casts = cast_tiles if cast_tiles is not None else c.XN
    for i, xb in enumerate(src_tiles):
        dst = casts[i]
        tk = p.emit("dve", "tensor_copy", dst.h[:, 0:Tn], xb.h[:, 0:Tn],
                    deps=xb.rdep() + dst.wdep())
        dst.wrote(tk)
        xb.read(tk)
    sqt = c.HT[2]
    ntc = ceil_div(Tn, 512)
    for tcI in range(ntc):
        t0 = tcI * 512
        t1 = min(Tn, t0 + 512)
        w = t1 - t0
        for stat in range(2):
            ps = c.PS[6 + stat]
            row = (m_row, r_row)[stat]
            mmdeps = ps.wdep() + c.setup_tok
            tk = None
            for i in range(n_dt):
                src = casts[i]
                if stat == 0:
                    rhs = src.h[:, t0:t1]
                    dep_extra = src.rdep()
                else:
                    tq = p.emit("act", "activation", sqt.h[:, t0:t1],
                                src.h[:, t0:t1], AF.Square,
                                deps=src.rdep() + sqt.wdep())
                    sqt.wrote(tq)
                    src.read(tq)
                    rhs = sqt.h[:, t0:t1]
                    dep_extra = [tq]
                tk = p.emit("pe", "matmul", ps.h[0:1, 0:w], c.ones_bf[:], rhs,
                            start=(i == 0), stop=(i == n_dt - 1),
                            deps=(mmdeps if i == 0 else []) + dep_extra,
                            inc=True)
                src.read(tk)
                if stat == 1:
                    sqt.read(tk)
            ps.wrote(tk)
            te = p.emit("dve", "tensor_copy", row.h[0:1, t0:t1],
                        ps.h[0:1, 0:w], deps=ps.rdep() + row.wdep())
            ps.read(te)
            row.wrote(te)
    nD = float(n_dt * 128)
    tm = p.emit("dve", "tensor_scalar_mul", m_row.h[0:1, 0:Tn],
                m_row.h[0:1, 0:Tn], 1.0 / nD, deps=m_row.wdep())
    m_row.wrote(tm)
    t2 = p.emit("dve", "tensor_scalar_mul", r_row.h[0:1, 0:Tn],
                r_row.h[0:1, 0:Tn], 1.0 / nD, deps=r_row.wdep())
    r_row.wrote(t2)
    m2 = c.ROW[2]
    t3 = p.emit("dve", "tensor_mul", m2.h[0:1, 0:Tn], m_row.h[0:1, 0:Tn],
                m_row.h[0:1, 0:Tn], deps=m_row.rdep() + m2.wdep())
    m2.wrote(t3)
    m_row.read(t3)
    t4 = p.emit("dve", "tensor_sub", r_row.h[0:1, 0:Tn], r_row.h[0:1, 0:Tn],
                m2.h[0:1, 0:Tn], deps=[t3] + r_row.wdep())
    r_row.wrote(t4)
    m2.read(t4)
    eps_ap, eps_tok = c.c11t(eps)
    t5 = p.emit("act", "activation", r_row.h[0:1, 0:Tn], r_row.h[0:1, 0:Tn],
                AF.Ln, bias=eps_ap, deps=[t4, eps_tok])
    r_row.wrote(t5)
    t6 = p.emit("act", "activation", r_row.h[0:1, 0:Tn], r_row.h[0:1, 0:Tn],
                AF.Exp, scale=-0.5, deps=[t5])
    r_row.wrote(t6)
    return m_row, r_row


def bcast_row(p, c, row_buf, row_off, Tn, dst_buf, scale=None):
    ntc = ceil_div(Tn, 512)
    for tcI in range(ntc):
        t0 = tcI * 512
        t1 = min(Tn, t0 + 512)
        w = t1 - t0
        ps = c.PS[4 + tcI % 2]
        tk = p.emit("pe", "matmul", ps.h[:, 0:w], c.ones_f1[:],
                    row_buf.h[row_off:row_off + 1, t0:t1],
                    start=True, stop=True,
                    deps=row_buf.rdep() + ps.wdep() + c.setup_tok)
        row_buf.read(tk)
        ps.wrote(tk)
        if scale is None:
            te = p.emit("dve", "tensor_copy", dst_buf.h[:, t0:t1],
                        ps.h[:, 0:w], deps=[tk] + dst_buf.wdep())
        else:
            te = p.emit("dve", "tensor_scalar_mul", dst_buf.h[:, t0:t1],
                        ps.h[:, 0:w], float(scale),
                        deps=[tk] + dst_buf.wdep())
        ps.read(te)
        dst_buf.wrote(te)
    return dst_buf


def ln_full(p, c, src_tiles, out_bf, gain_bias=None, eps=1e-5,
            out_f32=None, Tn=T):
    m_row, r_row = ln_stats(p, c, src_tiles, len(src_tiles), Tn, eps=eps,
                            cast_tiles=out_bf)
    Mb, Rb = c.SC[0], c.SC[1]
    bcast_row(p, c, m_row, 0, Tn, Mb)
    bcast_row(p, c, r_row, 0, Tn, Rb)
    for i, xb in enumerate(src_tiles):
        for tcI in range(ceil_div(Tn, 512)):
            t0 = tcI * 512
            w = min(Tn, t0 + 512) - t0
            tmp = c.XC[tcI % 4]
            t1 = p.emit("dve", "tensor_sub", tmp.h[:, 0:w],
                        xb.h[:, t0:t0 + w], Mb.h[:, t0:t0 + w],
                        deps=xb.rdep() + Mb.rdep() + tmp.wdep())
            tmp.wrote(t1)
            xb.read(t1)
            Mb.read(t1)
            t2 = p.emit("dve", "tensor_mul", out_bf[i].h[:, t0:t0 + w],
                        tmp.h[:, 0:w], Rb.h[:, t0:t0 + w],
                        deps=[t1] + Rb.rdep() + out_bf[i].wdep())
            out_bf[i].wrote(t2)
            tmp.read(t2)
            Rb.read(t2)
    return m_row, r_row


def col_reduce(p, c, src, Tn, out_row, op=ALU.max, absval=True, col0=0,
               psum_banks=(4, 5), scratch=None):
    nblk = ceil_div(Tn, 128)
    tmp = scratch
    for j in range(nblk):
        ps = c.PS[psum_banks[j % 2]]
        tk = p.emit("pe", "transpose", ps.h[:, 0:128],
                    src.h[:, j * 128:(j + 1) * 128], c.ident_f[:],
                    deps=src.rdep() + ps.wdep() + c.setup_tok)
        src.read(tk)
        ps.wrote(tk)
        tr = p.emit("dve", "tensor_reduce", tmp.h[:, j:j + 1], ps.h[:, 0:128],
                    axis=AX.X, op=op, apply_absolute_value=absval,
                    deps=[tk] + tmp.wdep())
        ps.read(tr)
        tmp.wrote(tr)
    ps = c.PS[psum_banks[0]]
    tk = p.emit("pe", "transpose", ps.h[0:nblk, 0:128], tmp.h[:, 0:nblk],
                c.ident_f[:], deps=tmp.rdep() + ps.wdep() + c.setup_tok)
    tmp.read(tk)
    ps.wrote(tk)
    tb = c.SH[1]
    te = p.emit("dve", "tensor_copy", tb.h[0:nblk, 0:128], ps.h[0:nblk, 0:128],
                deps=[tk] + tb.wdep())
    ps.read(te)
    tb.wrote(te)
    td1 = p.dma("pool", c.drow.h[0:Tn].rearrange("(p n) -> p n", p=nblk),
                tb.h[0:nblk, 0:128], deps=[te] + c.drow.wdep())
    tb.read(td1)
    c.drow.wrote(td1)
    td = p.dma("pool", out_row.h[0:1, col0:col0 + Tn],
               c.drow.h[0:Tn].rearrange("(o n) -> o n", o=1),
               deps=[td1] + out_row.wdep())
    c.drow.read(td)
    out_row.wrote(td)
    return out_row


def gemm_ht(p, c, w_dram, n_dt, n_ht, rhs_tiles, rhs_cols, evac_cb,
            wq_pool_cb=None, bank0=0):
    ntc = ceil_div(rhs_cols, 512)
    wcols = n_dt * 128
    for ht in range(n_ht):
        if wq_pool_cb is None:
            wb = c.WB[ht % 2]
            tk = p.dma("pool", wb.h[:, 0:wcols], w_dram[ht],
                       deps=wb.wdep())
            wb.wrote(tk)
        else:
            wf = c.WF[ht % 2]
            tk = p.dma("sp", wf.h[:, 0:wcols], w_dram[ht],
                       deps=wf.wdep())
            wf.wrote(tk)
            wb = c.WB[ht % 2]
            wq_pool_cb(wf, wb, ht)
        for tcI in range(ntc):
            t0 = tcI * 512
            w = min(rhs_cols, t0 + 512) - t0
            ps = c.PS[bank0 + (ht * ntc + tcI) % 2]
            mm = None
            for dt in range(n_dt):
                mm = p.emit("pe", "matmul", ps.h[:, 0:w],
                            wb.h[:, dt * 128:(dt + 1) * 128],
                            rhs_tiles[dt].h[:, t0:t0 + w],
                            start=(dt == 0), stop=(dt == n_dt - 1),
                            deps=(ps.wdep() if dt == 0 else [])
                            + wb.rdep() + rhs_tiles[dt].rdep(),
                            inc=(dt == n_dt - 1))
                rhs_tiles[dt].read(mm)
            wb.read(mm)
            ps.wrote(mm)
            evac_cb(ht, tcI, t0, w, ps)


def gemm_consume(p, c, h_dram, n_ht, w_dram, n_ot, Tn,
                 flush_cb, tile_cb=None, bias_cb=None, h_buf=None):
    assert n_ht % 8 == 0 and n_ot % 4 == 0
    n_g = n_ht // 8
    n_ob = n_ot // 4
    HB = [c.HT[0], c.HT[1]]
    HQ = [c.HT[2], c.HT[2]]
    for ob in range(n_ob):
        for tcb in range(ceil_div(Tn, 1024)):
            t0 = tcb * 1024
            tw = min(Tn, t0 + 1024) - t0
            nw = ceil_div(tw, 512)
            pss = [[c.PS[oi * 2 + h] for h in range(2)] for oi in range(4)]
            for g in range(n_g):
                wbs = [c.WB[(g % 2) * 2], c.WB[(g % 2) * 2 + 1]]
                for half in range(2):
                    tk = p.dma("pool", wbs[half].h[:, 0:2048],
                               w_dram[ob, g][:, half * 2048:(half + 1) * 2048],
                               deps=wbs[half].wdep())
                    wbs[half].wrote(tk)
                for hti in range(8):
                    ht = g * 8 + hti
                    hb = HB[ht % 2]
                    hq = ("pool" if h_dram[ht].dtype != hb.h.dtype
                          else ("sp" if hti % 2 == 0 else "act"))
                    tk = p.dma(hq, hb.h[:, 0:tw],
                               h_dram[ht][:, t0:t0 + tw], deps=hb.wdep()
                               + (h_buf.rdep() if h_buf is not None else []))
                    hb.wrote(tk)
                    if h_buf is not None:
                        h_buf.read(tk)
                    if tile_cb is not None:
                        hsrc = tile_cb(hb, HQ[ht % 2], ht, t0, tw)
                    else:
                        hsrc = hb
                    last = (g == n_g - 1 and hti == 7)
                    for oi in range(4):
                        ot = ob * 4 + oi
                        ci = (hti * 4 + oi) * 128
                        wsl = wbs[ci // 2048]
                        wap = wsl.h[:, ci % 2048:ci % 2048 + 128]
                        for h in range(nw):
                            w = min(512, tw - h * 512)
                            ps = pss[oi][h]
                            first = (g == 0 and hti == 0)
                            stop = last and bias_cb is None
                            mm = p.emit(
                                "pe", "matmul", ps.h[:, 0:w],
                                wap, hsrc.h[:, h * 512:h * 512 + w],
                                start=first, stop=stop,
                                deps=(ps.wdep() if first else [])
                                + wsl.rdep() + hsrc.rdep(),
                                inc=(stop or (oi == 3 and h == nw - 1)))
                            hsrc.read(mm)
                            wsl.read(mm)
                            ps.wrote(mm)
            for oi in range(4):
                ot = ob * 4 + oi
                for h in range(nw):
                    w = min(512, tw - h * 512)
                    ps = pss[oi][h]
                    if bias_cb is not None:
                        mm = bias_cb(ot, t0 + h * 512, w, ps)
                        ps.wrote(mm)
                flush_cb(ot, t0, tw, pss[oi])


def bcast_to_128(p, c, src_ap, src_dep, dst_buf, bank=5):
    ps = c.PS[bank]
    tk = p.emit("pe", "matmul", ps.h[:, 0:1], c.ones_f1[:], src_ap,
                start=True, stop=True,
                deps=src_dep + ps.wdep() + c.setup_tok)
    ps.wrote(tk)
    te = p.emit("dve", "tensor_copy", dst_buf.h[:, 0:1], ps.h[:, 0:1],
                deps=[tk] + dst_buf.wdep())
    ps.read(te)
    dst_buf.wrote(te)
    return dst_buf


def weight_absmean(p, c, w_view, n_chunks, cols, total, mean_slot_buf,
                   mean_off):
    ps = c.PS[6]
    mm = None
    nsub = cols // 512
    for ch in range(n_chunks):
        wf = c.WF[ch % 2]
        if len(w_view.shape) == 4:
            na = w_view.shape[1]
            src_ap = w_view[ch // na, ch % na]
        else:
            src_ap = w_view[ch]
        tk = p.dma("sp", wf.h[:, 0:cols], src_ap, deps=wf.wdep())
        wf.wrote(tk)
        ab = c.HT[ch % 2]
        ta = p.emit("act", "activation", ab.h[:, 0:cols], wf.h[:, 0:cols],
                    AF.Abs, deps=[tk] + ab.wdep())
        ab.wrote(ta)
        wf.read(ta)
        for sb in range(nsub):
            first = (ch == 0 and sb == 0)
            last = (ch == n_chunks - 1 and sb == nsub - 1)
            mm = p.emit("pe", "matmul", ps.h[0:1, 0:512], c.ones_bf[:],
                        ab.h[:, sb * 512:(sb + 1) * 512], start=first,
                        stop=last,
                        deps=(ps.wdep() + c.setup_tok if first else []) + [ta],
                        inc=True)
            ab.read(mm)
    ps.wrote(mm)
    row = c.ROW[2]
    te = p.emit("dve", "tensor_copy", row.h[0:1, 0:512], ps.h[0:1, 0:512],
                deps=[mm] + row.wdep())
    ps.read(te)
    row.wrote(te)
    tr = p.emit("dve", "tensor_reduce", row.h[0:1, 0:1], row.h[0:1, 0:512],
                axis=AX.X, op=ALU.add, deps=[te])
    row.wrote(tr)
    tm = p.emit("dve", "tensor_scalar", mean_slot_buf.h[0:1, mean_off:mean_off + 1],
                row.h[0:1, 0:1], 1.0 / total, 1e-5, ALU.mult, ALU.max,
                deps=[tr] + mean_slot_buf.wdep())
    mean_slot_buf.wrote(tm)
    row.read(tm)


def wq3_pool(p, c, wf, wb, cols, ws128):
    t1 = p.emit("pool", "tensor_scalar", wf.h[:, 0:cols], wf.h[:, 0:cols],
                ws128.h[:, 0:1], MAGIC, ALU.mult, ALU.add,
                deps=wf.wdep() + ws128.rdep())
    wf.wrote(t1)
    ws128.read(t1)
    t2 = p.emit("pool", "tensor_scalar", wf.h[:, 0:cols], wf.h[:, 0:cols],
                -MAGIC, 1.0, ALU.add, ALU.min, deps=[t1])
    wf.wrote(t2)
    t3 = p.emit("pool", "tensor_scalar_max", wb.h[:, 0:cols], wf.h[:, 0:cols],
                -1.0, deps=[t2] + wb.wdep())
    wb.wrote(t3)
    wf.read(t3)
    return t3


def ln_quant_chunks(p, c, amax_row, scale_row, qeps=1e-5):
    m_row, r_row = ln_stats(p, c, c.X, NDT, T, cast_tiles=c.XN)
    Mb, Rb = c.SC[0], c.SC[1]
    bcast_row(p, c, m_row, 0, T, Mb)
    bcast_row(p, c, r_row, 0, T, Rb)
    M = c.SH[0]
    for tcI in range(T // TC):
        t0 = tcI * TC
        for i in range(NDT):
            t1 = p.emit("dve", "tensor_sub", c.XC[i].h[:, 0:TC],
                        c.X[i].h[:, t0:t0 + TC], Mb.h[:, t0:t0 + TC],
                        deps=c.X[i].rdep() + Mb.rdep() + c.XC[i].wdep())
            c.XC[i].wrote(t1)
            c.X[i].read(t1)
            Mb.read(t1)
            t2 = p.emit("dve", "tensor_mul", c.XC[i].h[:, 0:TC],
                        c.XC[i].h[:, 0:TC], Rb.h[:, t0:t0 + TC],
                        deps=[t1])
            c.XC[i].wrote(t2)
            Rb.read(t2)
        tm = p.emit("act", "activation", M.h[:, 0:TC], c.XC[0].h[:, 0:TC],
                    AF.Square, deps=c.XC[0].rdep() + M.wdep())
        M.wrote(tm)
        c.XC[0].read(tm)
        for i in range(1, NDT):
            sq = c.SH[1]
            tq = p.emit("act", "activation", sq.h[:, 0:TC],
                        c.XC[i].h[:, 0:TC], AF.Square,
                        deps=c.XC[i].rdep() + sq.wdep())
            sq.wrote(tq)
            c.XC[i].read(tq)
            tm = p.emit("dve", "tensor_tensor", M.h[:, 0:TC], M.h[:, 0:TC],
                        sq.h[:, 0:TC], ALU.max, deps=[tm, tq])
            M.wrote(tm)
            sq.read(tm)
        col_reduce(p, c, M, TC, amax_row, col0=t0, scratch=c.SH[1])
        tq0 = p.emit("dve", "tensor_scalar", amax_row.h[0:1, t0:t0 + TC],
                     amax_row.h[0:1, t0:t0 + TC], 1e-38, None, ALU.add,
                     deps=amax_row.wdep())
        amax_row.wrote(tq0)
        tq1 = p.emit("act", "activation", amax_row.h[0:1, t0:t0 + TC],
                     amax_row.h[0:1, t0:t0 + TC], AF.Ln, deps=[tq0])
        amax_row.wrote(tq1)
        tq2 = p.emit("act", "activation", amax_row.h[0:1, t0:t0 + TC],
                     amax_row.h[0:1, t0:t0 + TC], AF.Exp, scale=0.5,
                     deps=[tq1])
        amax_row.wrote(tq2)
        ta = p.emit("dve", "tensor_scalar_max", amax_row.h[0:1, t0:t0 + TC],
                    amax_row.h[0:1, t0:t0 + TC], qeps,
                    deps=amax_row.wdep())
        amax_row.wrote(ta)
        ts0 = p.emit("dve", "reciprocal", scale_row.h[0:1, t0:t0 + TC],
                     amax_row.h[0:1, t0:t0 + TC],
                     deps=[ta] + scale_row.wdep())
        scale_row.wrote(ts0)
        amax_row.read(ts0)
        ts = p.emit("dve", "tensor_scalar_mul", scale_row.h[0:1, t0:t0 + TC],
                    scale_row.h[0:1, t0:t0 + TC], 127.0, deps=[ts0])
        scale_row.wrote(ts)
        SCb = c.SC[0]
        bc = bcast_chunk(p, c, scale_row, t0, TC, SCb)
        for i in range(NDT):
            t1 = p.emit("dve", "tensor_mul", c.XC[i].h[:, 0:TC],
                        c.XC[i].h[:, 0:TC], SCb.h[:, 0:TC],
                        deps=c.XC[i].rdep() + c.XC[i].wdep() + SCb.rdep())
            c.XC[i].wrote(t1)
            SCb.read(t1)
            t2 = p.emit("dve", "tensor_scalar", c.XN[i].h[:, t0:t0 + TC],
                        c.XC[i].h[:, 0:TC], MAGIC, -MAGIC, ALU.add, ALU.add,
                        deps=[t1] + c.XN[i].wdep())
            c.XN[i].wrote(t2)
            c.XC[i].read(t2)


def bcast_chunk(p, c, row_buf, t0, width, dst_buf):
    for k in range(ceil_div(width, 512)):
        a = k * 512
        w = min(width, a + 512) - a
        ps = c.PS[4 + k % 2]
        tk = p.emit("pe", "matmul", ps.h[:, 0:w], c.ones_f1[:],
                    row_buf.h[0:1, t0 + a:t0 + a + w], start=True, stop=True,
                    deps=row_buf.rdep() + ps.wdep() + c.setup_tok)
        row_buf.read(tk)
        ps.wrote(tk)
        te = p.emit("dve", "tensor_copy", dst_buf.h[:, a:a + w],
                    ps.h[:, 0:w], deps=[tk] + dst_buf.wdep())
        ps.read(te)
        dst_buf.wrote(te)
    return dst_buf


def l1_ffn(p, c, io, st):
    NH = 32
    first = "ws" not in st
    if first:
        ms = Buf(c.pers_slot())
        mu = Buf(c.pers_slot())
        md = Buf(c.pers_slot())
        weight_absmean(p, c, io["wgm_v"], 32, 1024, 4096 * 1024, ms, 0)
        weight_absmean(p, c, io["wum_v"], 32, 1024, 4096 * 1024, mu, 0)
        weight_absmean(p, c, io["wdm_v"], 32, 1024, 4096 * 1024, md, 0)
        st["ws"] = (ms, mu, md)
    ms, mu, md = st["ws"]

    if "wsg" not in st:
        st["wsg"] = Buf(p.sbuf("l1_wsg", [128, 1], F32))
        st["wsu"] = Buf(p.sbuf("l1_wsu", [128, 1], F32))
        st["wsd"] = Buf(p.sbuf("l1_wsd", [128, 1], F32))
        st["wmg"] = Buf(p.sbuf("l1_wmg", [128, 1], F32))
    wsg, wsu, wsd, wmg = st["wsg"], st["wsu"], st["wsd"], st["wmg"]
    rowt = c.ROW[2]
    for slot, dst in ((ms, wsg), (mu, wsu), (md, wsd)):
        tr = p.emit("dve", "reciprocal", rowt.h[0:1, 1:2], slot.h,
                    deps=slot.rdep() + rowt.wdep())
        rowt.wrote(tr)
        slot.read(tr)
        bcast_to_128(p, c, rowt.h[0:1, 1:2], [tr], dst)
        rowt.read(p.tok("pe"))
    bcast_to_128(p, c, ms.h, ms.rdep(), wmg)
    ms.read(p.tok("pe"))

    amax_row, scale_row = c.ROW[2], c.scratch([1, T], F32)
    ln_quant_chunks(p, c, amax_row, scale_row)

    inv_row = c.ROW[0]
    ti = p.emit("dve", "tensor_scalar_mul", inv_row.h[0:1, 0:T],
                amax_row.h[0:1, 0:T], 1.0 / 127.0,
                deps=amax_row.rdep() + inv_row.wdep())
    inv_row.wrote(ti)
    amax_row.read(ti)
    IVb = c.SC[0]
    bcast_row(p, c, inv_row, 0, T, IVb)

    M = c.SC[1]
    wg_t, wu_t = io["wg_t"], io["wu_t"]

    def stageA():
        ntc = 4
        for ht in range(NH):
            wfg = c.WF[0]
            wfu = c.WF[1]
            tkg = p.dma("pool", wfg.h[:, 0:1024], wg_t[ht], deps=wfg.wdep())
            wfg.wrote(tkg)
            tku = p.dma("pool", wfu.h[:, 0:1024], wu_t[ht], deps=wfu.wdep())
            wfu.wrote(tku)
            wbg, wbu = c.WB[0], c.WB[1]
            wq3_pool(p, c, wfg, wbg, 1024, wsg)
            wq3_pool(p, c, wfu, wbu, 1024, wsu)
            for tcI in range(ntc):
                t0 = tcI * 512
                psg = c.PS[0 + tcI % 2]
                psu = c.PS[2 + tcI % 2]
                for which, wb, ps in (("g", wbg, psg), ("u", wbu, psu)):
                    mm = None
                    for dt in range(NDT):
                        mm = p.emit("pe", "matmul", ps.h[:, 0:512],
                                    wb.h[:, dt * 128:(dt + 1) * 128],
                                    c.XN[dt].h[:, t0:t0 + 512],
                                    start=(dt == 0), stop=(dt == NDT - 1),
                                    deps=(ps.wdep() if dt == 0 else [])
                                    + wb.rdep() + c.XN[dt].rdep(),
                                    inc=(dt == NDT - 1))
                        c.XN[dt].read(mm)
                    wb.read(mm)
                    ps.wrote(mm)
                sg = c.SH[0]
                t1 = p.emit("dve", "tensor_mul", sg.h[:, 0:512],
                            psg.h[:, 0:512], IVb.h[:, t0:t0 + 512],
                            deps=psg.rdep() + IVb.rdep() + sg.wdep())
                sg.wrote(t1)
                psg.read(t1)
                IVb.read(t1)
                t2 = p.emit("act", "activation", sg.h[:, 0:512],
                            sg.h[:, 0:512], AF.Silu, scale=wmg.h[:, 0:1],
                            deps=[t1] + wmg.rdep())
                sg.wrote(t2)
                wmg.read(t2)
                gur = c.XC[tcI % 2]
                t3 = p.emit("dve", "tensor_mul", gur.h[:, 0:512],
                            sg.h[:, 0:512], psu.h[:, 0:512],
                            deps=[t2] + psu.rdep() + gur.wdep())
                gur.wrote(t3)
                sg.read(t3)
                psu.read(t3)
                if ht == 0:
                    tm = p.emit("act", "activation", M.h[:, t0:t0 + 512],
                                gur.h[:, 0:512], AF.Square,
                                deps=[t3] + M.wdep())
                    M.wrote(tm)
                    gur.read(tm)
                else:
                    tqb = p.emit("act", "activation", sg.h[:, 0:512],
                                 gur.h[:, 0:512], AF.Square,
                                 deps=[t3] + sg.wdep())
                    sg.wrote(tqb)
                    gur.read(tqb)
                    tm = p.emit("dve", "tensor_tensor",
                                M.h[:, t0:t0 + 512], M.h[:, t0:t0 + 512],
                                sg.h[:, 0:512], ALU.max,
                                deps=[tqb] + M.wdep())
                    M.wrote(tm)
                    sg.read(tm)
                td = p.dma("sp", io["hd"][ht][:, t0:t0 + 512],
                           gur.h[:, 0:512], deps=gur.rdep(),
                           key=gur.h.name)
                gur.read(td)
                io["hd_buf"].wrote(td)

    stageA()
    qa_row = c.ROW[1]
    col_reduce(p, c, M, T, qa_row, scratch=c.SH[0])
    tsq0 = p.emit("dve", "tensor_scalar", qa_row.h[0:1, 0:T],
                  qa_row.h[0:1, 0:T], 1e-38, None, ALU.add,
                  deps=qa_row.wdep())
    qa_row.wrote(tsq0)
    tsq1 = p.emit("act", "activation", qa_row.h[0:1, 0:T],
                  qa_row.h[0:1, 0:T], AF.Ln, deps=[tsq0])
    qa_row.wrote(tsq1)
    tsq2 = p.emit("act", "activation", qa_row.h[0:1, 0:T],
                  qa_row.h[0:1, 0:T], AF.Exp, scale=0.5, deps=[tsq1])
    qa_row.wrote(tsq2)
    t1 = p.emit("dve", "tensor_mul", qa_row.h[0:1, 0:T], qa_row.h[0:1, 0:T],
                inv_row.h[0:1, 0:T], deps=qa_row.wdep() + inv_row.rdep())
    qa_row.wrote(t1)
    inv_row.read(t1)
    t2 = p.emit("dve", "tensor_scalar", qa_row.h[0:1, 0:T],
                qa_row.h[0:1, 0:T], mu.h, 1e-5, ALU.mult, ALU.max,
                deps=[t1] + mu.rdep())
    qa_row.wrote(t2)
    mu.read(t2)
    es_row = c.ROW[0]
    es_row = scale_row
    tr = p.emit("dve", "reciprocal", es_row.h[0:1, 0:T], qa_row.h[0:1, 0:T],
                deps=[t2] + es_row.wdep())
    es_row.wrote(tr)
    qa_row.read(tr)
    t3 = p.emit("dve", "tensor_mul", es_row.h[0:1, 0:T], es_row.h[0:1, 0:T],
                inv_row.h[0:1, 0:T], deps=[tr] + inv_row.rdep())
    es_row.wrote(t3)
    inv_row.read(t3)
    t4 = p.emit("dve", "tensor_scalar", es_row.h[0:1, 0:T],
                es_row.h[0:1, 0:T], mu.h, 127.0, ALU.mult, ALU.mult,
                deps=[t3] + mu.rdep())
    es_row.wrote(t4)
    mu.read(t4)
    os_row = c.ROW[1]
    t5 = p.emit("dve", "tensor_scalar", os_row.h[0:1, 0:T],
                qa_row.h[0:1, 0:T], md.h, 0.5 / 127.0, ALU.mult, ALU.mult,
                deps=qa_row.rdep() + md.rdep() + os_row.wdep())
    os_row.wrote(t5)
    md.read(t5)
    ESb = c.SC[0]
    OSb = c.SC[1]
    bcast_row(p, c, es_row, 0, T, ESb)
    bcast_row(p, c, os_row, 0, T, OSb)

    def tile_cb(hb, hq, ht, t0, tw):
        raise RuntimeError("handled by custom loader")

    wd_s = io["wd_s"]
    n_g = 4
    for ob in range(2):
        for tcb in range(2):
            t0 = tcb * 1024
            pss = [[c.PS[oi * 2 + h] for h in range(2)] for oi in range(4)]
            final_mm = None
            for g in range(n_g):
                wbs = [c.WB[(g % 2) * 2], c.WB[(g % 2) * 2 + 1]]
                for q in range(4):
                    wf = c.WF[q % 2]
                    tk = p.dma("sp", wf.h[:, 0:1024],
                               wd_s[ob, g][:, q * 1024:(q + 1) * 1024],
                               deps=wf.wdep())
                    wf.wrote(tk)
                    wq3_pool_to(p, c, wf, wbs[q // 2], (q % 2) * 1024,
                                1024, wsd)
                for hti in range(8):
                    ht = g * 8 + hti
                    hq = c.HT[ht % 2]
                    for hh in range(2):
                        hf = c.XC[2 * (ht % 2) + hh]
                        a = t0 + hh * 512
                        tk = p.dma("pool", hf.h[:, 0:512],
                                   io["hd"][ht][:, a:a + 512],
                                   deps=hf.wdep() + io["hd_buf"].rdep())
                        hf.wrote(tk)
                        tq1 = p.emit("dve", "tensor_mul", hf.h[:, 0:512],
                                     hf.h[:, 0:512], ESb.h[:, a:a + 512],
                                     deps=[tk] + ESb.rdep())
                        hf.wrote(tq1)
                        ESb.read(tq1)
                        tq2 = p.emit("dve", "tensor_scalar",
                                     hq.h[:, hh * 512:hh * 512 + 512],
                                     hf.h[:, 0:512], MAGIC, -MAGIC, ALU.add,
                                     ALU.add, deps=[tq1] + hq.wdep())
                        hq.wrote(tq2)
                        hf.read(tq2)
                    last = (g == n_g - 1 and hti == 7)
                    for oi in range(4):
                        ci = (hti * 4 + oi) * 128
                        wsl = wbs[ci // 2048]
                        wap = wsl.h[:, ci % 2048:ci % 2048 + 128]
                        for h in range(2):
                            ps = pss[oi][h]
                            firstm = (g == 0 and hti == 0)
                            lastrd = (oi == 3 and h == 1)
                            mm = p.emit("pe", "matmul", ps.h[:, 0:512],
                                        wap,
                                        hq.h[:, h * 512:h * 512 + 512],
                                        start=firstm, stop=last,
                                        deps=(ps.wdep() if firstm else [])
                                        + wsl.rdep() + hq.rdep(),
                                        inc=(last or lastrd))
                            hq.read(mm)
                            wsl.read(mm)
                            ps.wrote(mm)
            for oi in range(4):
                ot = ob * 4 + oi
                for h in range(2):
                    ps = pss[oi][h]
                    a = t0 + h * 512
                    tmp = c.SH[1]
                    tf = p.emit("dve", "tensor_mul", tmp.h[:, 0:512],
                                ps.h[:, 0:512], OSb.h[:, a:a + 512],
                                deps=ps.rdep() + OSb.rdep() + tmp.wdep())
                    tmp.wrote(tf)
                    ps.read(tf)
                    OSb.read(tf)
                    tx = p.emit("dve", "tensor_add", c.X[ot].h[:, a:a + 512],
                                c.X[ot].h[:, a:a + 512], tmp.h[:, 0:512],
                                deps=[tf] + c.X[ot].wdep())
                    c.X[ot].wrote(tx)
                    tmp.read(tx)


def prologue(p, c, xin_ap):
    for dt in range(NDT):
        tk = p.dma("sp", c.X[dt].h[:, 0:T], xin_ap[dt], deps=c.X[dt].wdep())
        c.X[dt].wrote(tk)


def epilogue(p, c, xout_ap):
    toks = []
    for dt in range(NDT):
        tk = p.dma("sp", xout_ap[dt], c.X[dt].h[:, 0:T], deps=c.X[dt].rdep())
        c.X[dt].read(tk)
        toks.append(tk)
    allt = toks + [(s, v) for s, v in p.counts.items()
                   if s.startswith("dma:")]
    for tk in merge_tok(allt):
        p._wait("sp", tk)
    return toks


def declare_l1_ffn_io(nc, pref="L1f"):
    io = {}
    io["wg_t"] = nc.dram_tensor(f"{pref}_wg_t", [32, 128, 1024], F32,
                                kind="ExternalInput").ap()
    io["wu_t"] = nc.dram_tensor(f"{pref}_wu_t", [32, 128, 1024], F32,
                                kind="ExternalInput").ap()
    io["wd_s"] = nc.dram_tensor(f"{pref}_wd_s", [2, 4, 128, 4096], F32,
                                kind="ExternalInput").ap()
    io["wgm_v"] = io["wg_t"]
    io["wum_v"] = io["wu_t"]
    io["wdm_v"] = io["wd_s"].rearrange("b g p (a c) -> (b g) a p c", c=1024)
    io["hd"] = nc.dram_tensor(f"{pref}_hd", [32, 128, T], F32,
                              kind="Internal").ap()
    io["hd_buf"] = Buf(io["hd"])
    return io


def stage_l1_ffn(inputs, pref="L1f"):
    wg = np.ascontiguousarray(
        inputs["L1_ffn_w_gate"].reshape(32, 128, 8, 128)
        .transpose(0, 3, 2, 1).reshape(32, 128, 1024))
    wu = np.ascontiguousarray(
        inputs["L1_ffn_w_up"].reshape(32, 128, 8, 128)
        .transpose(0, 3, 2, 1).reshape(32, 128, 1024))
    wd = np.ascontiguousarray(
        inputs["L1_ffn_w_down"].reshape(2, 4, 128, 4, 8, 128)
        .transpose(0, 3, 5, 4, 1, 2).reshape(2, 4, 128, 4096))
    return {f"{pref}_wg_t": wg, f"{pref}_wu_t": wu, f"{pref}_wd_s": wd}


def shard_x(x):
    shards = []
    for cc in range(NCORES):
        sh = x[:, cc * SLOC:(cc + 1) * SLOC, :].reshape(T, D)
        fm = np.ascontiguousarray(sh.T.reshape(NDT, 128, T), dtype=np.float32)
        shards.append(fm)
    return shards


def unshard_x(outs):
    x = np.empty((B, S, D), np.float32)
    for cc, fm in enumerate(outs):
        sh = fm.reshape(D, T).T.reshape(B, SLOC, D)
        x[:, cc * SLOC:(cc + 1) * SLOC, :] = sh
    return x


def build_segment(seg):
    nc = bass.Bass("TRN2", target_bir_lowering=False)
    xin = nc.dram_tensor("xin", [NDT, 128, T], F32,
                         kind="ExternalInput").ap()
    xout = nc.dram_tensor("xout", [NDT, 128, T], F32,
                          kind="ExternalOutput").ap()
    p = Prog(nc)
    c = Ctx(p)
    c.setup()
    prologue(p, c, xin)
    stopped = False
    for pos, (name, plan, pref) in enumerate(seg):
        p.cc_plan = plan
        p.cc_idx = 0
        p.cc_prefix = f"cc{pos}"
        try:
            _build_single_body(name, nc, p, c, xout, pref=pref)
        except StopEmission:
            stopped = True
            break
        c.arena_reset()
    if stopped:
        finalize_stop(p)
    epilogue(p, c, xout)
    return nc


def build_single(sub_name, cc_plan=None):
    nc = bass.Bass("TRN2", target_bir_lowering=False)
    xin = nc.dram_tensor("xin", [NDT, 128, T], F32, kind="ExternalInput").ap()
    xout = nc.dram_tensor("xout", [NDT, 128, T], F32,
                          kind="ExternalOutput").ap()
    p = Prog(nc)
    p.cc_plan = cc_plan
    p.cc_idx = 0
    c = Ctx(p)
    c.setup()
    prologue(p, c, xin)
    try:
        _build_single_body(sub_name, nc, p, c, xout)
    except StopEmission:
        finalize_stop(p)
        return nc
    epilogue(p, c, xout)
    return nc


_DEF_PREF = {"l1_ffn": "L1f", "l0_attn": "L0a", "l2_ffn": "L2f",
             "l1_attn": "L1a", "l3_attn": "L3a", "l2_attn": "L2A",
             "l3_ffn": "L3f", "l0_ffn": "L0f"}


def _build_single_body(sub_name, nc, p, c, xout, pref=None):
    q = pref or _DEF_PREF.get(sub_name, "X")
    if sub_name == "l1_ffn":
        io = declare_l1_ffn_io(nc, pref=q)
        l1_ffn(p, c, io, {})
    elif sub_name == "l0_attn":
        io = declare_l0_attn_io(nc, p, c, pref=q)
        l0_attn(p, c, io, {})
    elif sub_name == "l2_ffn":
        io = declare_l2_ffn_io(nc, p, c, 0, pref=q)
        l2_ffn(p, c, io, {})
    elif sub_name == "l1_attn":
        io = declare_l1_attn_io(nc, p, c, pref=q)
        l1_attn(p, c, io, {})
    elif sub_name == "l3_attn":
        io = declare_l3_attn_io(nc, p, c, pref=q)
        l3_attn(p, c, io, {})
    elif sub_name == "l2_attn":
        io = declare_l2_attn_io(nc, p, c, pref=q)
        l2_attn(p, c, io, {})
    elif sub_name == "l3_ffn":
        io = declare_l3_ffn_io(nc, p, c, pref=q)
        l3_ffn(p, c, io, {})
    elif sub_name == "l0_ffn":
        io = declare_l0_ffn_io(nc, p, c, "", pref=q)
        l0_ffn_full(p, c, io, {}, "")
    elif sub_name == "l0_ffn_p1":
        io = declare_l0_ffn_io(nc, p, c, "")
        io["dbg"] = nc.dram_tensor("dbgt", [4, 1, T], F32,
                                   kind="ExternalOutput").ap()
        parts = l0_ffn(p, c, io, {}, "")
        sca = io["sca"]
        d1 = p.dma("sp", io["dbg"][1][0:1, 0:32], sca.h[0:1, 0:32],
                   deps=sca.rdep() + [p.tok("dve"), p.tok("act")],
                   key="dbg1")
        sca.read(d1)
        d2 = p.dma("sp", io["dbg"][2], c.ROW[2].h[0:1, 0:T],
                   deps=c.ROW[2].rdep() + [p.tok("dve")], key="dbg2")
        c.ROW[2].read(d2)
        d3 = p.dma("sp", io["dbg"][3], c.ROW[1].h[0:1, 0:T],
                   deps=c.ROW[1].rdep() + [p.tok("dve")], key="dbg3")
        c.ROW[1].read(d3)
    else:
        raise ValueError(sub_name)


_SUBLAYER_ORDER = ["l0_ffn", "l0_attn", "l0_ffn",
                   "l1_ffn", "l1_attn", "l1_ffn",
                   "l2_ffn", "l2_attn", "l2_ffn",
                   "l3_ffn", "l3_attn", "l3_ffn"]

_PROG_CACHE = {}


def _noise_const():
    import jax
    import jax.numpy as jnp
    cpu = jax.local_devices(backend="cpu")[0]
    with jax.default_device(cpu):
        v = jax.random.normal(jax.random.key(1), (B, S, 8), jnp.float32)
    return np.asarray(v, dtype=np.float32)


_CC_COUNT = {"l0_ffn": 2, "l0_attn": 1, "l1_ffn": 0, "l1_attn": 1,
             "l2_ffn": 1, "l2_attn": 1, "l3_attn": 0, "l3_ffn": 2}


def _segment_plan():
    calls = list(enumerate(_SUBLAYER_ORDER))
    segments = []
    cur = []
    i = 0
    done_cc = {gi: 0 for gi, _ in calls}
    while i < len(calls):
        gi, name = calls[i]
        ncc = _CC_COUNT[name]
        k = done_cc[gi]
        plan = {j: "in" for j in range(k)}
        if k < ncc:
            plan[k] = "out"
            cur.append((gi, name, plan))
            segments.append(cur)
            cur = []
            done_cc[gi] = k + 1
        else:
            cur.append((gi, name, plan))
            i += 1
    if cur:
        segments.append(cur)
    return segments


def kernel(**inputs):
    from concourse.bass_utils import run_bass_kernel_spmd
    stage_fn = {"l0_attn": stage_l0_attn, "l1_ffn": stage_l1_ffn,
                "l1_attn": stage_l1_attn, "l2_ffn": stage_l2_ffn,
                "l2_attn": stage_l2_attn, "l3_ffn": stage_l3_ffn,
                "l3_attn": stage_l3_attn}
    noise = _noise_const()
    stg = {}
    prefs = {}
    for gi, name in enumerate(_SUBLAYER_ORDER):
        prefs[gi] = f"{_DEF_PREF[name]}{gi}"
        if name == "l0_ffn":
            stg[gi] = stage_l0_ffn(inputs, noise, pref=prefs[gi])
        else:
            stg[gi] = stage_fn[name](inputs, pref=prefs[gi])
    shards = shard_x(np.ascontiguousarray(inputs["L0_attn_x"],
                                          dtype=np.float32))
    gathered = {}
    for seg in _segment_plan():
        key = tuple((gi, name, tuple(sorted(plan.items())))
                    for gi, name, plan in seg)
        if key not in _PROG_CACHE:
            _PROG_CACHE[key] = build_segment(
                [(name, plan, prefs[gi]) for gi, name, plan in seg])
        in_maps = []
        for cc in range(NCORES):
            m = {"xin": shards[cc]}
            for pos, (gi, name, plan) in enumerate(seg):
                sg = stg[gi]
                m.update(sg[cc] if isinstance(sg, list) else sg)
                for j, mode in plan.items():
                    if mode == "in":
                        m[f"cc{pos}g{j}"] = gathered[(gi, j)]
            in_maps.append(m)
        res = run_bass_kernel_spmd(_PROG_CACHE[key], in_maps,
                                   core_ids=list(range(NCORES)))
        shards = [np.asarray(res.results[cc]["xout"])
                  for cc in range(NCORES)]
        pos, (gi, name, plan) = len(seg) - 1, seg[-1]
        for j, mode in plan.items():
            if mode == "out":
                gathered[(gi, j)] = np.stack(
                    [np.asarray(res.results[cc][f"cc{pos}s{j}"])
                     for cc in range(NCORES)])
    return unshard_x(shards)


def wq3_pool_to(p, c, wf, wb, off, cols, ws128):
    t1 = p.emit("pool", "tensor_scalar", wf.h[:, 0:cols], wf.h[:, 0:cols],
                ws128.h[:, 0:1], MAGIC, ALU.mult, ALU.add,
                deps=wf.wdep() + ws128.rdep())
    wf.wrote(t1)
    ws128.read(t1)
    t2 = p.emit("pool", "tensor_scalar", wf.h[:, 0:cols], wf.h[:, 0:cols],
                -MAGIC, 1.0, ALU.add, ALU.min, deps=[t1])
    wf.wrote(t2)
    t3 = p.emit("pool", "tensor_scalar_max", wb.h[:, off:off + cols],
                wf.h[:, 0:cols], -1.0, deps=[t2] + wb.wdep())
    wb.wrote(t3)
    wf.read(t3)
    return t3


class StopEmission(Exception):
    pass


def finalize_stop(p):
    allt = [(s, v) for s, v in p.counts.items()]
    for tk in merge_tok(allt):
        p._wait("sp", tk)


def collective(p, kind, op, ins_aps, outs_aps, deps=(), sum_scratch=None,
               sum_psum=None, flat_out=None):
    idx = getattr(p, "cc_idx", 0)
    p.cc_idx = idx + 1
    mode = (getattr(p, "cc_plan", None) or {}).get(idx, "native")
    if mode == "native":
        for d in deps:
            p._wait("pool", d)
        ins = p.eng["pool"].collective_compute(
            kind, op, replica_groups=[list(range(NCORES))],
            ins=ins_aps, outs=outs_aps)
        if "cc" not in p.sems:
            p.sems["cc"] = p.nc.ctx.enter_context(p.nc.semaphore("sem_cc"))
            p.counts["cc"] = 0
        p.counts["cc"] += 16
        ins.then_inc(p.sems["cc"], 16)
        return ("cc", p.counts["cc"])
    in_ap = ins_aps[0]
    pref = getattr(p, "cc_prefix", "cc0")
    if mode == "out":
        ext = p.nc.dram_tensor(f"{pref}s{idx}", list(in_ap.shape),
                               in_ap.dtype, kind="ExternalOutput").ap()
        p.dma("sp", ext, in_ap, deps=list(deps), key=f"{pref}s{idx}")
        raise StopEmission()
    a, q = in_ap.shape
    ext = p.nc.dram_tensor(f"{pref}g{idx}", [NCORES, a, q], in_ap.dtype,
                           kind="ExternalInput").ap()
    if kind == "AllGather":
        return p.dma("sp", outs_aps[0], ext.rearrange("r a q -> (r a) q"),
                     deps=list(deps), key=f"{pref}g{idx}")
    assert sum_scratch is not None and sum_psum is not None
    assert flat_out is not None
    sc = sum_scratch
    scf = sc.h.bitcast(F32) if sc.h.dtype != F32 else sc.h
    N = a * q
    extf = ext.rearrange("r a q -> r (a q)")
    tos = p.emit("pool", "memset", scf[0:8, 1020:1021], 1.0,
                 deps=sc.wdep())
    sc.wrote(tos)
    last = None
    for j0 in range(0, N, 500):
        w = min(500, N - j0)
        tl = p.dma("sp", scf[0:8, 0:w], extf[0:8, j0:j0 + w],
                   deps=list(deps) + sc.wdep(), key=f"{pref}g{idx}")
        sc.wrote(tl)
        mm = p.emit("pe", "matmul", sum_psum.h[0:1, 0:w],
                    scf[0:8, 1020:1021], scf[0:8, 0:w],
                    start=True, stop=True,
                    deps=[tl, tos] + sum_psum.wdep(), inc=True)
        sc.read(mm)
        sum_psum.wrote(mm)
        tc = p.emit("dve", "tensor_copy", scf[0:1, 520:520 + w],
                    sum_psum.h[0:1, 0:w], deps=[mm] + sc.wdep())
        sum_psum.read(tc)
        sc.wrote(tc)
        td = p.dma("sp",
                   flat_out[j0:j0 + w].rearrange("(a q) -> a q", a=1),
                   scf[0:1, 520:520 + w], deps=[tc], key=f"{pref}r{idx}")
        sc.read(td)
        last = td
    return last


def l0_attn(p, c, io, st):
    EPSA = 1e-6

    m_row, r_row = ln_stats(p, c, c.X, NDT, T)
    R2 = c.ROW[2]
    t1 = p.emit("dve", "tensor_mul", R2.h[0:1, 0:T], r_row.h[0:1, 0:T],
                r_row.h[0:1, 0:T], deps=r_row.rdep() + R2.wdep())
    R2.wrote(t1)
    r_row.read(t1)
    t2 = p.emit("dve", "tensor_scalar", R2.h[0:1, 0:T], R2.h[0:1, 0:T],
                -1e-5, 1.0 + 1e-5, ALU.mult, ALU.add, deps=[t1])
    R2.wrote(t2)
    t3 = p.emit("act", "activation", R2.h[0:1, 0:T], R2.h[0:1, 0:T], AF.Ln,
                deps=[t2])
    R2.wrote(t3)
    t4 = p.emit("act", "activation", R2.h[0:1, 0:T], R2.h[0:1, 0:T], AF.Exp,
                scale=-0.5, deps=[t3])
    R2.wrote(t4)
    t5 = p.emit("dve", "tensor_mul", R2.h[0:1, 0:T], R2.h[0:1, 0:T],
                r_row.h[0:1, 0:T], deps=[t4] + r_row.rdep())
    R2.wrote(t5)
    r_row.read(t5)

    Mb, R2b = c.SC[0], c.SC[1]
    bcast_row(p, c, m_row, 0, T, Mb)
    bcast_row(p, c, R2, 0, T, R2b)
    g1, b1 = io["n1g"], io["n1b"]
    for i in range(NDT):
        for tcI in range(4):
            t0 = tcI * 512
            tmp = c.XC[i % 4]
            ta = p.emit("dve", "tensor_sub", tmp.h[:, 0:512],
                        c.X[i].h[:, t0:t0 + 512], Mb.h[:, t0:t0 + 512],
                        deps=c.X[i].rdep() + Mb.rdep() + tmp.wdep())
            tmp.wrote(ta)
            c.X[i].read(ta)
            Mb.read(ta)
            tb = p.emit("dve", "tensor_mul", tmp.h[:, 0:512],
                        tmp.h[:, 0:512], R2b.h[:, t0:t0 + 512],
                        deps=[ta] + R2b.rdep())
            tmp.wrote(tb)
            R2b.read(tb)
            tc_ = p.emit("dve", "tensor_scalar", c.XN[i].h[:, t0:t0 + 512],
                         tmp.h[:, 0:512], g1.h[:, i:i + 1], b1.h[:, i:i + 1],
                         ALU.mult, ALU.add,
                         deps=[tb] + g1.rdep() + b1.rdep() + c.XN[i].wdep())
            c.XN[i].wrote(tc_)
            tmp.read(tc_)
            g1.read(tc_)
            b1.read(tc_)

    KVPACK = io["kvpack"]
    ktb, vtb = io["ktb"], io["vtb"]
    KSUM = io["ksum"]

    def do_pair(i):
        KBc, VBc = c.HT[0], c.HT[1]
        for b in range(B):
            tk = p.emit("dve", "tensor_reduce", KSUM.h[:, b * 8 + i:b * 8 + i + 1],
                        KBc.h[:, b * 512:(b + 1) * 512], axis=AX.X,
                        op=ALU.add, deps=KBc.rdep() + KSUM.wdep())
            KSUM.wrote(tk)
            KBc.read(tk)
        for b in range(B):
            pkv = [c.PS[4], c.PS[5]]
            mmk = [None, None]
            for j in range(4):
                tb_ = b * 4 + j
                tt = p.emit("pe", "transpose",
                            c.PS[6].h[:, 0:256].bitcast(BF16)[:, 0:128],
                            KBc.h[:, tb_ * 128:(tb_ + 1) * 128], c.ident_b[:],
                            deps=KBc.rdep() + c.PS[6].wdep() + c.setup_tok)
                KBc.read(tt)
                c.PS[6].wrote(tt)
                te = p.emit("dve", "tensor_copy", ktb.h[:],
                            c.PS[6].h[:, 0:256].bitcast(BF16)[:, 0:128],
                            deps=[tt] + ktb.wdep())
                c.PS[6].read(te)
                ktb.wrote(te)
                tt2 = p.emit("pe", "transpose",
                             c.PS[7].h[:, 0:256].bitcast(BF16)[:, 0:128],
                             VBc.h[:, tb_ * 128:(tb_ + 1) * 128], c.ident_b[:],
                             deps=VBc.rdep() + c.PS[7].wdep() + c.setup_tok)
                VBc.read(tt2)
                c.PS[7].wrote(tt2)
                te2 = p.emit("dve", "tensor_copy", vtb.h[:],
                             c.PS[7].h[:, 0:256].bitcast(BF16)[:, 0:128],
                             deps=[tt2] + vtb.wdep())
                c.PS[7].read(te2)
                vtb.wrote(te2)
                for hh in range(2):
                    co = hh * 64
                    mmk[hh] = p.emit(
                        "pe", "matmul", pkv[hh].h[0:64, 0:64],
                        ktb.h[:, co:co + 64], vtb.h[:, co:co + 64],
                        start=(j == 0), stop=(j == 3),
                        deps=(pkv[hh].wdep() if j == 0 else [])
                        + [te, te2], inc=True)
                    ktb.read(mmk[hh])
                    vtb.read(mmk[hh])
            for hh in range(2):
                h = 2 * i + hh
                pkv[hh].wrote(mmk[hh])
                col = (b * 8 + h // 2) * 64
                pr = hh * 64
                te = p.emit("dve", "tensor_copy",
                            KVPACK.h[pr:pr + 64, col:col + 64],
                            pkv[hh].h[0:64, 0:64],
                            deps=[mmk[hh]] + KVPACK.wdep())
                pkv[hh].read(te)
                KVPACK.wrote(te)

    def qkv_evac(ht, tcI, t0, w, ps):
        kind = io["qkv_order"][ht]
        which, i = kind
        if which == 'q':
            sh = c.SH[0]
            ta = p.emit("dve", "tensor_scalar_min", sh.h[:, 0:w],
                        ps.h[:, 0:w], 0.0, deps=ps.rdep() + sh.wdep())
            sh.wrote(ta)
            ps.read(ta)
            tb = p.emit("act", "activation", sh.h[:, 0:w], sh.h[:, 0:w],
                        AF.Exp, deps=[ta])
            sh.wrote(tb)
            sh2 = c.SH[1]
            tc_ = p.emit("dve", "tensor_scalar_max", sh2.h[:, 0:w],
                         ps.h[:, 0:w], 0.0, deps=ps.rdep() + sh2.wdep())
            sh2.wrote(tc_)
            ps.read(tc_)
            qb = c.HT[2]
            td = p.emit("dve", "tensor_add", qb.h[:, t0:t0 + w],
                        sh.h[:, 0:w], sh2.h[:, 0:w],
                        deps=[tb, tc_] + qb.wdep())
            qb.wrote(td)
            sh.read(td)
            sh2.read(td)
            if t0 + w == T:
                ts = p.dma("sp", io["qd"][i], qb.h[:, 0:T],
                           deps=qb.rdep(), key="qspill")
                qb.read(ts)
                io["qd_buf"].wrote(ts)
        elif which == 'k':
            sh = c.SH[0]
            ta = p.emit("dve", "tensor_scalar_min", sh.h[:, 0:w],
                        ps.h[:, 0:w], 0.0, deps=ps.rdep() + sh.wdep())
            sh.wrote(ta)
            ps.read(ta)
            tb = p.emit("act", "activation", sh.h[:, 0:w], sh.h[:, 0:w],
                        AF.Exp, deps=[ta])
            sh.wrote(tb)
            sh2 = c.SH[1]
            tc_ = p.emit("dve", "tensor_scalar_max", sh2.h[:, 0:w],
                         ps.h[:, 0:w], 0.0, deps=ps.rdep() + sh2.wdep())
            sh2.wrote(tc_)
            ps.read(tc_)
            td = p.emit("dve", "tensor_add", c.HT[0].h[:, t0:t0 + w],
                        sh.h[:, 0:w], sh2.h[:, 0:w],
                        deps=[tb, tc_] + c.HT[0].wdep())
            c.HT[0].wrote(td)
            sh.read(td)
            sh2.read(td)
        else:
            td = p.emit("dve", "tensor_copy", c.HT[1].h[:, t0:t0 + w],
                        ps.h[:, 0:w], deps=ps.rdep() + c.HT[1].wdep())
            c.HT[1].wrote(td)
            ps.read(td)
            if t0 + w == T:
                do_pair(i)

    gemm_ht(p, c, io["qkv_t"], NDT, 24, c.XN, T, qkv_evac)

    cin, cout = io["cc_in"], io["cc_out"]
    NKV = 128 * 2048
    t1 = p.dma("sp", cin.h[0:NKV].rearrange("(q n) -> q n", q=128),
               KVPACK.h[:, 0:2048], deps=KVPACK.rdep() + cin.wdep())
    KVPACK.read(t1)
    cin.wrote(t1)
    t2 = p.dma("sp", cin.h[NKV:NKV + 4096].rearrange("(q n) -> q n", q=128),
               KSUM.h[:, 0:32], deps=KSUM.rdep() + [t1])
    KSUM.read(t2)
    cin.wrote(t2)
    tcc = collective(p, "AllReduce", ALU.add,
                     [cin.h[0:NKV + 4096].rearrange("(q n) -> q n", q=128)],
                     [cout.h[0:NKV + 4096].rearrange("(q n) -> q n", q=128)],
                     deps=[t1, t2] + cout.wdep(),
                     sum_scratch=c.HT[2], sum_psum=c.PS[7],
                     flat_out=cout.h[0:NKV + 4096])
    cout.wrote(tcc)
    cin.read(tcc)
    tks = p.dma("sp", KSUM.h[:, 0:32],
                cout.h[NKV:NKV + 4096].rearrange("(q n) -> q n", q=128),
                deps=[tcc] + KSUM.wdep())
    cout.read(tks)
    KSUM.wrote(tks)

    def kvd_view(g):
        buf = c.XC[4 + g // 8]
        ap = buf.h[:, 0:512].bitcast(BF16)[:, (g % 8) * 128:(g % 8) * 128 + 128]
        return buf, ap

    cov = cout.h[0:NKV].rearrange("(q a n) -> q a n", q=128, a=32)
    for j in range(4):
        tz = p.emit("pool", "memset",
                    c.XC[4 + j].h[:, 0:512].bitcast(BF16),
                    0.0, deps=c.XC[4 + j].wdep())
        c.XC[4 + j].wrote(tz)
    for b in range(B):
        for ot in range(8):
            g = b * 8 + ot
            buf, ap = kvd_view(g)
            tka = p.dma("pool", ap[0:64, 0:64], cov[0:64, g, :],
                        deps=buf.wdep() + [tcc], key="kvdfill")
            buf.wrote(tka)
            cout.read(tka)
            tkb = p.dma("pool", ap[64:128, 64:128], cov[64:128, g, :],
                        deps=[tka, tcc], key="kvdfill")
            buf.wrote(tkb)
            cout.read(tkb)
    for j in range(4):
        c.XC[4 + j].wrote(p.tok("dma.pool:kvdfill"))

    ZR = io["zr"]
    dent = io["dent"]
    for b in range(B):
        psd = c.PS[4]
        mm = None
        for dt in range(NDT):
            tz1 = p.emit("pool", "memset", dent.h[:], 0.0,
                         deps=dent.wdep())
            dent.wrote(tz1)
            t1_ = p.emit("dve", "tensor_copy",
                         dent.h[0:64, 2 * dt:2 * dt + 1],
                         KSUM.h[0:64, b * 8 + dt:b * 8 + dt + 1],
                         deps=[tz1] + KSUM.rdep())
            dent.wrote(t1_)
            KSUM.read(t1_)
            t2_ = p.emit("dve", "tensor_copy",
                         dent.h[64:128, 2 * dt + 1:2 * dt + 2],
                         KSUM.h[64:128, b * 8 + dt:b * 8 + dt + 1],
                         deps=[tz1] + KSUM.rdep())
            dent.wrote(t2_)
            KSUM.read(t2_)
            qt = c.HT[2]
            tq = p.dma("pool", qt.h[:, 0:512],
                       io["qd"][dt][:, b * 512:(b + 1) * 512],
                       deps=qt.wdep() + io["qd_buf"].rdep())
            qt.wrote(tq)
            dbt = c.SH[0]
            t3_ = p.emit("dve", "tensor_copy",
                         dbt.h[:, 0:16].bitcast(BF16)[:, 0:16],
                         dent.h[:, 0:16], deps=[t1_, t2_] + dbt.wdep())
            dbt.wrote(t3_)
            dent.read(t3_)
            mm = p.emit("pe", "matmul", psd.h[0:16, 0:512],
                        dbt.h[:, 0:16].bitcast(BF16)[:, 0:16],
                        qt.h[:, 0:512], start=(dt == 0), stop=(dt == 7),
                        deps=(psd.wdep() if dt == 0 else []) + [t3_, tq],
                        inc=True)
            dbt.read(mm)
            qt.read(mm)
        psd.wrote(mm)
        zs = c.SH[1]
        ta = p.emit("dve", "tensor_scalar_add", zs.h[0:16, 0:512],
                    psd.h[0:16, 0:512], EPSA, deps=[mm] + zs.wdep())
        zs.wrote(ta)
        psd.read(ta)
        tb = p.emit("dve", "reciprocal", zs.h[0:16, 0:512],
                    zs.h[0:16, 0:512], deps=[ta])
        zs.wrote(tb)
        tc_ = p.emit("dve", "tensor_copy", ZR.h[0:16, b * 512:(b + 1) * 512],
                     zs.h[0:16, 0:512], deps=[tb] + ZR.wdep())
        ZR.wrote(tc_)
        zs.read(tc_)

    SEL = io["sel"]
    tm_ = p.emit("pool", "memset", SEL.h[0:16, 0:1024], 1.0,
                 deps=SEL.wdep())
    SEL.wrote(tm_)
    for ot in range(8):
        ts_ = p.emit("pool", "affine_select",
                     SEL.h[0:16, ot * 128:(ot + 1) * 128],
                     SEL.h[0:16, ot * 128:(ot + 1) * 128],
                     pattern=[[1, 2], [0, 64]], compare_op=ALU.is_equal,
                     fill=0.0, base=2 * ot, channel_multiplier=-1,
                     deps=[tm_])
        SEL.wrote(ts_)

    for b in range(B):
        for ot in range(8):
            qt = c.HT[2]
            tq = p.dma("pool", qt.h[:, 0:512],
                       io["qd"][ot][:, b * 512:(b + 1) * 512],
                       deps=qt.wdep() + io["qd_buf"].rdep())
            qt.wrote(tq)
            kbuf, kap = kvd_view(b * 8 + ot)
            psa = c.PS[(b + ot) % 2]
            mm = p.emit("pe", "matmul", psa.h[:, 0:512],
                        kap, qt.h[:, 0:512],
                        start=True, stop=True,
                        deps=psa.wdep() + kbuf.rdep() + [tq], inc=True)
            qt.read(mm)
            kbuf.read(mm)
            psa.wrote(mm)
            psz = c.PS[2 + (b + ot) % 2]
            mz = p.emit("pe", "matmul", psz.h[:, 0:512],
                        SEL.h[0:16, ot * 128:(ot + 1) * 128],
                        ZR.h[0:16, b * 512:(b + 1) * 512],
                        start=True, stop=True,
                        deps=psz.wdep() + SEL.rdep() + ZR.rdep(), inc=True)
            SEL.read(mz)
            ZR.read(mz)
            psz.wrote(mz)
            zb = c.SH[0]
            te = p.emit("dve", "tensor_copy", zb.h[:, 0:512],
                        psz.h[:, 0:512], deps=[mz] + zb.wdep())
            zb.wrote(te)
            psz.read(te)
            tf = p.emit("dve", "tensor_mul",
                        c.XN[ot].h[:, b * 512:(b + 1) * 512],
                        psa.h[:, 0:512], zb.h[:, 0:512],
                        deps=[te, mm] + c.XN[ot].wdep())
            c.XN[ot].wrote(tf)
            psa.read(tf)
            zb.read(tf)

    Mb2, Rb = c.SC[0], c.SC[1]
    bcast_row(p, c, m_row, 0, T, Mb2)
    bcast_row(p, c, r_row, 0, T, Rb)
    pb = io["projb"]

    def proj_evac(ht, tcI, t0, w, ps):
        sh = c.SH[0]
        ta = p.emit("act", "activation", sh.h[:, 0:w], ps.h[:, 0:w],
                    AF.Identity, bias=pb.h[:, ht:ht + 1],
                    deps=ps.rdep() + pb.rdep() + sh.wdep())
        sh.wrote(ta)
        ps.read(ta)
        pb.read(ta)
        tmp = c.XC[(ht * 4 + tcI) % 4]
        tb = p.emit("dve", "tensor_sub", tmp.h[:, 0:w],
                    c.X[ht].h[:, t0:t0 + w], Mb2.h[:, t0:t0 + w],
                    deps=c.X[ht].rdep() + Mb2.rdep() + tmp.wdep())
        tmp.wrote(tb)
        c.X[ht].read(tb)
        Mb2.read(tb)
        tc_ = p.emit("dve", "tensor_mul", tmp.h[:, 0:w], tmp.h[:, 0:w],
                     Rb.h[:, t0:t0 + w], deps=[tb] + Rb.rdep())
        tmp.wrote(tc_)
        Rb.read(tc_)
        td = p.emit("dve", "tensor_add", tmp.h[:, 0:w], tmp.h[:, 0:w],
                    sh.h[:, 0:w], deps=[tc_, ta])
        tmp.wrote(td)
        sh.read(td)
        ts = p.dma("sp", io["ad"][ht][:, t0:t0 + w], tmp.h[:, 0:w],
                   deps=tmp.rdep(), key=tmp.h.name)
        tmp.read(ts)
        io["ad_buf"].wrote(ts)

    gemm_ht(p, c, io["proj_t"], NDT, NDT, c.XN, T, proj_evac)

    adb = io["ad_buf"]
    m2_row, r2_row = c.ROW[0], c.ROW[1]
    mm_last = None
    for i in range(NDT):
        ab = c.HT[i % 2]
        tk = p.dma("pool", ab.h[:, 0:T], io["ad"][i],
                   deps=ab.wdep() + adb.rdep())
        ab.wrote(tk)
        adb.read(tk)
        sq = c.HT[2]
        tq = p.emit("act", "activation", sq.h[:, 0:T], ab.h[:, 0:T],
                    AF.Square, deps=[tk] + sq.wdep())
        sq.wrote(tq)
        ab.read(tq)
        for tcI in range(4):
            t0 = tcI * 512
            for stat in range(2):
                ps = c.PS[tcI * 2 + stat]
                rb = (ab, sq)[stat]
                mm_last = p.emit(
                    "pe", "matmul", ps.h[0:1, 0:512], c.ones_bf[:],
                    rb.h[:, t0:t0 + 512], start=(i == 0),
                    stop=(i == NDT - 1),
                    deps=(ps.wdep() + c.setup_tok if i == 0 else [])
                    + rb.rdep(), inc=True)
                rb.read(mm_last)
                ps.wrote(mm_last)
    for tcI in range(4):
        t0 = tcI * 512
        for stat in range(2):
            ps = c.PS[tcI * 2 + stat]
            row = (m2_row, r2_row)[stat]
            te = p.emit("dve", "tensor_copy", row.h[0:1, t0:t0 + 512],
                        ps.h[0:1, 0:512], deps=ps.rdep() + row.wdep())
            ps.read(te)
            row.wrote(te)
    tm = p.emit("dve", "tensor_scalar_mul", m2_row.h[0:1, 0:T],
                m2_row.h[0:1, 0:T], 1.0 / D, deps=m2_row.wdep())
    m2_row.wrote(tm)
    t2 = p.emit("dve", "tensor_scalar_mul", r2_row.h[0:1, 0:T],
                r2_row.h[0:1, 0:T], 1.0 / D, deps=r2_row.wdep())
    r2_row.wrote(t2)
    mm2 = c.ROW[2]
    t3 = p.emit("dve", "tensor_mul", mm2.h[0:1, 0:T], m2_row.h[0:1, 0:T],
                m2_row.h[0:1, 0:T], deps=m2_row.rdep() + mm2.wdep())
    mm2.wrote(t3)
    m2_row.read(t3)
    t4 = p.emit("dve", "tensor_sub", r2_row.h[0:1, 0:T], r2_row.h[0:1, 0:T],
                mm2.h[0:1, 0:T], deps=[t3] + r2_row.wdep())
    r2_row.wrote(t4)
    mm2.read(t4)
    eps_ap, eps_tok = c.c11t(1e-5)
    t5 = p.emit("act", "activation", r2_row.h[0:1, 0:T], r2_row.h[0:1, 0:T],
                AF.Ln, bias=eps_ap, deps=[t4, eps_tok])
    r2_row.wrote(t5)
    t6 = p.emit("act", "activation", r2_row.h[0:1, 0:T], r2_row.h[0:1, 0:T],
                AF.Exp, scale=-0.5, deps=[t5])
    r2_row.wrote(t6)

    Mb2b, R2b2 = c.SC[0], c.SC[1]
    bcast_row(p, c, m2_row, 0, T, Mb2b)
    bcast_row(p, c, r2_row, 0, T, R2b2)
    g2, b2 = io["n2g"], io["n2b"]
    for i in range(NDT):
        for tcI in range(4):
            t0 = tcI * 512
            tmp = c.XC[tcI % 4]
            tk = p.dma("pool", tmp.h[:, 0:512], io["ad"][i][:, t0:t0 + 512],
                       deps=tmp.wdep() + adb.rdep())
            tmp.wrote(tk)
            adb.read(tk)
            ta = p.emit("dve", "tensor_sub", tmp.h[:, 0:512],
                        tmp.h[:, 0:512], Mb2b.h[:, t0:t0 + 512],
                        deps=[tk] + Mb2b.rdep())
            tmp.wrote(ta)
            Mb2b.read(ta)
            tb = p.emit("dve", "tensor_mul", tmp.h[:, 0:512],
                        tmp.h[:, 0:512], R2b2.h[:, t0:t0 + 512],
                        deps=[ta] + R2b2.rdep())
            tmp.wrote(tb)
            R2b2.read(tb)
            tc_ = p.emit("dve", "tensor_scalar", c.XN[i].h[:, t0:t0 + 512],
                         tmp.h[:, 0:512], g2.h[:, i:i + 1], b2.h[:, i:i + 1],
                         ALU.mult, ALU.add,
                         deps=[tb] + g2.rdep() + b2.rdep() + c.XN[i].wdep())
            c.XN[i].wrote(tc_)
            tmp.read(tc_)
            g2.read(tc_)
            b2.read(tc_)

    fb1 = io["fc1b"]

    def fc1_evac(ht, tcI, t0, w, ps):
        hb = c.HT[2]
        ta = p.emit("act", "activation", hb.h[:, t0:t0 + w], ps.h[:, 0:w],
                    AF.Gelu, bias=fb1.h[:, ht:ht + 1],
                    deps=ps.rdep() + fb1.rdep() + hb.wdep())
        hb.wrote(ta)
        ps.read(ta)
        fb1.read(ta)
        if t0 + w == T:
            ts = p.dma("sp", io["hd"][ht], hb.h[:, 0:T], deps=hb.rdep(),
                       key="hspill")
            hb.read(ts)
            io["hd_buf"].wrote(ts)

    gemm_ht(p, c, io["fc1_t"], NDT, 32, c.XN, T, fc1_evac)

    fb2 = io["fc2b"]

    def bias_cb(ot, t0, w, ps):
        mm = p.emit("pe", "matmul", ps.h[:, 0:w],
                    fb2.h[0:1, ot * 128:(ot + 1) * 128],
                    c.ones_row_bf[0:1, 0:w], start=False, stop=True,
                    deps=fb2.rdep() + c.setup_tok, inc=True)
        fb2.read(mm)
        return mm

    def flush_cb(ot, t0, tw, ps_pair):
        for h in range(ceil_div(tw, 512)):
            a = t0 + h * 512
            w = min(512, tw - h * 512)
            ps = ps_pair[h]
            tmp = c.XC[h % 4]
            tk = p.dma("pool", tmp.h[:, 0:w], io["ad"][ot][:, a:a + w],
                       deps=tmp.wdep() + adb.rdep())
            tmp.wrote(tk)
            adb.read(tk)
            ta = p.emit("dve", "tensor_add", tmp.h[:, 0:w], tmp.h[:, 0:w],
                        ps.h[:, 0:w], deps=[tk] + ps.rdep())
            tmp.wrote(ta)
            ps.read(ta)
            tb = p.emit("dve", "tensor_add", c.X[ot].h[:, a:a + w],
                        c.X[ot].h[:, a:a + w], tmp.h[:, 0:w],
                        deps=[ta] + c.X[ot].wdep())
            c.X[ot].wrote(tb)
            tmp.read(tb)

    gemm_consume(p, c, io["hd"], 32, io["fc2_s"], NDT, T, flush_cb,
                 bias_cb=bias_cb, h_buf=io["hd_buf"])


def declare_l0_attn_io(nc, p, c, pref="L0a"):
    io = {}
    io["qkv_t"] = nc.dram_tensor(f"{pref}_qkv_t", [24, 128, 1024], F32,
                                 kind="ExternalInput").ap()
    io["proj_t"] = nc.dram_tensor(f"{pref}_proj_t", [8, 128, 1024], F32,
                                  kind="ExternalInput").ap()
    io["fc1_t"] = nc.dram_tensor(f"{pref}_fc1_t", [32, 128, 1024], F32,
                                 kind="ExternalInput").ap()
    io["fc2_s"] = nc.dram_tensor(f"{pref}_fc2_s", [2, 4, 128, 4096], F32,
                                 kind="ExternalInput").ap()
    io["vecs"] = nc.dram_tensor(f"{pref}_vecs", [128, 72], F32,
                                kind="ExternalInput").ap()
    io["fc2b_in"] = nc.dram_tensor(f"{pref}_fc2b", [1, 1024], F32,
                                   kind="ExternalInput").ap()
    io["qd"] = nc.dram_tensor(f"{pref}_qd", [8, 128, T], BF16,
                              kind="Internal").ap()
    io["qd_buf"] = Buf(io["qd"])
    io["ad"] = nc.dram_tensor(f"{pref}_ad", [8, 128, T], F32,
                              kind="Internal").ap()
    io["ad_buf"] = Buf(io["ad"])
    io["hd"] = nc.dram_tensor(f"{pref}_hd", [32, 128, T], BF16,
                              kind="Internal").ap()
    io["hd_buf"] = Buf(io["hd"])
    io["cc_in"] = Buf(nc.dram_tensor(f"{pref}_ccin", [128 * 2048 + 4096],
                                     F32, kind="Internal").ap())
    io["cc_out"] = Buf(nc.dram_tensor(f"{pref}_ccout", [128 * 2048 + 4096],
                                      F32, kind="Internal",
                                      addr_space="Shared").ap())
    io["qkv_order"] = ([("q", i) for i in range(8)]
                       + [kv for i in range(8)
                          for kv in (("k", i), ("v", i))])
    io["kvpack"] = c.SC[0]
    io["ktb"] = c.scratch([128, 128], BF16)
    io["vtb"] = c.scratch([128, 128], BF16)
    io["ksum"] = c.scratch([128, 32], F32)
    io["zr"] = c.HT[0]
    io["dent"] = c.scratch([128, 16], F32)
    io["sel"] = c.HT[1]
    vec = c.scratch([128, 72], F32)
    tk = p.dma("sp", vec.h[:, 0:72], io["vecs"], deps=vec.wdep(),
               key=vec.h.name)
    vec.wrote(tk)
    io["n1g"] = _subbuf(vec, 0, 8)
    io["n1b"] = _subbuf(vec, 8, 8)
    io["n2g"] = _subbuf(vec, 16, 8)
    io["n2b"] = _subbuf(vec, 24, 8)
    io["projb"] = _subbuf(vec, 32, 8)
    io["fc1b"] = _subbuf(vec, 40, 32)
    io["vec"] = vec
    fb2 = c.scratch([1, 1024], BF16)
    tk2 = p.dma("pool", fb2.h[0:1, 0:1024], io["fc2b_in"], deps=fb2.wdep(),
                key=fb2.h.name)
    fb2.wrote(tk2)
    io["fc2b"] = fb2
    return io


def _subbuf(parent, off, n):
    b = Buf(_ColShift(parent.h, off))
    b.w = parent.w
    b.r = parent.r
    return b


class _ColShift:
    def __init__(self, h, off):
        self._h = h
        self._off = off
        self.name = getattr(h, "name", "colshift")

    def __getitem__(self, idx):
        ps, cs = idx
        ns = slice(cs.start + self._off, cs.stop + self._off)
        return self._h[ps, ns]


def stage_l0_attn(inputs, pref="L0a"):
    def tile_od(w, order=None):
        O, Din = w.shape
        no, nd = O // 128, Din // 128
        t = w.reshape(no, 128, nd, 128).transpose(0, 3, 2, 1)
        if order is not None:
            t = t[order]
        return np.ascontiguousarray(t.reshape(no, 128, nd * 128),
                                    dtype=np.float32)

    def tile_slab(w):
        O, H = w.shape
        n_ob, n_g = O // 512, H // 1024
        t = w.reshape(2, 4, 128, n_g, 8, 128).transpose(0, 3, 5, 4, 1, 2)
        return np.ascontiguousarray(t.reshape(2, n_g, 128, 4096),
                                    dtype=np.float32)

    qkv = inputs["L0_attn_qkv_w"]
    order = list(range(8)) + [k for i in range(8) for k in (8 + i, 16 + i)]
    vecs = np.zeros((128, 72), np.float32)
    vecs[:, 0:8] = inputs["L0_attn_norm1_g"].reshape(8, 128).T
    vecs[:, 8:16] = inputs["L0_attn_norm1_b"].reshape(8, 128).T
    vecs[:, 16:24] = inputs["L0_attn_norm2_g"].reshape(8, 128).T
    vecs[:, 24:32] = inputs["L0_attn_norm2_b"].reshape(8, 128).T
    vecs[:, 32:40] = inputs["L0_attn_proj_b"].reshape(8, 128).T
    vecs[:, 40:72] = inputs["L0_attn_fc1_b"].reshape(32, 128).T
    return {
        f"{pref}_qkv_t": tile_od(qkv, order),
        f"{pref}_proj_t": tile_od(inputs["L0_attn_proj_w"]),
        f"{pref}_fc1_t": tile_od(inputs["L0_attn_fc1_w"]),
        f"{pref}_fc2_s": tile_slab(inputs["L0_attn_fc2_w"]),
        f"{pref}_vecs": vecs,
        f"{pref}_fc2b": inputs["L0_attn_fc2_b"].reshape(1, 1024).astype(np.float32),
    }


def l2_ffn(p, c, io, st):
    NH = 32
    if "ws" not in st:
        m1 = Buf(c.pers_slot())
        m2 = Buf(c.pers_slot())
        weight_absmean(p, c, io["w1_t"], 32, 1024, 4096 * 1024, m1, 0)
        weight_absmean(p, c, io["w2m_v"], 32, 1024, 4096 * 1024, m2, 0)
        st["ws"] = (m1, m2)
        st["ws1"] = Buf(p.sbuf("l2_ws1", [128, 1], F32))
        st["ws2"] = Buf(p.sbuf("l2_ws2", [128, 1], F32))
    m1, m2 = st["ws"]
    ws1, ws2 = st["ws1"], st["ws2"]
    rowt = c.ROW[2]
    for slot, dst in ((m1, ws1), (m2, ws2)):
        tr = p.emit("dve", "reciprocal", rowt.h[0:1, 1:2], slot.h,
                    deps=slot.rdep() + rowt.wdep())
        rowt.wrote(tr)
        slot.read(tr)
        bcast_to_128(p, c, rowt.h[0:1, 1:2], [tr], dst)
        rowt.read(p.tok("pe"))

    cin, cout = io["hcc_in"], io["hcc_out"]
    stg = c.SH[0]
    for dt in range(NDT):
        xv = c.X[dt].h[:, 0:T].rearrange("p (b s) -> p b s", b=B)
        for s_ in range(2):
            col = SLOC - 1 if s_ else 0
            tcp = p.emit("dve", "tensor_copy",
                         stg.h[:, s_ * 32 + dt:s_ * 32 + dt + 25:8],
                         xv[:, :, col],
                         deps=c.X[dt].rdep() + stg.wdep())
            stg.wrote(tcp)
            c.X[dt].read(tcp)
    pst = c.PS[4]
    tt_ = p.emit("pe", "transpose", pst.h[0:64, 0:128], stg.h[:, 0:64],
                 c.ident_f[:], deps=stg.rdep() + pst.wdep() + c.setup_tok)
    stg.read(tt_)
    pst.wrote(tt_)
    stg2 = c.SH[1]
    te_ = p.emit("dve", "tensor_copy", stg2.h[0:64, 0:128],
                 pst.h[0:64, 0:128], deps=[tt_] + stg2.wdep())
    pst.read(te_)
    stg2.wrote(te_)
    tdm = p.dma("sp", cin.h[0:8192].rearrange("(a q) -> a q", a=64),
                stg2.h[0:64, 0:128], deps=[te_] + cin.wdep())
    stg2.read(tdm)
    cin.wrote(tdm)
    toks = [tdm]
    tcc = collective(p, "AllGather", ALU.bypass,
                     [cin.h[0:8192].rearrange("(a q) -> a q", a=64)],
                     [cout.h[0:65536].rearrange("(a q) -> a q", a=512)],
                     deps=toks + cout.wdep())
    cout.wrote(tcc)
    cin.read(tcc)

    amax_row, scale_row = c.ROW[2], c.scratch([1, T], F32)
    ln_quant_chunks(p, c, amax_row, scale_row)

    hxg = io["hxg"]
    idx = io["hidx"]
    hmask = io["hmask"]
    grp = io["grp"]
    grpT = io["grpT"]
    cov = cout.h[0:65536].rearrange("(a q) -> a q", a=512)
    tg = p.idma(hxg.h[0:64, 0:128], None, cov,
                IndirectOffsetOnAxis(idx.h[0:64, 0:1], 0),
                deps=[tcc] + hxg.wdep() + io["hidx_tok"] + io["hmask_tok"],
                key="hxg")
    hxg.wrote(tg)
    cout.read(tg)
    tm_ = p.emit("dve", "tensor_scalar_mul", hxg.h[0:64, 0:128],
                 hxg.h[0:64, 0:128], hmask.h[0:64, 0:1],
                 deps=[tg] + hmask.rdep())
    hxg.wrote(tm_)
    hmask.read(tm_)
    s12 = c.SH[0]
    t1 = p.emit("dve", "tensor_reduce", s12.h[0:64, 0:1], hxg.h[0:64, 0:128],
                axis=AX.X, op=ALU.add, deps=[tm_] + s12.wdep())
    s12.wrote(t1)
    hxg.read(t1)
    sq = c.SH[1]
    t2 = p.emit("act", "activation", sq.h[0:64, 0:128], hxg.h[0:64, 0:128],
                AF.Square, deps=[tm_] + sq.wdep())
    sq.wrote(t2)
    hxg.read(t2)
    t3 = p.emit("dve", "tensor_reduce", s12.h[0:64, 1:2], sq.h[0:64, 0:128],
                axis=AX.X, op=ALU.add, deps=[t2, t1])
    s12.wrote(t3)
    sq.read(t3)
    s12b = c.SH[1]
    t4 = p.emit("dve", "tensor_copy", s12b.h[0:64, 0:4].bitcast(BF16)[:, 0:2],
                s12.h[0:64, 0:2], deps=[t3] + s12b.wdep())
    s12b.wrote(t4)
    s12.read(t4)
    psx = c.PS[4]
    t5 = p.emit("pe", "matmul", psx.h[0:8, 0:2], grp.h[0:64, 0:8],
                s12b.h[0:64, 0:4].bitcast(BF16)[:, 0:2], start=True,
                stop=True, deps=[t4] + grp.rdep() + psx.wdep(), inc=True)
    grp.read(t5)
    s12b.read(t5)
    psx.wrote(t5)
    hst = io["hst"]
    t6 = p.emit("dve", "tensor_scalar_mul", hst.h[0:8, 0:2],
                psx.h[0:8, 0:2], 1.0 / D, deps=[t5] + hst.wdep())
    hst.wrote(t6)
    psx.read(t6)
    t7 = p.emit("dve", "tensor_mul", hst.h[0:8, 2:3], hst.h[0:8, 0:1],
                hst.h[0:8, 0:1], deps=[t6])
    hst.wrote(t7)
    t8 = p.emit("dve", "tensor_sub", hst.h[0:8, 1:2], hst.h[0:8, 1:2],
                hst.h[0:8, 2:3], deps=[t7])
    hst.wrote(t8)
    eps_ap, eps_tok = c.c128t(1e-5)
    t9 = p.emit("act", "activation", hst.h[0:8, 1:2], hst.h[0:8, 1:2],
                AF.Ln, bias=eps_ap[0:8, 0:1], deps=[t8, eps_tok])
    hst.wrote(t9)
    ta = p.emit("act", "activation", hst.h[0:8, 1:2], hst.h[0:8, 1:2],
                AF.Exp, scale=-0.5, deps=[t9])
    hst.wrote(ta)
    stb = c.SH[0]
    tb = p.emit("dve", "tensor_copy", stb.h[0:8, 0:4].bitcast(BF16)[:, 0:2],
                hst.h[0:8, 0:2], deps=[ta] + stb.wdep())
    stb.wrote(tb)
    hst.read(tb)
    psy = c.PS[5]
    tcm = p.emit("pe", "matmul", psy.h[0:64, 0:2], grpT.h[0:8, 0:64],
                 stb.h[0:8, 0:4].bitcast(BF16)[:, 0:2], start=True, stop=True,
                 deps=[tb] + grpT.rdep() + psy.wdep(), inc=True)
    grpT.read(tcm)
    stb.read(tcm)
    psy.wrote(tcm)
    mr64 = io["mr64"]
    td = p.emit("dve", "tensor_copy", mr64.h[0:64, 0:2], psy.h[0:64, 0:2],
                deps=[tcm] + mr64.wdep())
    mr64.wrote(td)
    psy.read(td)
    te = p.emit("dve", "tensor_scalar", hxg.h[0:64, 0:128],
                hxg.h[0:64, 0:128], mr64.h[0:64, 0:1], mr64.h[0:64, 1:2],
                ALU.subtract, ALU.mult, deps=[td] + hxg.wdep())
    hxg.wrote(te)
    mr64.read(te)
    am = c.SH[0]
    tf = p.emit("dve", "tensor_reduce", am.h[0:64, 0:1], hxg.h[0:64, 0:128],
                axis=AX.X, op=ALU.max, apply_absolute_value=True,
                deps=[te] + am.wdep())
    am.wrote(tf)
    hxg.read(tf)
    psz = c.PS[4]
    tg2 = p.emit("pe", "transpose", psz.h[0:1, 0:64], am.h[0:64, 0:1],
                 c.ident_f[0:64, 0:64], deps=[tf] + psz.wdep() + c.setup_tok)
    am.read(tg2)
    psz.wrote(tg2)
    amr = c.SH[1]
    th = p.emit("dve", "tensor_copy", amr.h[0:1, 0:64], psz.h[0:1, 0:64],
                deps=[tg2] + amr.wdep())
    amr.wrote(th)
    psz.read(th)
    ti2 = p.emit("dve", "tensor_reduce", amr.h[0:1, 64:72],
                 amr.h[0:1, 0:64].rearrange("p (a q) -> p a q", a=8),
                 axis=AX.X, op=ALU.max, deps=[th])
    amr.wrote(ti2)
    tj = p.emit("dve", "tensor_scalar_max", amr.h[0:1, 64:72],
                amr.h[0:1, 64:72], 1e-5, deps=[ti2])
    amr.wrote(tj)
    tk2 = p.emit("dve", "reciprocal", amr.h[0:1, 72:80],
                 amr.h[0:1, 64:72], deps=[tj])
    amr.wrote(tk2)
    tl2 = p.emit("dve", "tensor_scalar_mul", amr.h[0:1, 72:80],
                 amr.h[0:1, 72:80], 127.0, deps=[tk2])
    amr.wrote(tl2)
    tmv = p.emit("dve", "tensor_scalar_mul", amr.h[0:1, 64:72],
                 amr.h[0:1, 64:72], 1.0 / 127.0, deps=[tl2])
    amr.wrote(tmv)
    psw = c.PS[5]
    tn = p.emit("pe", "transpose", psw.h[0:8, 0:1], amr.h[0:1, 72:80],
                c.ident_f[0:1, 0:1], deps=[tl2] + psw.wdep() + c.setup_tok)
    amr.read(tn)
    psw.wrote(tn)
    scb = c.SH[0]
    to = p.emit("dve", "tensor_copy", scb.h[0:8, 0:2].bitcast(BF16)[:, 0:1],
                psw.h[0:8, 0:1], deps=[tn] + scb.wdep())
    scb.wrote(to)
    psw.read(to)
    ps64 = c.PS[4]
    tp = p.emit("pe", "matmul", ps64.h[0:64, 0:1], grpT.h[0:8, 0:64],
                scb.h[0:8, 0:2].bitcast(BF16)[:, 0:1], start=True, stop=True,
                deps=[to] + grpT.rdep() + ps64.wdep(), inc=True)
    grpT.read(tp)
    scb.read(tp)
    ps64.wrote(tp)
    tq2 = p.emit("dve", "tensor_copy", mr64.h[0:64, 2:3], ps64.h[0:64, 0:1],
                 deps=[tp] + mr64.wdep())
    mr64.wrote(tq2)
    ps64.read(tq2)
    tr3 = p.emit("dve", "tensor_scalar_mul", hxg.h[0:64, 0:128],
                 hxg.h[0:64, 0:128], mr64.h[0:64, 2:3],
                 deps=[tq2] + hxg.wdep())
    hxg.wrote(tr3)
    mr64.read(tr3)
    hq64 = c.SH[1]
    ts2 = p.emit("dve", "tensor_scalar",
                 hq64.h[0:64, 0:64].bitcast(BF16)[:, 0:128],
                 hxg.h[0:64, 0:128], MAGIC, -MAGIC, ALU.add, ALU.add,
                 deps=[tr3] + hq64.wdep())
    hq64.wrote(ts2)
    hxg.read(ts2)
    psv = c.PS[5]
    tt2 = p.emit("pe", "transpose", psv.h[:, 0:32].bitcast(BF16)[:, 0:64],
                 hq64.h[0:64, 0:64].bitcast(BF16)[:, 0:128],
                 c.ident_b[0:64, 0:64],
                 deps=[ts2] + psv.wdep() + c.setup_tok)
    hq64.read(tt2)
    psv.wrote(tt2)
    hqf = c.SH[0]
    tu = p.emit("dve", "tensor_copy", hqf.h[:, 0:32].bitcast(BF16)[:, 0:64],
                psv.h[:, 0:32].bitcast(BF16)[:, 0:64],
                deps=[tt2] + hqf.wdep())
    hqf.wrote(tu)
    psv.read(tu)
    for dt in range(NDT):
        tv = p.emit("dve", "tensor_copy", c.XN[dt].h[:, T:T + 8],
                    hqf.h[:, 0:32].bitcast(BF16)[:, dt:64:8],
                    deps=[tu] + c.XN[dt].wdep())
        c.XN[dt].wrote(tv)
        hqf.read(tv)

    inv_row = c.ROW[0]
    ti = p.emit("dve", "tensor_scalar_mul", inv_row.h[0:1, 0:T],
                amax_row.h[0:1, 0:T], 1.0 / 127.0,
                deps=amax_row.rdep() + inv_row.wdep())
    inv_row.wrote(ti)
    amax_row.read(ti)
    IVW = c.SC[0]
    bcast_row(p, c, inv_row, 0, T, IVW)
    m1b = io["m1b"]
    bcast_to_128(p, c, m1.h, m1.rdep(), m1b)
    m1.read(p.tok("pe"))
    for tcI in range(4):
        t0 = tcI * 512
        tw_ = p.emit("dve", "tensor_scalar_mul", IVW.h[:, t0:t0 + 512],
                     IVW.h[:, t0:t0 + 512], m1b.h[:, 0:1],
                     deps=IVW.wdep() + m1b.rdep())
        IVW.wrote(tw_)
        m1b.read(tw_)
    HIV = io["hiv"]
    psh = c.PS[4]
    th2 = p.emit("pe", "matmul", psh.h[:, 0:8], c.ones_f1[:],
                 amr.h[0:1, 64:72], start=True, stop=True,
                 deps=amr.rdep() + psh.wdep() + c.setup_tok, inc=True)
    amr.read(th2)
    psh.wrote(th2)
    th3 = p.emit("dve", "tensor_scalar_mul", HIV.h[:, 0:8], psh.h[:, 0:8],
                 m1b.h[:, 0:1], deps=[th2] + m1b.rdep() + HIV.wdep())
    HIV.wrote(th3)
    psh.read(th3)
    m1b.read(th3)

    def hext_v(b):
        return c.XC[4 + b], c.XC[4 + b].h[:, 0:512].bitcast(BF16)[:, 0:514]

    cw = io["convw"]
    cb = io["convb"]
    M4 = c.XC[0:4]

    def w1_evac(ht, tcI, t0, w, ps):
        if tcI < 4:
            b = tcI
            hb_, hv_ = hext_v(b)
            tm = p.emit("dve", "tensor_mul", hv_[:, 1:513], ps.h[:, 0:512],
                        IVW.h[:, t0:t0 + 512],
                        deps=ps.rdep() + IVW.rdep() + hb_.wdep())
            hb_.wrote(tm)
            ps.read(tm)
            IVW.read(tm)
        else:
            sh = c.SH[0]
            tm = p.emit("dve", "tensor_mul", sh.h[:, 0:8], ps.h[:, 0:8],
                        HIV.h[:, 0:8], deps=ps.rdep() + HIV.rdep()
                        + sh.wdep())
            sh.wrote(tm)
            ps.read(tm)
            HIV.read(tm)
            for b in range(B):
                hb_, hv_ = hext_v(b)
                tl3 = p.emit("dve", "tensor_copy", hv_[:, 0:1],
                             sh.h[:, b:b + 1], deps=[tm] + hb_.wdep())
                hb_.wrote(tl3)
                sh.read(tl3)
                tr4 = p.emit("dve", "tensor_copy", hv_[:, 513:514],
                             sh.h[:, 4 + b:5 + b], deps=[tm] + hb_.wdep())
                hb_.wrote(tr4)
                sh.read(tr4)
            y = c.SC[1]
            for b in range(B):
                hb_, hv_ = hext_v(b)
                ysl = y.h[:, b * 512:(b + 1) * 512]
                tc1 = p.emit("dve", "tensor_scalar_mul", ysl,
                             hv_[:, 0:512], cw.h[:, 3 * ht:3 * ht + 1],
                             deps=hb_.rdep() + cw.rdep() + y.wdep())
                y.wrote(tc1)
                hb_.read(tc1)
                cw.read(tc1)
                tc2 = p.emit("dve", "scalar_tensor_tensor", ysl,
                             hv_[:, 1:513], cw.h[:, 3 * ht + 1:3 * ht + 2],
                             ysl, ALU.mult, ALU.add, deps=[tc1])
                y.wrote(tc2)
                hb_.read(tc2)
                cw.read(tc2)
                tc3 = p.emit("dve", "scalar_tensor_tensor", ysl,
                             hv_[:, 2:514], cw.h[:, 3 * ht + 2:3 * ht + 3],
                             ysl, ALU.mult, ALU.add, deps=[tc2])
                y.wrote(tc3)
                hb_.read(tc3)
                cw.read(tc3)
            ts3 = p.emit("act", "activation", y.h[:, 0:T], y.h[:, 0:T],
                         AF.Silu, bias=cb.h[:, ht:ht + 1],
                         deps=[tc3] + cb.rdep())
            y.wrote(ts3)
            cb.read(ts3)
            for k in range(4):
                a = k * 512
                if ht == 0:
                    tm2 = p.emit("act", "activation", M4[k].h[:, 0:512],
                                 y.h[:, a:a + 512], AF.Square,
                                 deps=[ts3] + M4[k].wdep())
                    M4[k].wrote(tm2)
                    y.read(tm2)
                else:
                    sqc = c.SH[1]
                    tqc = p.emit("act", "activation", sqc.h[:, 0:512],
                                 y.h[:, a:a + 512], AF.Square,
                                 deps=[ts3] + sqc.wdep())
                    sqc.wrote(tqc)
                    y.read(tqc)
                    tm2 = p.emit("dve", "tensor_tensor",
                                 M4[k].h[:, 0:512], M4[k].h[:, 0:512],
                                 sqc.h[:, 0:512], ALU.max,
                                 deps=[tqc] + M4[k].wdep())
                    M4[k].wrote(tm2)
                    sqc.read(tm2)
            td2 = p.dma("sp", io["hd"][ht], y.h[:, 0:T],
                        deps=y.rdep() + [p.tok("pool")], key="l2spill")
            y.read(td2)
            io["hd_buf"].wrote(td2)

    gemm_ht(p, c, io["w1_t"], NDT, NH, c.XN, T + 8, w1_evac,
            wq_pool_cb=lambda wf, wb, ht: wq3_pool(p, c, wf, wb, 1024, ws1))

    qa_row = c.ROW[1]
    for k in range(4):
        col_reduce(p, c, M4[k], 512, qa_row, col0=k * 512, scratch=c.SH[0])
    tcq0 = p.emit("dve", "tensor_scalar", qa_row.h[0:1, 0:T],
                  qa_row.h[0:1, 0:T], 1e-38, None, ALU.add,
                  deps=qa_row.wdep())
    qa_row.wrote(tcq0)
    tcq1 = p.emit("act", "activation", qa_row.h[0:1, 0:T],
                  qa_row.h[0:1, 0:T], AF.Ln, deps=[tcq0])
    qa_row.wrote(tcq1)
    tcq2 = p.emit("act", "activation", qa_row.h[0:1, 0:T],
                  qa_row.h[0:1, 0:T], AF.Exp, scale=0.5, deps=[tcq1])
    qa_row.wrote(tcq2)
    t2q = p.emit("dve", "tensor_scalar_max", qa_row.h[0:1, 0:T],
                 qa_row.h[0:1, 0:T], 1e-5, deps=qa_row.wdep())
    qa_row.wrote(t2q)
    es_row = scale_row
    tm0 = p.emit("dve", "reciprocal", es_row.h[0:1, 0:T],
                 qa_row.h[0:1, 0:T], deps=[t2q] + es_row.wdep())
    es_row.wrote(tm0)
    qa_row.read(tm0)
    ts4 = p.emit("dve", "tensor_scalar_mul", es_row.h[0:1, 0:T],
                 es_row.h[0:1, 0:T], 127.0, deps=[tm0])
    es_row.wrote(ts4)
    os_row = c.ROW[0]
    t5o = p.emit("dve", "tensor_scalar", os_row.h[0:1, 0:T],
                 qa_row.h[0:1, 0:T], m2.h, 0.5 / 127.0, ALU.mult, ALU.mult,
                 deps=qa_row.rdep() + m2.rdep() + os_row.wdep())
    os_row.wrote(t5o)
    m2.read(t5o)
    ESb = c.SC[0]
    OSb = c.SC[1]
    bcast_row(p, c, es_row, 0, T, ESb)
    bcast_row(p, c, os_row, 0, T, OSb)

    def tile_cb(hb, hq, ht, t0, tw):
        raise RuntimeError("unused")

    wd_s = io["w2_s"]
    n_g = 4
    for ob in range(2):
        for tcb in range(2):
            t0 = tcb * 1024
            pss = [[c.PS[oi * 2 + h] for h in range(2)] for oi in range(4)]
            for g in range(n_g):
                wbs = [c.WB[(g % 2) * 2], c.WB[(g % 2) * 2 + 1]]
                for q in range(4):
                    wf = c.WF[q % 2]
                    tk = p.dma("sp", wf.h[:, 0:1024],
                               wd_s[ob, g][:, q * 1024:(q + 1) * 1024],
                               deps=wf.wdep())
                    wf.wrote(tk)
                    wq3_pool_to(p, c, wf, wbs[q // 2], (q % 2) * 1024,
                                1024, ws2)
                for hti in range(8):
                    ht = g * 8 + hti
                    hq = c.HT[ht % 2]
                    for hh in range(2):
                        hf = c.XC[2 * (ht % 2) + hh]
                        a = t0 + hh * 512
                        tk = p.dma("pool", hf.h[:, 0:512],
                                   io["hd"][ht][:, a:a + 512],
                                   deps=hf.wdep() + io["hd_buf"].rdep())
                        hf.wrote(tk)
                        tq1 = p.emit("dve", "tensor_mul", hf.h[:, 0:512],
                                     hf.h[:, 0:512], ESb.h[:, a:a + 512],
                                     deps=[tk] + ESb.rdep())
                        hf.wrote(tq1)
                        ESb.read(tq1)
                        tq2 = p.emit("dve", "tensor_scalar",
                                     hq.h[:, hh * 512:hh * 512 + 512],
                                     hf.h[:, 0:512], MAGIC, -MAGIC, ALU.add,
                                     ALU.add, deps=[tq1] + hq.wdep())
                        hq.wrote(tq2)
                        hf.read(tq2)
                    last = (g == n_g - 1 and hti == 7)
                    for oi in range(4):
                        ci = (hti * 4 + oi) * 128
                        wsl = wbs[ci // 2048]
                        wap = wsl.h[:, ci % 2048:ci % 2048 + 128]
                        for h in range(2):
                            ps = pss[oi][h]
                            firstm = (g == 0 and hti == 0)
                            mm = p.emit("pe", "matmul", ps.h[:, 0:512],
                                        wap,
                                        hq.h[:, h * 512:h * 512 + 512],
                                        start=firstm, stop=last,
                                        deps=(ps.wdep() if firstm else [])
                                        + wsl.rdep() + hq.rdep(),
                                        inc=(last or (oi == 3 and h == 1)))
                            hq.read(mm)
                            wsl.read(mm)
                            ps.wrote(mm)
            for oi in range(4):
                ot = ob * 4 + oi
                for h in range(2):
                    ps = pss[oi][h]
                    a = t0 + h * 512
                    tmp = c.SH[1]
                    tf = p.emit("dve", "tensor_mul", tmp.h[:, 0:512],
                                ps.h[:, 0:512], OSb.h[:, a:a + 512],
                                deps=ps.rdep() + OSb.rdep() + tmp.wdep())
                    tmp.wrote(tf)
                    ps.read(tf)
                    OSb.read(tf)
                    tx = p.emit("dve", "tensor_add", c.X[ot].h[:, a:a + 512],
                                c.X[ot].h[:, a:a + 512], tmp.h[:, 0:512],
                                deps=[tf] + c.X[ot].wdep())
                    c.X[ot].wrote(tx)
                    tmp.read(tx)


def declare_l2_ffn_io(nc, p, c, core_id, pref="L2f"):
    io = {}
    io["w1_t"] = nc.dram_tensor(f"{pref}_w1_t", [32, 128, 1024], F32,
                                kind="ExternalInput").ap()
    io["w2_s"] = nc.dram_tensor(f"{pref}_w2_s", [2, 4, 128, 4096], F32,
                                kind="ExternalInput").ap()
    io["w2m_v"] = io["w2_s"].rearrange("b g p (a q) -> (b g) a p q", q=1024)
    io["cvec"] = nc.dram_tensor(f"{pref}_cvec", [128, 96 + 32], F32,
                                kind="ExternalInput").ap()
    io["hidx_in"] = nc.dram_tensor(f"{pref}_hidx", [64, 1], I32,
                                   kind="ExternalInput").ap()
    io["hmask_in"] = nc.dram_tensor(f"{pref}_hmask", [64, 1], F32,
                                    kind="ExternalInput").ap()
    io["grp_in"] = nc.dram_tensor(f"{pref}_grp", [64, 8], F32,
                                  kind="ExternalInput").ap()
    io["grpT_in"] = nc.dram_tensor(f"{pref}_grpT", [8, 64], F32,
                                   kind="ExternalInput").ap()
    io["hd"] = nc.dram_tensor(f"{pref}_hd", [32, 128, T], F32,
                              kind="Internal").ap()
    io["hd_buf"] = Buf(io["hd"])
    io["hcc_in"] = Buf(nc.dram_tensor(f"{pref}_hccin", [8192], F32,
                                      kind="Internal").ap())
    io["hcc_out"] = Buf(nc.dram_tensor(f"{pref}_hccout", [65536], F32,
                                       kind="Internal",
                                       addr_space="Shared").ap())
    cv = c.scratch([128, 128], F32)
    tk = p.dma("sp", cv.h[:, 0:128], io["cvec"], deps=cv.wdep(),
               key=cv.h.name)
    cv.wrote(tk)
    io["convw"] = _subbuf(cv, 0, 96)
    io["convb"] = _subbuf(cv, 96, 32)
    io["hxg"] = c.scratch([64, 128], F32)
    idx = c.scratch([64, 1], I32)
    t1 = p.dma("sp", idx.h[0:64, 0:1], io["hidx_in"], deps=idx.wdep(),
               key=idx.h.name)
    idx.wrote(t1)
    io["hidx"] = idx
    io["hidx_tok"] = [t1]
    hm = c.scratch([64, 1], F32)
    t2 = p.dma("sp", hm.h[0:64, 0:1], io["hmask_in"], deps=hm.wdep(),
               key=hm.h.name)
    hm.wrote(t2)
    io["hmask"] = hm
    io["hmask_tok"] = [t2]
    grp = c.scratch([64, 8], BF16)
    t3 = p.dma("pool", grp.h[0:64, 0:8], io["grp_in"], deps=grp.wdep(),
               key=grp.h.name)
    grp.wrote(t3)
    io["grp"] = grp
    grpT = c.scratch([8, 64], BF16)
    t4 = p.dma("pool", grpT.h[0:8, 0:64], io["grpT_in"], deps=grpT.wdep(),
               key=grpT.h.name)
    grpT.wrote(t4)
    io["grpT"] = grpT
    io["hst"] = c.scratch([8, 16], F32)
    io["mr64"] = c.scratch([64, 4], F32)
    io["m1b"] = c.scratch([128, 1], F32)
    io["hiv"] = c.scratch([128, 8], F32)
    return io


def stage_l2_ffn(inputs, pref="L2f"):
    def tile_od(w):
        O, Din = w.shape
        no, nd = O // 128, Din // 128
        t = w.reshape(no, 128, nd, 128).transpose(0, 3, 2, 1)
        return np.ascontiguousarray(t.reshape(no, 128, nd * 128),
                                    dtype=np.float32)

    def tile_slab(w):
        O, H = w.shape
        n_g = H // 1024
        t = w.reshape(2, 4, 128, n_g, 8, 128).transpose(0, 3, 5, 4, 1, 2)
        return np.ascontiguousarray(t.reshape(2, n_g, 128, 4096),
                                    dtype=np.float32)

    cvec = np.zeros((128, 128), np.float32)
    cw = inputs["L2_ffn_conv_w"].reshape(32, 128, 3)
    cvec[:, 0:96] = cw.transpose(1, 0, 2).reshape(128, 96)
    cvec[:, 96:128] = inputs["L2_ffn_conv_b"].reshape(32, 128).T
    grp = np.zeros((64, 8), np.float32)
    for r in range(64):
        grp[r, r // 8] = 1.0
    base = {
        f"{pref}_w1_t": tile_od(inputs["L2_ffn_w1"]),
        f"{pref}_w2_s": tile_slab(inputs["L2_ffn_w2"]),
        f"{pref}_cvec": cvec,
        f"{pref}_grp": grp,
        f"{pref}_grpT": np.ascontiguousarray(grp.T),
    }
    per_core = []
    for cc in range(NCORES):
        idx = np.zeros((64, 1), np.int32)
        mask = np.zeros((64, 1), np.float32)
        for b in range(B):
            for dt in range(NDT):
                r = b * 8 + dt
                idx[r, 0] = ((cc - 1) % 8) * 64 + 32 + b * 8 + dt
                mask[r, 0] = 0.0 if cc == 0 else 1.0
                idx[32 + r, 0] = ((cc + 1) % 8) * 64 + 0 + b * 8 + dt
                mask[32 + r, 0] = 0.0 if cc == 7 else 1.0
        per_core.append({f"{pref}_hidx": idx, f"{pref}_hmask": mask, **base})
    return per_core


def l1_attn(p, c, io, st):
    CS = 16
    NCH = SLOC // CS

    ln_full(p, c, c.X, c.XN)

    def xp_evac(ht, tcI, t0, w, ps):
        hb = c.HT[2]
        ta = p.emit("dve", "tensor_copy", hb.h[:, t0:t0 + w], ps.h[:, 0:w],
                    deps=ps.rdep() + hb.wdep())
        hb.wrote(ta)
        ps.read(ta)
        if t0 + w == T:
            ts = p.dma("sp", io["xpd"][ht], hb.h[:, 0:T], deps=hb.rdep(),
                       key="xpspill")
            hb.read(ts)
            io["xpd_buf"].wrote(ts)

    gemm_ht(p, c, io["win_t"], NDT, NDT, c.XN, T, xp_evac)

    xpb = io["xpd_buf"]
    m_row, r_row = c.ROW[0], c.ROW[1]
    for i in range(NDT):
        ab = c.HT[i % 2]
        tk = p.dma("pool", ab.h[:, 0:T], io["xpd"][i],
                   deps=ab.wdep() + xpb.rdep())
        ab.wrote(tk)
        xpb.read(tk)
        sq = c.HT[2]
        tq = p.emit("act", "activation", sq.h[:, 0:T], ab.h[:, 0:T],
                    AF.Square, deps=[tk] + sq.wdep())
        sq.wrote(tq)
        ab.read(tq)
        for tcI in range(4):
            t0 = tcI * 512
            for stat in range(2):
                ps = c.PS[tcI * 2 + stat]
                rb = (ab, sq)[stat]
                mm = p.emit("pe", "matmul", ps.h[0:1, 0:512], c.ones_bf[:],
                            rb.h[:, t0:t0 + 512], start=(i == 0),
                            stop=(i == NDT - 1),
                            deps=(ps.wdep() + c.setup_tok if i == 0 else [])
                            + rb.rdep(), inc=True)
                rb.read(mm)
                ps.wrote(mm)
    for tcI in range(4):
        t0 = tcI * 512
        for stat in range(2):
            ps = c.PS[tcI * 2 + stat]
            row = (m_row, r_row)[stat]
            te = p.emit("dve", "tensor_copy", row.h[0:1, t0:t0 + 512],
                        ps.h[0:1, 0:512], deps=ps.rdep() + row.wdep())
            ps.read(te)
            row.wrote(te)
    tm = p.emit("dve", "tensor_scalar_mul", m_row.h[0:1, 0:T],
                m_row.h[0:1, 0:T], 1.0 / D, deps=m_row.wdep())
    m_row.wrote(tm)
    t2 = p.emit("dve", "tensor_scalar_mul", r_row.h[0:1, 0:T],
                r_row.h[0:1, 0:T], 1.0 / D, deps=r_row.wdep())
    r_row.wrote(t2)
    mm2 = c.ROW[2]
    t3 = p.emit("dve", "tensor_mul", mm2.h[0:1, 0:T], m_row.h[0:1, 0:T],
                m_row.h[0:1, 0:T], deps=m_row.rdep() + mm2.wdep())
    mm2.wrote(t3)
    m_row.read(t3)
    t4 = p.emit("dve", "tensor_sub", r_row.h[0:1, 0:T], r_row.h[0:1, 0:T],
                mm2.h[0:1, 0:T], deps=[t3] + r_row.wdep())
    r_row.wrote(t4)
    mm2.read(t4)
    eps_ap, eps_tok = c.c11t(1e-5)
    t5 = p.emit("act", "activation", r_row.h[0:1, 0:T], r_row.h[0:1, 0:T],
                AF.Ln, bias=eps_ap, deps=[t4, eps_tok])
    r_row.wrote(t5)
    t6 = p.emit("act", "activation", r_row.h[0:1, 0:T], r_row.h[0:1, 0:T],
                AF.Exp, scale=-0.5, deps=[t5])
    r_row.wrote(t6)
    Mb, Rb = c.SC[0], c.SC[1]
    bcast_row(p, c, m_row, 0, T, Mb)
    bcast_row(p, c, r_row, 0, T, Rb)
    lg, lb = io["lng"], io["lnb"]
    for i in range(NDT):
        for tcI in range(4):
            t0 = tcI * 512
            tmp = c.XC[tcI % 4]
            tk = p.dma("pool", tmp.h[:, 0:512],
                       io["xpd"][i][:, t0:t0 + 512],
                       deps=tmp.wdep() + xpb.rdep())
            tmp.wrote(tk)
            xpb.read(tk)
            ta = p.emit("dve", "tensor_sub", tmp.h[:, 0:512],
                        tmp.h[:, 0:512], Mb.h[:, t0:t0 + 512],
                        deps=[tk] + Mb.rdep())
            tmp.wrote(ta)
            Mb.read(ta)
            tb = p.emit("dve", "tensor_mul", tmp.h[:, 0:512],
                        tmp.h[:, 0:512], Rb.h[:, t0:t0 + 512],
                        deps=[ta] + Rb.rdep())
            tmp.wrote(tb)
            Rb.read(tb)
            tc_ = p.emit("dve", "tensor_scalar", c.XN[i].h[:, t0:t0 + 512],
                         tmp.h[:, 0:512], lg.h[:, i:i + 1], lb.h[:, i:i + 1],
                         ALU.mult, ALU.add,
                         deps=[tb] + lg.rdep() + lb.rdep() + c.XN[i].wdep())
            c.XN[i].wrote(tc_)
            tmp.read(tc_)
            lg.read(tc_)
            lb.read(tc_)

    eigc = io["eig"]
    tt = p.emit("act", "activation", eigc.h[0:16, 1:2], eigc.h[0:16, 0:1],
                AF.Tanh, deps=eigc.rdep() + eigc.wdep())
    eigc.wrote(tt)
    bg = io["bgate"]
    for tcI in range(4):
        t0 = tcI * 512
        psg = c.PS[4 + tcI % 2]
        mm = None
        for dt in range(NDT):
            mm = p.emit("pe", "matmul", psg.h[0:32, 0:512],
                        io["wg_t"].h[:, dt * 32:(dt + 1) * 32],
                        c.XN[dt].h[:, t0:t0 + 512], start=(dt == 0),
                        stop=(dt == NDT - 1),
                        deps=(psg.wdep() if dt == 0 else [])
                        + io["wg_t"].rdep() + c.XN[dt].rdep(), inc=True)
            c.XN[dt].read(mm)
            io["wg_t"].read(mm)
        psg.wrote(mm)
        sg = c.SH[0]
        ta = p.emit("act", "activation", sg.h[0:32, 0:512],
                    psg.h[0:32, 0:512], AF.Sigmoid, bias=bg.h[0:32, 0:1],
                    deps=[mm] + bg.rdep() + sg.wdep())
        sg.wrote(ta)
        psg.read(ta)
        bg.read(ta)
        tb = p.emit("dve", "tensor_scalar_mul",
                    c.HT[0].h[0:16, t0:t0 + 512], sg.h[0:16, 0:512],
                    eigc.h[0:16, 1:2], deps=[ta] + eigc.rdep()
                    + c.HT[0].wdep())
        c.HT[0].wrote(tb)
        sg.read(tb)
        eigc.read(tb)
        td = p.emit("dve", "tensor_copy", c.HT[1].h[0:32, t0:t0 + 512],
                    sg.h[0:32, 0:512], deps=[ta] + c.HT[1].wdep())
        c.HT[1].wrote(td)
        sg.read(td)

    SELH = c.HT[2]
    tmx = p.emit("pool", "memset", SELH.h[0:32, 0:2048], 0.0,
                 deps=SELH.wdep())
    SELH.wrote(tmx)
    for dt in range(NDT):
        ta_ = p.emit("pool", "memset", SELH.h[0:16, dt * 128:dt * 128 + 128],
                     1.0, deps=[tmx])
        SELH.wrote(ta_)
        ts_ = p.emit("pool", "affine_select",
                     SELH.h[0:16, dt * 128:dt * 128 + 128],
                     SELH.h[0:16, dt * 128:dt * 128 + 128],
                     pattern=[[1, 2], [0, 64]], compare_op=ALU.is_equal,
                     fill=0.0, base=2 * dt, channel_multiplier=-1,
                     deps=[ta_])
        SELH.wrote(ts_)
        tb_ = p.emit("pool", "memset",
                     SELH.h[0:32, 1024 + dt * 128:1024 + dt * 128 + 128],
                     1.0, deps=[tmx])
        SELH.wrote(tb_)
        tu_ = p.emit("pool", "affine_select",
                     SELH.h[0:32, 1024 + dt * 128:1024 + dt * 128 + 128],
                     SELH.h[0:32, 1024 + dt * 128:1024 + dt * 128 + 128],
                     pattern=[[1, 2], [0, 64]], compare_op=ALU.is_equal,
                     fill=0.0, base=2 * dt + 16, channel_multiplier=-1,
                     deps=[tb_])
        SELH.wrote(tu_)

    ANs = io["ans"]
    HCE = io["hce"]
    for dt in range(NDT):
        aexp, bc = c.SC[0], c.SC[1]
        for tcI in range(4):
            t0 = tcI * 512
            psa = c.PS[tcI % 2]
            ma = p.emit("pe", "matmul", psa.h[:, 0:512],
                        SELH.h[0:16, dt * 128:dt * 128 + 128],
                        c.HT[0].h[0:16, t0:t0 + 512], start=True, stop=True,
                        deps=psa.wdep() + SELH.rdep() + c.HT[0].rdep(),
                        inc=True)
            SELH.read(ma)
            c.HT[0].read(ma)
            psa.wrote(ma)
            ta_ = p.emit("dve", "tensor_copy", aexp.h[:, t0:t0 + 512],
                         psa.h[:, 0:512], deps=[ma] + aexp.wdep())
            aexp.wrote(ta_)
            psa.read(ta_)
            psb = c.PS[2 + tcI % 2]
            mb = p.emit("pe", "matmul", psb.h[:, 0:512],
                        SELH.h[0:32, 1024 + dt * 128:1024 + dt * 128 + 128],
                        c.HT[1].h[0:32, t0:t0 + 512], start=True, stop=True,
                        deps=psb.wdep() + SELH.rdep() + c.HT[1].rdep(),
                        inc=True)
            SELH.read(mb)
            c.HT[1].read(mb)
            psb.wrote(mb)
            tb_ = p.emit("dve", "tensor_mul", bc.h[:, t0:t0 + 512],
                         psb.h[:, 0:512], c.XN[dt].h[:, t0:t0 + 512],
                         deps=[mb] + c.XN[dt].rdep() + bc.wdep())
            bc.wrote(tb_)
            psb.read(tb_)
            c.XN[dt].read(tb_)
        for b in range(B):
            t0 = b * 512
            seg = slice(t0, t0 + 512)
            tmp = c.XC[0]
            ts1 = p.emit("pool", "affine_select", tmp.h[:, 0:512],
                         aexp.h[:, seg], pattern=[[0, NCH], [1, CS]],
                         compare_op=ALU.is_equal, fill=0.0, base=0,
                         channel_multiplier=0,
                         deps=aexp.rdep() + tmp.wdep())
            tmp.wrote(ts1)
            aexp.read(ts1)
            am_ = c.XC[1]
            ts2 = p.emit("pool", "affine_select", am_.h[:, 0:512],
                         aexp.h[:, seg], pattern=[[0, NCH], [1, CS]],
                         compare_op=ALU.is_gt, fill=0.0, base=0,
                         channel_multiplier=0,
                         deps=aexp.rdep() + am_.wdep())
            am_.wrote(ts2)
            aexp.read(ts2)
            cum = c.XC[2]
            tsc = p.emit("dve", "tensor_tensor_scan", cum.h[:, 0:512],
                         am_.h[:, 0:512], tmp.h[:, 0:512], 0.0,
                         ALU.mult, ALU.add,
                         deps=[ts1, ts2] + cum.wdep())
            cum.wrote(tsc)
            tmp.read(tsc)
            am_.read(tsc)
            j0 = (dt * 4 + b) * 32
            tan = p.emit("dve", "tensor_copy", ANs.h[:, j0:j0 + 32],
                         cum.h[:, CS - 1:512:CS], deps=[tsc] + ANs.wdep())
            ANs.wrote(tan)
            cum.read(tan)
            cbf = c.XC[3]
            tcb = p.emit("dve", "tensor_copy",
                         cbf.h[:, 0:256].bitcast(BF16)[:, 0:512],
                         cum.h[:, 0:512], deps=[tsc] + cbf.wdep())
            cbf.wrote(tcb)
            cum.read(tcb)
            tsp = p.dma("sp", io["cumd"][dt][:, seg],
                        cbf.h[:, 0:256].bitcast(BF16)[:, 0:512],
                        deps=[tcb], key="cumspill")
            cbf.read(tsp)
            io["cumd_buf"].wrote(tsp)
            d1 = c.XC[0]
            tg1 = p.emit("dve", "tensor_copy", d1.h[:, 1:512],
                         cum.h[:, 0:511], deps=cum.rdep() + d1.wdep())
            d1.wrote(tg1)
            cum.read(tg1)
            tg2 = p.emit("pool", "affine_select", d1.h[:, 0:512],
                         d1.h[:, 0:512], pattern=[[0, NCH], [1, CS]],
                         compare_op=ALU.is_gt, fill=1.0, base=0,
                         channel_multiplier=0, deps=[tg1])
            d1.wrote(tg2)
            tg3a = p.emit("dve", "tensor_mul", d1.h[:, 0:512],
                          d1.h[:, 0:512], d1.h[:, 0:512], deps=[tg2])
            d1.wrote(tg3a)
            tg3 = p.emit("dve", "tensor_scalar", d1.h[:, 0:512],
                         d1.h[:, 0:512], 1e-16, None, ALU.is_gt,
                         deps=[tg3a])
            d1.wrote(tg3)
            td0 = p.emit("dve", "tensor_mul", d1.h[:, 0:512],
                         d1.h[:, 0:512], aexp.h[:, seg],
                         deps=[tg3] + aexp.rdep())
            d1.wrote(td0)
            aexp.read(td0)
            td1 = p.emit("dve", "tensor_mul", d1.h[:, 0:512],
                         d1.h[:, 0:512], bc.h[:, seg],
                         deps=[td0] + bc.rdep())
            d1.wrote(td1)
            bc.read(td1)
            g = c.XC[2]
            tg = p.emit("dve", "tensor_tensor_scan", g.h[:, 0:512],
                        am_.h[:, 0:512], d1.h[:, 0:512], 0.0,
                        ALU.mult, ALU.add, deps=[td1] + g.wdep())
            g.wrote(tg)
            am_.read(tg)
            d1.read(tg)
            hc = c.XC[1]
            th1 = p.emit("dve", "tensor_sub", hc.h[:, 0:512], g.h[:, 0:512],
                         d1.h[:, 0:512], deps=[tg] + hc.wdep())
            hc.wrote(th1)
            g.read(th1)
            d1.read(th1)
            th2 = p.emit("dve", "tensor_add", hc.h[:, 0:512], hc.h[:, 0:512],
                         bc.h[:, seg], deps=[th1] + bc.rdep())
            hc.wrote(th2)
            bc.read(th2)
            the = p.emit("dve", "tensor_copy", HCE.h[:, j0:j0 + 32],
                         hc.h[:, CS - 1:512:CS], deps=[th2] + HCE.wdep())
            HCE.wrote(the)
            hc.read(the)
            hbf = c.XC[3]
            thb = p.emit("dve", "tensor_copy",
                         hbf.h[:, 256:512].bitcast(BF16)[:, 0:512],
                         hc.h[:, 0:512], deps=[th2] + hbf.wdep())
            hbf.wrote(thb)
            hc.read(thb)
            tsp2 = p.dma("sp", io["hcd"][dt][:, seg],
                         hbf.h[:, 256:512].bitcast(BF16)[:, 0:512],
                         deps=[thb], key="hcspill")
            hbf.read(tsp2)
            io["hcd_buf"].wrote(tsp2)

    PV = io["pv"]
    for dt in range(NDT):
        for b in range(B):
            j0 = (dt * 4 + b) * 32
            jc = (dt * 4 + b) * 2
            sh = c.SH[0]
            tp1 = p.emit("dve", "tensor_mul", sh.h[:, 0:16],
                         ANs.h[:, j0:j0 + 16], ANs.h[:, j0 + 16:j0 + 32],
                         deps=ANs.rdep() + sh.wdep())
            sh.wrote(tp1)
            ANs.read(tp1)
            for wdt in (8, 4, 2, 1):
                tp1 = p.emit("dve", "tensor_mul", sh.h[:, 0:wdt],
                             sh.h[:, 0:wdt], sh.h[:, wdt:2 * wdt],
                             deps=[tp1])
                sh.wrote(tp1)
            tpv = p.emit("dve", "tensor_copy", PV.h[:, jc:jc + 1],
                         sh.h[:, 0:1], deps=[tp1] + PV.wdep())
            PV.wrote(tpv)
            sh.read(tpv)
            tvv = p.emit("dve", "tensor_tensor_scan", sh.h[:, 32:64],
                         ANs.h[:, j0:j0 + 32], HCE.h[:, j0:j0 + 32], 0.0,
                         ALU.mult, ALU.add,
                         deps=ANs.rdep() + HCE.rdep() + sh.wdep())
            sh.wrote(tvv)
            ANs.read(tvv)
            HCE.read(tvv)
            tpv2 = p.emit("dve", "tensor_copy", PV.h[:, jc + 1:jc + 2],
                          sh.h[:, 63:64], deps=[tvv] + PV.wdep())
            PV.wrote(tpv2)
            sh.read(tpv2)
    cin, cout = io["scc_in"], io["scc_out"]
    t1 = p.dma("sp", cin.h[0:8192].rearrange("(q j) -> q j", q=128),
               PV.h[:, 0:64], deps=PV.rdep() + cin.wdep())
    PV.read(t1)
    cin.wrote(t1)
    tcc = collective(p, "AllGather", ALU.bypass,
                     [cin.h[0:8192].rearrange("(a q) -> a q", a=64)],
                     [cout.h[0:65536].rearrange("(a q) -> a q", a=512)],
                     deps=[t1] + cout.wdep())
    cout.wrote(tcc)
    cin.read(tcc)
    S_ = c.SH[0]
    CIN = io["cinb"]
    tz = p.emit("pool", "memset", S_.h[:, 0:32], 0.0, deps=S_.wdep())
    S_.wrote(tz)
    tz2 = p.emit("pool", "memset", CIN.h[:, 0:32], 0.0, deps=CIN.wdep())
    CIN.wrote(tz2)
    rmask = io["rmask"]
    PVr = c.SH[1]
    for r in range(NCORES):
        tl = p.dma("pool", PVr.h[:, 0:64],
                   cout.h[r * 8192:(r + 1) * 8192].rearrange(
                       "(q j) -> q j", q=128), deps=[tcc] + PVr.wdep())
        PVr.wrote(tl)
        cout.read(tl)
        tsm = p.emit("dve", "tensor_scalar_mul", c.XC[0].h[:, 0:32],
                     S_.h[:, 0:32], rmask.h[:, r:r + 1],
                     deps=S_.rdep() + rmask.rdep() + c.XC[0].wdep())
        c.XC[0].wrote(tsm)
        S_.read(tsm)
        rmask.read(tsm)
        tac = p.emit("dve", "tensor_add", CIN.h[:, 0:32], CIN.h[:, 0:32],
                     c.XC[0].h[:, 0:32], deps=[tsm] + CIN.wdep())
        CIN.wrote(tac)
        c.XC[0].read(tac)
        tp_ = p.emit("dve", "tensor_tensor", S_.h[:, 0:32], S_.h[:, 0:32],
                     PVr.h[:, 0:64:2], ALU.mult,
                     deps=[tl] + S_.rdep() + S_.wdep())
        S_.wrote(tp_)
        PVr.read(tp_)
        tv_ = p.emit("dve", "tensor_add", S_.h[:, 0:32], S_.h[:, 0:32],
                     PVr.h[:, 1:64:2], deps=[tp_])
        S_.wrote(tv_)
        PVr.read(tv_)

    CC = io["ccar"]
    for dt in range(NDT):
        for b in range(B):
            j0 = (dt * 4 + b) * 32
            jc = (dt * 4 + b) * 2
            sh = c.SH[0]
            tsc2 = p.emit("dve", "tensor_tensor_scan", sh.h[:, 0:32],
                          ANs.h[:, j0:j0 + 32], HCE.h[:, j0:j0 + 32],
                          CIN.h[:, jc // 2:jc // 2 + 1],
                          ALU.mult, ALU.add,
                          deps=ANs.rdep() + HCE.rdep() + CIN.rdep()
                          + sh.wdep())
            sh.wrote(tsc2)
            ANs.read(tsc2)
            HCE.read(tsc2)
            CIN.read(tsc2)
            tcp1 = p.emit("dve", "tensor_copy", CC.h[:, j0 + 1:j0 + 32],
                          sh.h[:, 0:31], deps=[tsc2] + CC.wdep())
            CC.wrote(tcp1)
            sh.read(tcp1)
            tcp2 = p.emit("dve", "tensor_copy", CC.h[:, j0:j0 + 1],
                          CIN.h[:, jc // 2:jc // 2 + 1],
                          deps=CIN.rdep() + [tcp1])
            CC.wrote(tcp2)
            CIN.read(tcp2)
    for dt in range(NDT):
        for b in range(B):
            t0 = b * 512
            j0 = (dt * 4 + b) * 32
            cumt = c.XC[0]
            tk1 = p.dma("pool", cumt.h[:, 0:256].bitcast(BF16)[:, 0:512],
                        io["cumd"][dt][:, t0:t0 + 512],
                        deps=cumt.wdep() + io["cumd_buf"].rdep())
            cumt.wrote(tk1)
            io["cumd_buf"].read(tk1)
            hct = c.XC[1]
            tk2 = p.dma("pool", hct.h[:, 0:256].bitcast(BF16)[:, 0:512],
                        io["hcd"][dt][:, t0:t0 + 512],
                        deps=hct.wdep() + io["hcd_buf"].rdep())
            hct.wrote(tk2)
            io["hcd_buf"].read(tk2)
            hall = c.XC[2]
            tmu = p.emit("dve", "tensor_tensor",
                         hall.h[:, 0:512].rearrange(
                             "p (n k) -> p n k", k=CS),
                         cumt.h[:, 0:256].bitcast(BF16)[:, 0:512].rearrange(
                             "p (n k) -> p n k", k=CS),
                         CC.h[:, j0:j0 + 32].broadcast_to([128, NCH, CS]),
                         ALU.mult,
                         deps=[tk1] + CC.rdep() + hall.wdep())
            hall.wrote(tmu)
            CC.read(tmu)
            cumt.read(tmu)
            tad = p.emit("dve", "tensor_add",
                         c.XN[dt].h[:, t0:t0 + 512], hall.h[:, 0:512],
                         hct.h[:, 0:256].bitcast(BF16)[:, 0:512],
                         deps=[tmu, tk2] + c.XN[dt].wdep())
            c.XN[dt].wrote(tad)
            hall.read(tad)
            hct.read(tad)

    def wout_evac(ht, tcI, t0, w, ps):
        tx = p.emit("dve", "tensor_add", c.X[ht].h[:, t0:t0 + w],
                    c.X[ht].h[:, t0:t0 + w], ps.h[:, 0:w],
                    deps=ps.rdep() + c.X[ht].wdep())
        c.X[ht].wrote(tx)
        ps.read(tx)

    gemm_ht(p, c, io["wout_t"], NDT, NDT, c.XN, T, wout_evac)


def declare_l1_attn_io(nc, p, c, pref="L1a"):
    io = {}
    io["win_t"] = nc.dram_tensor(f"{pref}_win_t", [8, 128, 1024], F32,
                                 kind="ExternalInput").ap()
    io["wout_t"] = nc.dram_tensor(f"{pref}_wout_t", [8, 128, 1024], F32,
                                  kind="ExternalInput").ap()
    io["wg_in"] = nc.dram_tensor(f"{pref}_wg", [128, 256], F32,
                                 kind="ExternalInput").ap()
    io["lnv"] = nc.dram_tensor(f"{pref}_lnv", [128, 16], F32,
                               kind="ExternalInput").ap()
    io["eig_in"] = nc.dram_tensor(f"{pref}_eig", [16, 1], F32,
                                  kind="ExternalInput").ap()
    io["bg_in"] = nc.dram_tensor(f"{pref}_bg", [32, 1], F32,
                                 kind="ExternalInput").ap()
    io["rmask_in"] = nc.dram_tensor(f"{pref}_rmask", [128, 8], F32,
                                    kind="ExternalInput").ap()
    io["xpd"] = nc.dram_tensor(f"{pref}_xpd", [8, 128, T], BF16,
                               kind="Internal").ap()
    io["xpd_buf"] = Buf(io["xpd"])
    io["cumd"] = nc.dram_tensor(f"{pref}_cumd", [8, 128, T], BF16,
                                kind="Internal").ap()
    io["cumd_buf"] = Buf(io["cumd"])
    io["hcd"] = nc.dram_tensor(f"{pref}_hcd", [8, 128, T], BF16,
                               kind="Internal").ap()
    io["hcd_buf"] = Buf(io["hcd"])
    io["scc_in"] = Buf(nc.dram_tensor(f"{pref}_sccin", [8192], F32,
                                      kind="Internal").ap())
    io["scc_out"] = Buf(nc.dram_tensor(f"{pref}_sccout", [65536], F32,
                                       kind="Internal",
                                       addr_space="Shared").ap())
    wg = c.scratch([128, 256], BF16)
    tk = p.dma("pool", wg.h[:, 0:256], io["wg_in"], deps=wg.wdep(),
               key=wg.h.name)
    wg.wrote(tk)
    io["wg_t"] = wg
    lnv = c.scratch([128, 16], F32)
    tk2 = p.dma("sp", lnv.h[:, 0:16], io["lnv"], deps=lnv.wdep(),
                key=lnv.h.name)
    lnv.wrote(tk2)
    io["lng"] = _subbuf(lnv, 0, 8)
    io["lnb"] = _subbuf(lnv, 8, 8)
    eig = c.scratch([16, 2], F32)
    tk3 = p.dma("sp", eig.h[0:16, 0:1], io["eig_in"], deps=eig.wdep(),
                key=eig.h.name)
    eig.wrote(tk3)
    io["eig"] = eig
    bg = c.scratch([32, 1], F32)
    tk4 = p.dma("sp", bg.h[0:32, 0:1], io["bg_in"], deps=bg.wdep(),
                key=bg.h.name)
    bg.wrote(tk4)
    io["bgate"] = bg
    rm = c.scratch([128, 8], F32)
    tk5 = p.dma("sp", rm.h[:, 0:8], io["rmask_in"], deps=rm.wdep(),
                key=rm.h.name)
    rm.wrote(tk5)
    io["rmask"] = rm
    io["ans"] = c.scratch([128, 1024], F32)
    io["hce"] = c.scratch([128, 1024], F32)
    io["pv"] = c.scratch([128, 64], F32)
    io["cinb"] = c.scratch([128, 32], F32)
    io["ccar"] = io["hce"]
    return io


def stage_l1_attn(inputs, pref="L1a"):
    def tile_od(w):
        O, Din = w.shape
        no, nd = O // 128, Din // 128
        t = w.reshape(no, 128, nd, 128).transpose(0, 3, 2, 1)
        return np.ascontiguousarray(t.reshape(no, 128, nd * 128),
                                    dtype=np.float32)

    lnv = np.zeros((128, 16), np.float32)
    lnv[:, 0:8] = inputs["L1_attn_ln_g"].reshape(8, 128).T
    lnv[:, 8:16] = inputs["L1_attn_ln_b"].reshape(8, 128).T
    wg = np.ascontiguousarray(
        inputs["L1_attn_W_gate"].reshape(32, 8, 128).transpose(2, 1, 0)
        .reshape(128, 256))
    wg = np.ascontiguousarray(
        inputs["L1_attn_W_gate"].reshape(32, 8, 128)
        .transpose(2, 1, 0).reshape(128, 256))
    base = {
        f"{pref}_win_t": tile_od(inputs["L1_attn_W_in"]),
        f"{pref}_wout_t": tile_od(inputs["L1_attn_W_out"]),
        f"{pref}_wg": wg.astype(np.float32),
        f"{pref}_lnv": lnv,
        f"{pref}_eig": inputs["L1_attn_eig_raw"].reshape(16, 1).astype(np.float32),
        f"{pref}_bg": inputs["L1_attn_b_gate"].reshape(32, 1).astype(np.float32),
    }
    per_core = []
    for cc in range(NCORES):
        rmask = np.zeros((128, 8), np.float32)
        rmask[:, cc] = 1.0
        per_core.append({f"{pref}_rmask": rmask, **base})
    return per_core


def l3_attn(p, c, io, st):
    import math
    EPS3 = 0.001
    ln_full(p, c, c.X, c.XN)

    qb = io["qkvb"]

    def qkv_evac(ht, tcI, t0, w, ps):
        hb = c.HT[2]
        func = AF.Relu if ht < 16 else AF.Identity
        ta = p.emit("act", "activation", hb.h[:, t0:t0 + w], ps.h[:, 0:w],
                    func, bias=qb.h[:, ht:ht + 1],
                    deps=ps.rdep() + qb.rdep() + hb.wdep())
        hb.wrote(ta)
        ps.read(ta)
        qb.read(ta)
        if t0 + w == T:
            ts = p.dma("sp", io["qkvd"][ht], hb.h[:, 0:T], deps=hb.rdep(),
                       key="qkvspill")
            hb.read(ts)
            io["qkvd_buf"].wrote(ts)

    gemm_ht(p, c, io["qkv_t"], NDT, 24, c.XN, T, qkv_evac)

    HS = io["hsel"]
    tz = p.emit("pool", "memset", HS.h[:, 0:128], 0.0, deps=HS.wdep())
    HS.wrote(tz)
    for dt in range(NDT):
        t1 = p.emit("pool", "memset",
                    HS.h[0:64, dt * 16 + 2 * dt:dt * 16 + 2 * dt + 1], 1.0,
                    deps=[tz])
        HS.wrote(t1)
        t2 = p.emit("pool", "memset",
                    HS.h[64:128, dt * 16 + 2 * dt + 1:dt * 16 + 2 * dt + 2],
                    1.0, deps=[tz])
        HS.wrote(t2)
    SELH = c.HT[2]
    tmx = p.emit("pool", "memset", SELH.h[0:16, 0:1024], 1.0,
                 deps=SELH.wdep())
    SELH.wrote(tmx)
    for dt in range(NDT):
        ts_ = p.emit("pool", "affine_select",
                     SELH.h[0:16, dt * 128:dt * 128 + 128],
                     SELH.h[0:16, dt * 128:dt * 128 + 128],
                     pattern=[[1, 2], [0, 64]], compare_op=ALU.is_equal,
                     fill=0.0, base=2 * dt, channel_multiplier=-1,
                     deps=[tmx])
        SELH.wrote(ts_)

    theta = [math.pi / 2 * (l + 1) / 4.0 for l in range(4)]
    qvb = io["qkvd_buf"]
    DEN = io["den"]
    for l in range(4):
        qsl = []
        for dt in range(NDT):
            xb = c.XC[dt // 2]
            ap = xb.h[:, 0:512].bitcast(BF16)[:, (dt % 2) * 512:(dt % 2) * 512 + 512]
            tk = p.dma("pool", ap, io["qkvd"][dt][:, l * 512:(l + 1) * 512],
                       deps=(xb.wdep() if dt % 2 == 0 else []) + qvb.rdep(),
                       key=xb.h.name)
            xb.wrote(tk)
            qvb.read(tk)
            qsl.append((xb, ap))
        for dt in range(0, NDT, 2):
            c.XC[dt // 2].wrote(p.tok("dma.pool:" + c.XC[dt // 2].h.name))
        qk_ps = [c.PS[4 + s] for s in range(4)]
        for s_ in range(4):
            ksl = []
            for dt in range(NDT):
                xb = c.XC[4 + dt // 2]
                ap = xb.h[:, 0:512].bitcast(BF16)[:, (dt % 2) * 512:(dt % 2) * 512 + 512]
                tk = p.dma("pool", ap,
                           io["qkvd"][8 + dt][:, s_ * 512:(s_ + 1) * 512],
                           deps=(xb.wdep() if dt % 2 == 0 else [])
                           + qvb.rdep(), key=xb.h.name)
                xb.wrote(tk)
                qvb.read(tk)
                ksl.append((xb, ap))
            for dt in range(0, NDT, 2):
                b_ = c.XC[4 + dt // 2]
                b_.wrote(p.tok("dma.pool:" + b_.h.name))
            ps = qk_ps[s_]
            mm = None
            for dt in range(NDT):
                mb, map_ = qsl[dt]
                kb, kap = ksl[dt]
                mt = c.SH[0]
                tmu = p.emit("dve", "tensor_mul",
                             mt.h[:, 0:256].bitcast(BF16)[:, 0:512],
                             map_, kap,
                             deps=mb.rdep() + kb.rdep() + mt.wdep())
                mt.wrote(tmu)
                mb.read(tmu)
                kb.read(tmu)
                mmb = c.SH[1]
                tcv = p.emit("dve", "tensor_copy",
                             mmb.h[:, 0:8].bitcast(BF16)[:, 0:16],
                             HS.h[:, dt * 16:dt * 16 + 16],
                             deps=HS.rdep() + mmb.wdep())
                mmb.wrote(tcv)
                HS.read(tcv)
                mm = p.emit("pe", "matmul", ps.h[0:16, 0:512],
                            mmb.h[:, 0:8].bitcast(BF16)[:, 0:16],
                            mt.h[:, 0:256].bitcast(BF16)[:, 0:512],
                            start=(dt == 0), stop=(dt == NDT - 1),
                            deps=(ps.wdep() if dt == 0 else [])
                            + [tmu, tcv], inc=True)
                mmb.read(mm)
                mt.read(mm)
            ps.wrote(mm)
        tz1 = p.emit("pool", "memset", DEN.h[0:16, 0:512], 0.0,
                     deps=DEN.wdep())
        DEN.wrote(tz1)
        for s_ in range(4):
            cosv = math.cos(theta[l] - theta[s_])
            tac = p.emit("dve", "scalar_tensor_tensor", DEN.h[0:16, 0:512],
                         qk_ps[s_].h[0:16, 0:512], cosv, DEN.h[0:16, 0:512],
                         ALU.mult, ALU.add,
                         deps=[tz1] + qk_ps[s_].rdep() + DEN.wdep())
            DEN.wrote(tac)
            qk_ps[s_].read(tac)
        tm1 = p.emit("dve", "tensor_scalar_max", DEN.h[0:16, 0:512],
                     DEN.h[0:16, 0:512], EPS3, deps=DEN.wdep())
        DEN.wrote(tm1)
        tm2 = p.emit("dve", "reciprocal", DEN.h[0:16, 0:512],
                     DEN.h[0:16, 0:512], deps=[tm1])
        DEN.wrote(tm2)
        for s_ in range(4):
            cosv = math.cos(theta[l] - theta[s_])
            zq = c.SH[0]
            tzq = p.emit("dve", "tensor_mul",
                         zq.h[0:16, 0:256].bitcast(BF16)[:, 0:512],
                         qk_ps[s_].h[0:16, 0:512], DEN.h[0:16, 0:512],
                         deps=qk_ps[s_].rdep() + DEN.rdep() + zq.wdep())
            zq.wrote(tzq)
            qk_ps[s_].read(tzq)
            DEN.read(tzq)
            for dt in range(NDT):
                hb = c.HT[dt // 4]
                vap = hb.h[:, (dt % 4) * 512:(dt % 4) * 512 + 512]
                tk = p.dma("pool", vap,
                           io["qkvd"][16 + dt][:, s_ * 512:(s_ + 1) * 512],
                           deps=(hb.wdep() if dt % 4 == 0 else [])
                           + qvb.rdep(), key=hb.h.name)
                hb.wrote(tk)
                qvb.read(tk)
            for j in range(2):
                c.HT[j].wrote(p.tok("dma.pool:" + c.HT[j].h.name))
            for dt in range(NDT):
                hb = c.HT[dt // 4]
                vap = hb.h[:, (dt % 4) * 512:(dt % 4) * 512 + 512]
                psz = c.PS[(dt % 2)]
                mz = p.emit("pe", "matmul", psz.h[:, 0:512],
                            SELH.h[0:16, dt * 128:dt * 128 + 128],
                            zq.h[0:16, 0:256].bitcast(BF16)[:, 0:512],
                            start=True, stop=True,
                            deps=psz.wdep() + SELH.rdep() + [tzq], inc=True)
                SELH.read(mz)
                zq.read(mz)
                psz.wrote(mz)
                zbt = c.SH[1]
                te = p.emit("dve", "tensor_mul", zbt.h[:, 0:512],
                            psz.h[:, 0:512], vap, deps=[mz] + hb.rdep()
                            + zbt.wdep())
                zbt.wrote(te)
                psz.read(te)
                hb.read(te)
                if s_ == 0:
                    ta2 = p.emit("dve", "tensor_scalar_mul",
                                 c.XN[dt].h[:, l * 512:(l + 1) * 512],
                                 zbt.h[:, 0:512], cosv,
                                 deps=[te] + c.XN[dt].wdep())
                else:
                    ta2 = p.emit("dve", "scalar_tensor_tensor",
                                 c.XN[dt].h[:, l * 512:(l + 1) * 512],
                                 zbt.h[:, 0:512], cosv,
                                 c.XN[dt].h[:, l * 512:(l + 1) * 512],
                                 ALU.mult, ALU.add,
                                 deps=[te] + c.XN[dt].wdep())
                c.XN[dt].wrote(ta2)
                zbt.read(ta2)

    ob = io["outb"]

    def oproj_evac(ht, tcI, t0, w, ps):
        sh = c.SH[0]
        ta = p.emit("act", "activation", sh.h[:, 0:w], ps.h[:, 0:w],
                    AF.Identity, bias=ob.h[:, ht:ht + 1],
                    deps=ps.rdep() + ob.rdep() + sh.wdep())
        sh.wrote(ta)
        ps.read(ta)
        ob.read(ta)
        tx = p.emit("dve", "tensor_add", c.X[ht].h[:, t0:t0 + w],
                    c.X[ht].h[:, t0:t0 + w], sh.h[:, 0:w],
                    deps=[ta] + c.X[ht].wdep())
        c.X[ht].wrote(tx)
        sh.read(tx)

    gemm_ht(p, c, io["outw_t"], NDT, NDT, c.XN, T, oproj_evac)


def declare_l3_attn_io(nc, p, c, pref="L3a"):
    io = {}
    io["qkv_t"] = nc.dram_tensor(f"{pref}_qkv_t", [24, 128, 1024], F32,
                                 kind="ExternalInput").ap()
    io["outw_t"] = nc.dram_tensor(f"{pref}_outw_t", [8, 128, 1024], F32,
                                  kind="ExternalInput").ap()
    io["vecs"] = nc.dram_tensor(f"{pref}_vecs", [128, 32], F32,
                                kind="ExternalInput").ap()
    io["qkvd"] = nc.dram_tensor(f"{pref}_qkvd", [24, 128, T], BF16,
                                kind="Internal").ap()
    io["qkvd_buf"] = Buf(io["qkvd"])
    vec = c.scratch([128, 32], F32)
    tk = p.dma("sp", vec.h[:, 0:32], io["vecs"], deps=vec.wdep(),
               key=vec.h.name)
    vec.wrote(tk)
    io["qkvb"] = _subbuf(vec, 0, 24)
    io["outb"] = _subbuf(vec, 24, 8)
    io["hsel"] = c.scratch([128, 128], F32)
    io["den"] = c.scratch([16, 512], F32)
    return io


def stage_l3_attn(inputs, pref="L3a"):
    def tile_od(w):
        O, Din = w.shape
        no, nd = O // 128, Din // 128
        t = w.reshape(no, 128, nd, 128).transpose(0, 3, 2, 1)
        return np.ascontiguousarray(t.reshape(no, 128, nd * 128),
                                    dtype=np.float32)

    vecs = np.zeros((128, 32), np.float32)
    vecs[:, 0:24] = inputs["L3_attn_in_proj_bias"].reshape(24, 128).T
    vecs[:, 24:32] = inputs["L3_attn_out_b"].reshape(8, 128).T
    return {
        f"{pref}_qkv_t": tile_od(inputs["L3_attn_in_proj_weight"]),
        f"{pref}_outw_t": tile_od(inputs["L3_attn_out_w"]),
        f"{pref}_vecs": vecs,
    }


def l2_attn(p, c, io, st):
    SCL = 128.0 ** -0.5
    ln_full(p, c, c.X, c.XN)

    rot = io["rot"]
    cosb = io["cosb"]
    sinb = io["sinb"]
    qoff = io["qoff"]

    cin, cout = io["acc_in"], io["acc_out"]

    def qkv_evac(ht, tcI, t0, w, ps):
        cosp, sinp = c.XC[0], c.XC[1]
        if ht < 3:
            raw = c.SH[0]
            rbf = raw.h[:, 0:256].bitcast(BF16)
            t1 = p.emit("dve", "tensor_copy", rbf[:, 0:w], ps.h[:, 0:w],
                        deps=ps.rdep() + raw.wdep())
            raw.wrote(t1)
            ps.read(t1)
            psr = c.PS[2 + ht % 2]
            mr = p.emit("pe", "matmul", psr.h[:, 0:w], rot.h[0:128, 0:128],
                        rbf[:, 0:w], start=True, stop=True,
                        deps=psr.wdep() + rot.rdep() + [t1], inc=True)
            rot.read(mr)
            raw.read(mr)
            psr.wrote(mr)
            t2 = p.emit("dve", "tensor_mul", cosp.h[:, 0:w], ps.h[:, 0:w],
                        cosb.h[:, 0:w],
                        deps=ps.rdep() + cosb.rdep() + cosp.wdep())
            cosp.wrote(t2)
            ps.read(t2)
            cosb.read(t2)
            t3 = p.emit("dve", "tensor_mul", sinp.h[:, 0:w], psr.h[:, 0:w],
                        sinb.h[:, 0:w],
                        deps=[mr] + sinb.rdep() + sinp.wdep())
            sinp.wrote(t3)
            psr.read(t3)
            sinb.read(t3)
            t4 = p.emit("dve", "tensor_add", cosp.h[:, 0:w], cosp.h[:, 0:w],
                        sinp.h[:, 0:w], deps=[t2, t3])
            cosp.wrote(t4)
            sinp.read(t4)
            if ht < 2:
                t5 = p.emit("dve", "tensor_scalar_mul",
                            c.HT[ht].h[:, t0:t0 + w], cosp.h[:, 0:w],
                            SCL, deps=[t4] + c.HT[ht].wdep())
                c.HT[ht].wrote(t5)
                cosp.read(t5)
            else:
                td = p.dma("pool", io["kslot"][:, t0:t0 + w],
                           cosp.h[:, 0:w], deps=[t4], key="kstage")
                cosp.read(td)
                cin.wrote(td)
        else:
            t1 = p.emit("dve", "tensor_copy", cosp.h[:, 0:w], ps.h[:, 0:w],
                        deps=ps.rdep() + cosp.wdep())
            cosp.wrote(t1)
            ps.read(t1)
            td = p.dma("pool", io["vslot"][:, t0:t0 + w], cosp.h[:, 0:w],
                       deps=[t1], key="vstage")
            cosp.read(td)
            cin.wrote(td)

    gemm_ht(p, c, io["qkv_t"], NDT, 4, c.XN, T, qkv_evac)

    tcc = collective(p, "AllGather", ALU.bypass,
                     [cin.h[0:2 * 128 * T].rearrange("(a q) -> a q", q=4096)],
                     [cout.h[0:16 * 128 * T].rearrange("(a q) -> a q",
                                                       q=4096)],
                     deps=cin.rdep() + cout.wdep())
    cout.wrote(tcc)
    cin.read(tcc)

    J = io["jt"]
    tj = p.emit("pool", "iota", J.h[:, 0:512], pattern=[[1, 512]], base=0,
                channel_multiplier=-1, allow_small_or_imprecise_dtypes=True,
                deps=J.wdep())
    J.wrote(tj)
    VTM = c.SC[0]
    SS = c.SC[1]
    MM8 = io["mm8"]
    AT = [c.HT[0], c.HT[1]]
    kvw = io["kv_view"]

    for b in range(B):
        for r in range(NCORES):
            vsl = c.HT[2]
            tk = p.dma("pool", vsl.h[:, 0:512],
                       kvw(r, "v")[:, b * 512:(b + 1) * 512],
                       deps=vsl.wdep() + [tcc], key="vload")
            vsl.wrote(tk)
            cout.read(tk)
            for kb_ in range(4):
                pst = c.PS[4 + kb_ % 2]
                tt = p.emit("pe", "transpose",
                            pst.h[:, 0:64].bitcast(BF16),
                            vsl.h[:, kb_ * 128:(kb_ + 1) * 128],
                            c.ident_b[:],
                            deps=vsl.rdep() + pst.wdep() + c.setup_tok)
                vsl.read(tt)
                pst.wrote(tt)
                col = (r * 4 + kb_) * 128
                te = p.emit("dve", "tensor_copy",
                            VTM.h[:, 0:2048].bitcast(BF16)[:, col:col + 128],
                            pst.h[:, 0:64].bitcast(BF16),
                            deps=[tt] + VTM.wdep())
                pst.read(te)
                VTM.wrote(te)
        for h in range(2):
            for qb_ in range(4):
                q0 = b * 512 + qb_ * 128
                for r in range(NCORES):
                    ksl = c.HT[2]
                    tk = p.dma("pool", ksl.h[:, 0:512],
                               kvw(r, "k")[:, b * 512:(b + 1) * 512],
                               deps=ksl.wdep() + [tcc], key="kload")
                    ksl.wrote(tk)
                    cout.read(tk)
                    pss = c.PS[r % 2]
                    ms = p.emit("pe", "matmul", pss.h[:, 0:512],
                                c.HT[h].h[:, q0:q0 + 128],
                                ksl.h[:, 0:512], start=True, stop=True,
                                deps=pss.wdep() + c.HT[h].rdep() + [tk],
                                inc=True)
                    c.HT[h].read(ms)
                    ksl.read(ms)
                    pss.wrote(ms)
                    mk = c.SH[0]
                    tmk = p.emit("dve", "tensor_scalar", mk.h[:, 0:512],
                                 J.h[:, 0:512],
                                 qoff.h[:, r * 4 + qb_:r * 4 + qb_ + 1],
                                 -30000.0, ALU.is_gt, ALU.mult,
                                 deps=J.rdep() + qoff.rdep() + mk.wdep())
                    mk.wrote(tmk)
                    J.read(tmk)
                    qoff.read(tmk)
                    tad = p.emit("dve", "tensor_tensor", pss.h[:, 0:512],
                                 pss.h[:, 0:512], mk.h[:, 0:512], ALU.add,
                                 deps=[ms, tmk])
                    pss.wrote(tad)
                    mk.read(tad)
                    tmx = p.emit("dve", "tensor_reduce",
                                 MM8.h[:, r:r + 1], pss.h[:, 0:512],
                                 axis=AX.X, op=ALU.max,
                                 deps=[tad] + MM8.wdep())
                    MM8.wrote(tmx)
                    pss.read(tmx)
                    tst = p.emit("dve", "tensor_copy",
                                 SS.h[:, 0:2048].bitcast(BF16)
                                 [:, r * 512:(r + 1) * 512],
                                 pss.h[:, 0:512], deps=[tad] + SS.wdep())
                    SS.wrote(tst)
                    pss.read(tst)
                tM = p.emit("dve", "tensor_reduce", MM8.h[:, 8:9],
                            MM8.h[:, 0:8], axis=AX.X, op=ALU.max,
                            deps=MM8.wdep())
                MM8.wrote(tM)
                tN = p.emit("dve", "tensor_scalar_mul", MM8.h[:, 9:10],
                            MM8.h[:, 8:9], -1.0, deps=[tM])
                MM8.wrote(tN)
                acc = c.PS[6]
                first_av = True
                for r in range(NCORES):
                    pp = c.SH[1]
                    tex = p.emit("act", "activation",
                                 pp.h[:, 0:256].bitcast(BF16)[:, 0:512],
                                 SS.h[:, 0:2048].bitcast(BF16)
                                 [:, r * 512:(r + 1) * 512],
                                 AF.Exp, bias=MM8.h[:, 9:10],
                                 accum_out=MM8.h[:, 10 + r:11 + r],
                                 deps=SS.rdep() + [tN] + pp.wdep()
                                 + MM8.wdep())
                    pp.wrote(tex)
                    SS.read(tex)
                    MM8.wrote(tex)
                    MM8.read(tex)
                    for kb_ in range(4):
                        pst = c.PS[4 + kb_ % 2]
                        tt = p.emit("pe", "transpose",
                                    pst.h[:, 0:64].bitcast(BF16),
                                    pp.h[:, 0:256].bitcast(BF16)
                                    [:, kb_ * 128:(kb_ + 1) * 128],
                                    c.ident_b[:],
                                    deps=[tex] + pst.wdep() + c.setup_tok)
                        pp.read(tt)
                        pst.wrote(tt)
                        ptb = c.SH[0]
                        te = p.emit("dve", "tensor_copy",
                                    ptb.h[:, 0:64].bitcast(BF16),
                                    pst.h[:, 0:64].bitcast(BF16),
                                    deps=[tt] + ptb.wdep())
                        pst.read(te)
                        ptb.wrote(te)
                        col = (r * 4 + kb_) * 128
                        mav = p.emit(
                            "pe", "matmul", acc.h[:, 0:128],
                            VTM.h[:, 0:2048].bitcast(BF16)[:, col:col + 128],
                            ptb.h[:, 0:64].bitcast(BF16),
                            start=first_av,
                            stop=(r == NCORES - 1 and kb_ == 3),
                            deps=(acc.wdep() if first_av else [])
                            + VTM.rdep() + [te],
                            inc=(r == NCORES - 1 and kb_ == 3))
                        first_av = False
                        VTM.read(mav)
                        ptb.read(mav)
                        acc.wrote(mav)
                tl = p.emit("dve", "tensor_reduce", MM8.h[:, 18:19],
                            MM8.h[:, 10:18], axis=AX.X, op=ALU.add,
                            deps=MM8.wdep())
                MM8.wrote(tl)
                tz = p.emit("dve", "reciprocal", MM8.h[:, 19:20],
                            MM8.h[:, 18:19], deps=[tl])
                MM8.wrote(tz)
                psz = c.PS[5]
                tt2 = p.emit("pe", "transpose", psz.h[0:1, 0:128],
                             MM8.h[:, 19:20], c.ident_f[:],
                             deps=[tz] + psz.wdep() + c.setup_tok)
                MM8.read(tt2)
                psz.wrote(tt2)
                zr2 = c.SH[0]
                te2 = p.emit("dve", "tensor_copy", zr2.h[0:1, 0:128],
                             psz.h[0:1, 0:128], deps=[tt2] + zr2.wdep())
                psz.read(te2)
                zr2.wrote(te2)
                psb = c.PS[4]
                mb2 = p.emit("pe", "matmul", psb.h[:, 0:128], c.ones_f1[:],
                             zr2.h[0:1, 0:128], start=True, stop=True,
                             deps=[te2] + psb.wdep() + c.setup_tok,
                             inc=True)
                zr2.read(mb2)
                psb.wrote(mb2)
                zb2 = c.SH[1]
                te3 = p.emit("dve", "tensor_copy", zb2.h[:, 0:128],
                             psb.h[:, 0:128], deps=[mb2] + zb2.wdep())
                psb.read(te3)
                zb2.wrote(te3)
                tf = p.emit("dve", "tensor_mul",
                            c.XN[h].h[:, q0:q0 + 128] if False else
                            c.XN[h].h[:, b * 512 + qb_ * 128:
                                      b * 512 + qb_ * 128 + 128],
                            acc.h[:, 0:128], zb2.h[:, 0:128],
                            deps=acc.rdep() + [te3] + c.XN[h].wdep())
                c.XN[h].wrote(tf)
                acc.read(tf)
                zb2.read(tf)

    def wo_evac(ht, tcI, t0, w, ps):
        tx = p.emit("dve", "tensor_add", c.X[ht].h[:, t0:t0 + w],
                    c.X[ht].h[:, t0:t0 + w], ps.h[:, 0:w],
                    deps=ps.rdep() + c.X[ht].wdep())
        c.X[ht].wrote(tx)
        ps.read(tx)

    gemm_ht(p, c, io["wo_t"], 2, NDT, c.XN[0:2], T, wo_evac)


def declare_l2_attn_io(nc, p, c, pref="L2A"):
    io = {}
    io["qkv_t"] = nc.dram_tensor(f"{pref}_qkv_t", [4, 128, 1024], F32,
                                 kind="ExternalInput").ap()
    io["wo_t"] = nc.dram_tensor(f"{pref}_wo_t", [8, 128, 256], F32,
                                kind="ExternalInput").ap()
    io["rot_in"] = nc.dram_tensor(f"{pref}_rot", [128, 128], F32,
                                  kind="ExternalInput").ap()
    io["tabs"] = nc.dram_tensor(f"{pref}_tabs", [128, 1024 + 32], F32,
                                kind="ExternalInput").ap()
    cc_in = nc.dram_tensor(f"{pref}_cc_in", [2 * 128 * T], BF16,
                           kind="Internal").ap()
    cc_out = nc.dram_tensor(f"{pref}_cc_out", [16 * 128 * T], BF16,
                            kind="Internal", addr_space="Shared").ap()
    io["acc_in"] = Buf(cc_in)
    io["acc_out"] = Buf(cc_out)
    io["kslot"] = cc_in[0:128 * T].rearrange("(q n) -> q n", q=128)
    io["vslot"] = cc_in[128 * T:2 * 128 * T].rearrange("(q n) -> q n", q=128)

    def kv_view(r, which):
        off = r * 2 * 128 * T + (0 if which == "k" else 128 * T)
        return cc_out[off:off + 128 * T].rearrange("(q n) -> q n", q=128)

    io["kv_view"] = kv_view
    rot = c.scratch([128, 128], BF16)
    tk = p.dma("pool", rot.h[0:128, 0:128], io["rot_in"], deps=rot.wdep(),
               key=rot.h.name)
    rot.wrote(tk)
    io["rot"] = rot
    tabs = c.scratch([128, 1056], F32)
    tk2 = p.dma("sp", tabs.h[:, 0:1056], io["tabs"], deps=tabs.wdep(),
                key=tabs.h.name)
    tabs.wrote(tk2)
    io["cosb"] = _subbuf(tabs, 0, 512)
    io["sinb"] = _subbuf(tabs, 512, 512)
    io["qoff"] = _subbuf(tabs, 1024, 32)
    io["jt"] = c.scratch([128, 512], F32)
    io["mm8"] = c.scratch([128, 24], F32)
    return io


def stage_l2_attn(inputs, pref="L2A"):
    def tile_od(w):
        O, Din = w.shape
        no, nd = O // 128, Din // 128
        t = w.reshape(no, 128, nd, 128).transpose(0, 3, 2, 1)
        return np.ascontiguousarray(t.reshape(no, 128, nd * 128),
                                    dtype=np.float32)

    Wq = inputs["L2_attn_Wq"]
    qkv = np.concatenate([Wq[0], Wq[1], inputs["L2_attn_Wk"],
                          inputs["L2_attn_Wv"]], axis=0)
    rot = np.zeros((128, 128), np.float32)
    for i in range(64):
        rot[2 * i + 1, 2 * i] = -1.0
        rot[2 * i, 2 * i + 1] = 1.0
    base = {f"{pref}_qkv_t": tile_od(qkv),
            f"{pref}_wo_t": tile_od(inputs["L2_attn_Wo"]),
            f"{pref}_rot": rot}
    per_core = []
    d = 128
    positions = np.arange(S, dtype=np.float64)
    theta = 10000.0 ** (-2.0 * np.arange(0, d, 2) / d)
    for cc in range(NCORES):
        pos = positions[cc * SLOC:(cc + 1) * SLOC]
        ang = pos[None, :] * theta[:, None]
        cosb = np.repeat(np.cos(ang), 2, axis=0).astype(np.float32)
        sinb = np.repeat(np.sin(ang), 2, axis=0).astype(np.float32)
        tabs = np.zeros((128, 1056), np.float32)
        tabs[:, 0:512] = cosb
        tabs[:, 512:1024] = sinb
        for r in range(NCORES):
            for qb_ in range(4):
                tabs[:, 1024 + r * 4 + qb_] = 512.0 * (cc - r) + 128.0 * qb_
        per_core.append({f"{pref}_tabs": tabs, **base})
    return per_core


def tm_stage_and_ag(p, c, src_tiles, cin, cout, deps_extra=()):
    civ = cin.h[0:T * 1024].rearrange("(t q) -> t q", q=1024)
    for dt in range(NDT):
        for tb in range(16):
            pst = c.PS[4 + tb % 2]
            tt = p.emit("pe", "transpose", pst.h[:, 0:64].bitcast(BF16),
                        src_tiles[dt].h[:, tb * 128:(tb + 1) * 128],
                        c.ident_b[:],
                        deps=src_tiles[dt].rdep() + pst.wdep()
                        + c.setup_tok)
            src_tiles[dt].read(tt)
            pst.wrote(tt)
            sh = c.SH[tb % 2]
            te = p.emit("dve", "tensor_copy", sh.h[:, 0:64].bitcast(BF16),
                        pst.h[:, 0:64].bitcast(BF16), deps=[tt] + sh.wdep())
            pst.read(te)
            sh.wrote(te)
            td = p.dma("sp", civ[tb * 128:(tb + 1) * 128,
                                 dt * 128:(dt + 1) * 128],
                       sh.h[:, 0:64].bitcast(BF16), deps=[te],
                       key="tmstage" + str(tb % 2))
            sh.read(td)
            cin.wrote(td)
    tcc = collective(p, "AllGather", ALU.bypass,
                     [cin.h[0:T * 1024].rearrange("(a q) -> a q", q=4096)],
                     [cout.h[0:NCORES * T * 1024].rearrange(
                         "(a q) -> a q", q=4096)],
                     deps=cin.rdep() + cout.wdep() + list(deps_extra))
    cout.wrote(tcc)
    cin.read(tcc)
    return tcc


def l3_ffn(p, c, io, st):
    ln_full(p, c, c.X, c.XN)
    tcc = tm_stage_and_ag(p, c, c.XN, io["xcc_in"], io["xcc_out"])

    gidx = io["gidx"]
    xg = io["xcc_out"]
    xgv = xg.h[0:NCORES * T * 1024].rearrange("(t q) -> t q", q=1024)
    for grp in range(16):
        gt = c.HT[grp % 2]
        tg = p.idma(gt.h[:, 0:1024], None, xgv,
                    IndirectOffsetOnAxis(gidx.h[:, grp:grp + 1], 0),
                    deps=[tcc] + gt.wdep() + io["gidx_tok"],
                    key="xgather" + str(grp % 2))
        gt.wrote(tg)
        xg.read(tg)
        for dt in range(NDT):
            pst = c.PS[4 + dt % 2]
            tt = p.emit("pe", "transpose", pst.h[:, 0:64].bitcast(BF16),
                        gt.h[:, dt * 128:(dt + 1) * 128], c.ident_b[:],
                        deps=gt.rdep() + pst.wdep() + c.setup_tok)
            gt.read(tt)
            pst.wrote(tt)
            te = p.emit("dve", "tensor_copy",
                        c.XN[dt].h[:, grp * 128:(grp + 1) * 128],
                        pst.h[:, 0:64].bitcast(BF16),
                        deps=[tt] + c.XN[dt].wdep())
            pst.read(te)
            c.XN[dt].wrote(te)

    b1 = io["b1"]

    def fc1_evac(ht, tcI, t0, w, ps):
        hb = c.HT[2]
        ta = p.emit("act", "activation", hb.h[:, t0:t0 + w], ps.h[:, 0:w],
                    AF.Gelu, bias=b1.h[:, ht:ht + 1],
                    deps=ps.rdep() + b1.rdep() + hb.wdep())
        hb.wrote(ta)
        ps.read(ta)
        b1.read(ta)
        if t0 + w == T:
            ts = p.dma("sp", io["hd"][ht], hb.h[:, 0:T], deps=hb.rdep(),
                       key="hspill3")
            hb.read(ts)
            io["hd_buf"].wrote(ts)

    gemm_ht(p, c, io["w1_t"], NDT, 32, c.XN, T, fc1_evac)

    wrow = io["wrow"]
    Wb = c.SC[0]
    bcast_row(p, c, wrow, 0, T, Wb, scale=0.5)
    b2r = io["b2r"]

    def bias_cb(ot, t0, w, ps):
        mm = p.emit("pe", "matmul", ps.h[:, 0:w],
                    b2r.h[0:1, ot * 128:(ot + 1) * 128],
                    c.ones_row_bf[0:1, 0:w], start=False, stop=True,
                    deps=b2r.rdep() + c.setup_tok, inc=True)
        b2r.read(mm)
        return mm

    def flush_cb(ot, t0, tw, ps_pair):
        for h in range(ceil_div(tw, 512)):
            a = t0 + h * 512
            w = min(512, tw - h * 512)
            ps = ps_pair[h]
            tf = p.emit("dve", "tensor_mul", c.XN[ot].h[:, a:a + w],
                        ps.h[:, 0:w], Wb.h[:, a:a + w],
                        deps=ps.rdep() + Wb.rdep() + c.XN[ot].wdep())
            c.XN[ot].wrote(tf)
            ps.read(tf)
            Wb.read(tf)

    gemm_consume(p, c, io["hd"], 32, io["w2_s"], NDT, T, flush_cb,
                 bias_cb=bias_cb, h_buf=io["hd_buf"])

    tcc2 = tm_stage_and_ag(p, c, c.XN, io["occ_in"], io["occ_out"])

    bidx = io["bidx"]
    BT = io["bt"]
    ov = io["occ_out"].h[0:NCORES * T * 1024].rearrange("(t q) -> t q",
                                                        q=1024)
    for tile in range(16):
        cbufs = []
        for j in range(3):
            ct = (c.HT[j % 2] if j < 2 else c.HT[2])
            col = 0 if j < 2 else 0
            tg = p.idma(ct.h[:, 0:1024], None, ov,
                        IndirectOffsetOnAxis(bidx.h[:, tile * 3 + j:
                                                    tile * 3 + j + 1], 0),
                        deps=[tcc2] + ct.wdep() + io["bidx_tok"],
                        key="cgather" + str(j))
            ct.wrote(tg)
            io["occ_out"].read(tg)
            cbufs.append(ct)
        bb = c.SH[0]
        tb2 = p.dma("pool", bb.h[:, 0:192].bitcast(BF16)[:, 0:384],
                    BT.h[tile], deps=bb.wdep(), key="btload")
        bb.wrote(tb2)
        for dt in range(NDT):
            psx = c.PS[dt % 2]
            mm = None
            for j in range(3):
                mm = p.emit("pe", "matmul", psx.h[:, 0:128],
                            cbufs[j].h[:, dt * 128:(dt + 1) * 128],
                            bb.h[:, 0:192].bitcast(BF16)
                            [:, j * 128:(j + 1) * 128],
                            start=(j == 0), stop=(j == 2),
                            deps=(psx.wdep() if j == 0 else [])
                            + cbufs[j].rdep() + [tb2], inc=True)
                cbufs[j].read(mm)
                bb.read(mm)
            psx.wrote(mm)
            tx = p.emit("dve", "tensor_add",
                        c.X[dt].h[:, tile * 128:(tile + 1) * 128],
                        c.X[dt].h[:, tile * 128:(tile + 1) * 128],
                        psx.h[:, 0:128], deps=[mm] + c.X[dt].wdep())
            c.X[dt].wrote(tx)
            psx.read(tx)


def declare_l3_ffn_io(nc, p, c, pref="L3f"):
    io = {}
    io["w1_t"] = nc.dram_tensor(f"{pref}_w1_t", [32, 128, 1024], F32,
                                kind="ExternalInput").ap()
    io["w2_s"] = nc.dram_tensor(f"{pref}_w2_s", [2, 4, 128, 4096], F32,
                                kind="ExternalInput").ap()
    io["vecs"] = nc.dram_tensor(f"{pref}_vecs", [128, 32], F32,
                                kind="ExternalInput").ap()
    io["b2_in"] = nc.dram_tensor(f"{pref}_b2", [1, 1024], F32,
                                 kind="ExternalInput").ap()
    io["wrow_in"] = nc.dram_tensor(f"{pref}_wrow", [1, T], F32,
                                   kind="ExternalInput").ap()
    io["gidx_in"] = nc.dram_tensor(f"{pref}_gidx", [128, 16], I32,
                                   kind="ExternalInput").ap()
    io["bidx_in"] = nc.dram_tensor(f"{pref}_bidx", [128, 48], I32,
                                   kind="ExternalInput").ap()
    io["bt"] = Buf(nc.dram_tensor(f"{pref}_bt", [16, 128, 384], F32,
                                  kind="ExternalInput").ap())
    io["hd"] = nc.dram_tensor(f"{pref}_hd", [32, 128, T], BF16,
                              kind="Internal").ap()
    io["hd_buf"] = Buf(io["hd"])
    io["xcc_in"] = Buf(nc.dram_tensor(f"{pref}_xccin", [T * 1024], BF16,
                                      kind="Internal").ap())
    io["xcc_out"] = Buf(nc.dram_tensor(f"{pref}_xccout",
                                       [NCORES * T * 1024], BF16,
                                       kind="Internal",
                                       addr_space="Shared").ap())
    io["occ_in"] = Buf(nc.dram_tensor(f"{pref}_occin", [T * 1024], BF16,
                                      kind="Internal").ap())
    io["occ_out"] = Buf(nc.dram_tensor(f"{pref}_occout",
                                       [NCORES * T * 1024], BF16,
                                       kind="Internal",
                                       addr_space="Shared").ap())
    vec = c.scratch([128, 32], F32)
    tk = p.dma("sp", vec.h[:, 0:32], io["vecs"], deps=vec.wdep(),
               key=vec.h.name)
    vec.wrote(tk)
    io["b1"] = vec
    b2r = c.scratch([1, 1024], BF16)
    tk2 = p.dma("pool", b2r.h[0:1, 0:1024], io["b2_in"], deps=b2r.wdep(),
                key=b2r.h.name)
    b2r.wrote(tk2)
    io["b2r"] = b2r
    wrow = c.scratch([1, T], F32)
    tk3 = p.dma("sp", wrow.h[0:1, 0:T], io["wrow_in"], deps=wrow.wdep(),
                key=wrow.h.name)
    wrow.wrote(tk3)
    io["wrow"] = wrow
    gidx = c.scratch([128, 16], I32)
    tk4 = p.dma("sp", gidx.h[:, 0:16], io["gidx_in"], deps=gidx.wdep(),
                key=gidx.h.name)
    gidx.wrote(tk4)
    io["gidx"] = gidx
    io["gidx_tok"] = [tk4]
    bidx = c.scratch([128, 48], I32)
    tk5 = p.dma("sp", bidx.h[:, 0:48], io["bidx_in"], deps=bidx.wdep(),
                key=bidx.h.name)
    bidx.wrote(tk5)
    io["bidx"] = bidx
    io["bidx_tok"] = [tk5]
    return io


def stage_l3_ffn(inputs, pref="L3f"):
    def tile_do(w):
        Din, Hn = w.shape
        nd, nh = Din // 128, Hn // 128
        t = w.reshape(nd, 128, nh, 128).transpose(2, 1, 0, 3)
        return np.ascontiguousarray(t.reshape(nh, 128, nd * 128),
                                    dtype=np.float32)

    def tile_slab_do(w):
        Hn, Dn = w.shape
        n_g = Hn // 1024
        t = w.reshape(n_g, 8, 128, 2, 4, 128).transpose(3, 0, 2, 1, 4, 5)
        return np.ascontiguousarray(t.reshape(2, n_g, 128, 4096),
                                    dtype=np.float32)

    top_idx = inputs["L3_ffn_top_idx"].astype(np.int64)
    cw = inputs["L3_ffn_combine_weights"]
    W1 = inputs["L3_ffn_W1"]; b1 = inputs["L3_ffn_b1"]
    W2 = inputs["L3_ffn_W2"]; b2 = inputs["L3_ffn_b2"]

    def ag_row(n):
        b = n // 4096
        s = n % 4096
        r = s // SLOC
        return r * T + b * SLOC + (s % SLOC)

    per_core = []
    NPAD = 384
    tok = top_idx.T
    contrib = []
    for e in range(8):
        for cslot in range(2048):
            n = int(tok[e, cslot])
            contrib.append((e * 2048 + cslot, ag_row(n), cw[n, e]))
    for cc in range(NCORES):
        vecs = np.zeros((128, 32), np.float32)
        vecs[:, 0:32] = b1[cc].reshape(32, 128).T
        gidx = np.zeros((128, 16), np.int32)
        wrow = np.zeros((1, T), np.float32)
        for cslot in range(2048):
            n = int(tok[cc, cslot])
            gidx[cslot % 128, cslot // 128] = ag_row(n)
            wrow[0, cslot] = cw[n, cc]
        bidx = np.zeros((128, 48), np.int32)
        bt = np.zeros((16, 3, 128, 128), np.float32)
        base = cc * T
        for tile in range(16):
            rows = [s_ for (s_, d, w_) in contrib
                    if base + tile * 128 <= d < base + (tile + 1) * 128]
            dests = [d - base - tile * 128 for (s_, d, w_) in contrib
                     if base + tile * 128 <= d < base + (tile + 1) * 128]
            assert len(rows) <= NPAD, f"contribution overflow {len(rows)}"
            for j, (s_, dloc) in enumerate(zip(rows, dests)):
                bidx[j % 128, tile * 3 + j // 128] = s_
                bt[tile, j // 128, j % 128, dloc] = 1.0
            for j in range(len(rows), NPAD):
                bidx[j % 128, tile * 3 + j // 128] = 0
        per_core.append({
            f"{pref}_w1_t": tile_do(W1[cc]),
            f"{pref}_w2_s": tile_slab_do(W2[cc]),
            f"{pref}_vecs": vecs,
            f"{pref}_b2": b2[cc].reshape(1, 1024).astype(np.float32),
            f"{pref}_wrow": wrow,
            f"{pref}_gidx": gidx,
            f"{pref}_bidx": bidx,
            f"{pref}_bt": np.ascontiguousarray(
                bt.transpose(0, 2, 1, 3).reshape(16, 128, 384)),
        })
    return per_core


def scalar11(p, c, buf, off):
    return buf.h[0:1, off:off + 1]


def l0_ffn(p, c, io, st, call_tag):
    PERM = [0, 3, 6, 1, 4, 7, 2, 5]
    m_row, r_row = ln_stats(p, c, c.X, NDT, T)
    Mb, Rb = c.SC[0], c.SC[1]
    bcast_row(p, c, m_row, 0, T, Mb)
    bcast_row(p, c, r_row, 0, T, Rb)

    s2_row, s1_row = c.ROW[0], c.ROW[1]
    tsq = p.emit("act", "activation", s2_row.h[0:1, 0:T], r_row.h[0:1, 0:T],
                 AF.Square, deps=r_row.rdep() + s2_row.wdep())
    s2_row.wrote(tsq)
    r_row.read(tsq)
    for tcI in range(4):
        t0 = tcI * 512
        zc = c.XC[tcI % 2]
        for i in range(NDT):
            ta = p.emit("dve", "tensor_sub", zc.h[:, 0:512],
                        c.X[i].h[:, t0:t0 + 512], Mb.h[:, t0:t0 + 512],
                        deps=c.X[i].rdep() + Mb.rdep() + zc.wdep())
            zc.wrote(ta)
            c.X[i].read(ta)
            Mb.read(ta)
            tb = p.emit("dve", "tensor_mul", zc.h[:, 0:512], zc.h[:, 0:512],
                        Rb.h[:, t0:t0 + 512], deps=[ta] + Rb.rdep())
            zc.wrote(tb)
            Rb.read(tb)
            ab = c.HT[2]
            t1 = p.emit("act", "activation", ab.h[:, 0:512],
                        zc.h[:, 0:512], AF.Abs, deps=[tb] + ab.wdep())
            ab.wrote(t1)
            zc.read(t1)
            ps = c.PS[5]
            mm = p.emit("pe", "matmul", ps.h[0:1, 0:512], c.ones_bf[:],
                        ab.h[:, 0:512], start=(i == 0),
                        stop=(i == NDT - 1),
                        deps=(ps.wdep() + c.setup_tok if i == 0 else [])
                        + ab.rdep(), inc=True)
            ab.read(mm)
            ps.wrote(mm)
        ps = c.PS[5]
        te = p.emit("dve", "tensor_copy", s1_row.h[0:1, t0:t0 + 512],
                    ps.h[0:1, 0:512], deps=ps.rdep() + s1_row.wdep())
        ps.read(te)
        s1_row.wrote(te)
    if "dbg" in io:
        tdbg = p.dma("sp", io["dbg"][0], s2_row.h[0:1, 0:T],
                     deps=s2_row.rdep(), key="dbg0")
        s2_row.read(tdbg)
    cin, cout = io["icc_in"], io["icc_out"]
    td = p.dma("sp", cin.h[0:T].rearrange("(a q) -> a q", a=1),
               s2_row.h[0:1, 0:T], deps=s2_row.rdep() + cin.wdep())
    s2_row.read(td)
    cin.wrote(td)
    tcc = collective(p, "AllGather", ALU.bypass,
                     [cin.h[0:T].rearrange("(a q) -> a q", a=1)],
                     [cout.h[0:NCORES * T].rearrange("(a q) -> a q", a=8)],
                     deps=[td] + cout.wdep())
    cout.wrote(tcc)
    cin.read(tcc)
    S128 = io["s128"]
    tl = p.dma("sp", S128.h[:, 0:128],
               cout.h[0:NCORES * T].rearrange("(a q) -> a q", a=128),
               deps=[tcc] + S128.wdep())
    cout.read(tl)
    S128.wrote(tl)

    sca = io["sca"]
    bisv = io["bis"]
    mask = io["mask128"]
    SQ = io["sq128"]
    tz0 = p.emit("pool", "memset", sca.h[0:1, 0:32], 0.0, deps=sca.wdep())
    sca.wrote(tz0)
    tz1 = p.emit("pool", "memset", bisv.h[:, 0:8], 0.0, deps=bisv.wdep())
    bisv.wrote(tz1)
    tz2 = p.emit("pool", "memset", bisv.h[:, 0:1], 1.0, deps=bisv.wdep())
    bisv.wrote(tz2)
    NT = float(NCORES * T)
    psA, psB, psC = c.PS[4], c.PS[6], c.PS[7]
    m1 = p.emit("pe", "matmul", psA.h[:, 0:1], S128.h[:, 0:128],
                bisv.h[:, 0:1], start=True, stop=True,
                deps=S128.rdep() + bisv.rdep() + psA.wdep() + c.setup_tok,
                inc=True)
    S128.read(m1)
    bisv.read(m1)
    psA.wrote(m1)
    tc1 = p.emit("dve", "tensor_copy", bisv.h[:, 2:3], psA.h[:, 0:1],
                 deps=[m1] + bisv.wdep())
    psA.read(tc1)
    bisv.wrote(tc1)
    m2 = p.emit("pe", "matmul", psB.h[0:1, 0:1], bisv.h[:, 2:3],
                bisv.h[:, 0:1], start=True, stop=True,
                deps=bisv.rdep() + psB.wdep() + c.setup_tok, inc=True)
    bisv.read(m2)
    psB.wrote(m2)
    ts0 = p.emit("dve", "tensor_scalar_mul", sca.h[0:1, 4:5],
                 psB.h[0:1, 0:1], 1.0 / NT, deps=[m2] + sca.wdep())
    psB.read(ts0)
    sca.wrote(ts0)
    m3 = p.emit("pe", "matmul", psC.h[:, 0:1], c.ones_f1[0:1, 0:128],
                sca.h[0:1, 4:5], start=True, stop=True,
                deps=[ts0] + psC.wdep() + c.setup_tok, inc=True)
    sca.read(m3)
    psC.wrote(m3)
    tnm = p.emit("dve", "tensor_scalar_mul", bisv.h[:, 3:4], psC.h[:, 0:1],
                 -1.0, deps=[m3] + bisv.wdep())
    psC.read(tnm)
    bisv.wrote(tnm)
    tq = p.emit("act", "activation", SQ.h[:, 0:128], S128.h[:, 0:128],
                AF.Square, bias=bisv.h[:, 3:4],
                deps=S128.rdep() + SQ.wdep() + [tnm])
    SQ.wrote(tq)
    S128.read(tq)
    bisv.read(tq)
    m4 = p.emit("pe", "matmul", psA.h[:, 0:1], SQ.h[:, 0:128],
                bisv.h[:, 0:1], start=True, stop=True,
                deps=SQ.rdep() + bisv.rdep() + psA.wdep() + c.setup_tok,
                inc=True)
    SQ.read(m4)
    bisv.read(m4)
    psA.wrote(m4)
    tc2 = p.emit("dve", "tensor_copy", bisv.h[:, 2:3], psA.h[:, 0:1],
                 deps=[m4] + bisv.wdep())
    psA.read(tc2)
    bisv.wrote(tc2)
    m5 = p.emit("pe", "matmul", psB.h[0:1, 0:1], bisv.h[:, 2:3],
                bisv.h[:, 0:1], start=True, stop=True,
                deps=bisv.rdep() + psB.wdep() + c.setup_tok, inc=True)
    bisv.read(m5)
    psB.wrote(m5)
    tv0 = p.emit("dve", "tensor_scalar", sca.h[0:1, 2:3], psB.h[0:1, 0:1],
                 1.0 / NT, 1e-40, ALU.mult, ALU.add, deps=[m5] + sca.wdep())
    psB.read(tv0)
    sca.wrote(tv0)
    tv1 = p.emit("act", "activation", sca.h[0:1, 2:3], sca.h[0:1, 2:3],
                 AF.Ln, deps=[tv0])
    sca.wrote(tv1)
    tv2 = p.emit("act", "activation", sca.h[0:1, 2:3], sca.h[0:1, 2:3],
                 AF.Exp, scale=0.5, deps=[tv1])
    sca.wrote(tv2)
    tv3 = p.emit("dve", "tensor_scalar", sca.h[0:1, 6:7], sca.h[0:1, 2:3],
                 1.6e-5, 1.0, ALU.mult, ALU.add, deps=[tv2] + sca.wdep())
    sca.wrote(tv3)
    tmx = p.emit("dve", "reduce_max", bisv.h[:, 4:5], S128.h[:, 0:128],
                 axis=AX.X, deps=S128.rdep() + bisv.wdep())
    S128.read(tmx)
    bisv.wrote(tmx)
    m6 = p.emit("pe", "transpose", psC.h[0:1, 0:128], bisv.h[:, 4:5],
                c.ident_f[:, 0:128], deps=bisv.rdep() + psC.wdep()
                + c.setup_tok)
    bisv.read(m6)
    psC.wrote(m6)
    tc3 = p.emit("dve", "tensor_copy", SQ.h[0:1, 0:128], psC.h[0:1, 0:128],
                 deps=[m6] + SQ.wdep())
    psC.read(tc3)
    SQ.wrote(tc3)
    tr3 = p.emit("dve", "reduce_max", sca.h[0:1, 3:4], SQ.h[0:1, 0:128],
                 axis=AX.X, deps=[tc3] + sca.wdep())
    SQ.read(tr3)
    sca.wrote(tr3)
    th0 = p.emit("dve", "tensor_scalar", sca.h[0:1, 9:10], sca.h[0:1, 3:4],
                 1.001, 1e-30, ALU.mult, ALU.add, deps=sca.wdep())
    sca.wrote(th0)
    TARGET = 1639.0
    for it in range(40):
        tm0 = p.emit("dve", "tensor_add", sca.h[0:1, 10:11],
                     sca.h[0:1, 8:9], sca.h[0:1, 9:10], deps=sca.wdep())
        sca.wrote(tm0)
        tm1 = p.emit("dve", "tensor_scalar_mul", sca.h[0:1, 10:11],
                     sca.h[0:1, 10:11], 0.5, deps=[tm0])
        sca.wrote(tm1)
        mb = p.emit("pe", "matmul", psC.h[:, 0:1], c.ones_f1[0:1, 0:128],
                    sca.h[0:1, 10:11], start=True, stop=True,
                    deps=[tm1] + psC.wdep() + c.setup_tok, inc=True)
        sca.read(mb)
        psC.wrote(mb)
        tb0 = p.emit("dve", "tensor_copy", bisv.h[:, 1:2], psC.h[:, 0:1],
                     deps=[mb] + bisv.wdep())
        psC.read(tb0)
        bisv.wrote(tb0)
        tb1 = p.emit("dve", "tensor_single_scalar", mask.h[:, 0:128],
                     S128.h[:, 0:128], bisv.h[:, 1:2], ALU.is_lt,
                     deps=[tb0] + S128.rdep() + mask.wdep())
        mask.wrote(tb1)
        S128.read(tb1)
        bisv.read(tb1)
        mc = p.emit("pe", "matmul", psA.h[:, 0:1], mask.h[:, 0:128],
                    c.ones_bf[:], start=True, stop=True,
                    deps=mask.rdep() + psA.wdep() + c.setup_tok, inc=True)
        mask.read(mc)
        psA.wrote(mc)
        tb2 = p.emit("dve", "tensor_copy", bisv.h[:, 2:3], psA.h[:, 0:1],
                     deps=[mc] + bisv.wdep())
        psA.read(tb2)
        bisv.wrote(tb2)
        md = p.emit("pe", "matmul", psB.h[0:1, 0:1], bisv.h[:, 2:3],
                    bisv.h[:, 0:1], start=True, stop=True,
                    deps=bisv.rdep() + psB.wdep() + c.setup_tok, inc=True)
        bisv.read(md)
        psB.wrote(md)
        tb3 = p.emit("dve", "tensor_scalar", sca.h[0:1, 11:12],
                     psB.h[0:1, 0:1], TARGET, None, ALU.is_gt,
                     deps=[md] + sca.wdep())
        psB.read(tb3)
        sca.wrote(tb3)
        tu0 = p.emit("dve", "tensor_sub", sca.h[0:1, 12:13],
                     sca.h[0:1, 10:11], sca.h[0:1, 9:10], deps=sca.wdep())
        sca.wrote(tu0)
        tu1 = p.emit("dve", "tensor_mul", sca.h[0:1, 12:13],
                     sca.h[0:1, 12:13], sca.h[0:1, 11:12], deps=[tu0])
        sca.wrote(tu1)
        tu2 = p.emit("dve", "tensor_add", sca.h[0:1, 9:10],
                     sca.h[0:1, 9:10], sca.h[0:1, 12:13], deps=[tu1])
        sca.wrote(tu2)
        tu3 = p.emit("dve", "tensor_sub", sca.h[0:1, 13:14],
                     sca.h[0:1, 8:9], sca.h[0:1, 10:11], deps=sca.wdep())
        sca.wrote(tu3)
        tu4 = p.emit("dve", "tensor_mul", sca.h[0:1, 13:14],
                     sca.h[0:1, 13:14], sca.h[0:1, 11:12], deps=[tu3])
        sca.wrote(tu4)
        tu5 = p.emit("dve", "tensor_add", sca.h[0:1, 8:9],
                     sca.h[0:1, 10:11], sca.h[0:1, 13:14], deps=[tu4])
        sca.wrote(tu5)
    g_row = c.ROW[2]
    tg1 = p.emit("dve", "tensor_single_scalar", g_row.h[0:1, 0:T],
                 s2_row.h[0:1, 0:T], sca.h[0:1, 8:9], ALU.is_lt,
                 deps=s2_row.rdep() + sca.rdep() + g_row.wdep())
    g_row.wrote(tg1)
    s2_row.read(tg1)
    sca.read(tg1)
    tg2 = p.emit("dve", "tensor_mul", s1_row.h[0:1, 0:T],
                 s1_row.h[0:1, 0:T], g_row.h[0:1, 0:T],
                 deps=[tg1] + s1_row.wdep())
    s1_row.wrote(tg2)
    g_row.read(tg2)
    tg3 = p.emit("dve", "tensor_scalar_add", s1_row.h[0:1, 0:T],
                 s1_row.h[0:1, 0:T], 1e-8, deps=[tg2])
    s1_row.wrote(tg3)
    tg4 = p.emit("dve", "tensor_single_scalar", s1_row.h[0:1, 0:T],
                 s1_row.h[0:1, 0:T], sca.h[0:1, 6:7], ALU.mult,
                 deps=[tg3] + sca.rdep())
    s1_row.wrote(tg4)
    sca.read(tg4)
    tg5 = p.emit("dve", "reciprocal", s1_row.h[0:1, 0:T],
                 s1_row.h[0:1, 0:T], deps=[tg4])
    s1_row.wrote(tg5)
    tg6 = p.emit("dve", "tensor_mul", g_row.h[0:1, 0:T], g_row.h[0:1, 0:T],
                 s1_row.h[0:1, 0:T], deps=[tg5] + g_row.wdep())
    g_row.wrote(tg6)
    s1_row.read(tg6)

    Gb = c.SC[1]
    AFF = io["aff"]
    for tcI in range(4):
        t0 = tcI * 512
        gb = c.SH[0]
        bcast_chunk(p, c, g_row, t0, 512, gb)
        for i in range(NDT):
            zc = c.XC[i % 2 + 2]
            ta = p.emit("dve", "tensor_sub", zc.h[:, 0:512],
                        c.X[i].h[:, t0:t0 + 512], Mb.h[:, t0:t0 + 512],
                        deps=c.X[i].rdep() + Mb.rdep() + zc.wdep())
            zc.wrote(ta)
            c.X[i].read(ta)
            Mb.read(ta)
            tb2 = p.emit("dve", "tensor_mul", zc.h[:, 0:512],
                         zc.h[:, 0:512], Rb.h[:, t0:t0 + 512],
                         deps=[ta] + Rb.rdep())
            zc.wrote(tb2)
            Rb.read(tb2)
            tc_ = p.emit("dve", "tensor_mul", c.XN[i].h[:, t0:t0 + 512],
                         zc.h[:, 0:512], gb.h[:, 0:512],
                         deps=[tb2] + gb.rdep() + c.XN[i].wdep())
            c.XN[i].wrote(tc_)
            zc.read(tc_)
            gb.read(tc_)
            tf = p.emit("dve", "tensor_reduce", AFF.h[:, i * 4 + tcI:
                                                     i * 4 + tcI + 1],
                        c.XN[i].h[:, t0:t0 + 512], axis=AX.X, op=ALU.add,
                        deps=[tc_] + AFF.wdep())
            AFF.wrote(tf)
            c.XN[i].read(tf)
    cin2, cout2 = io["acc2_in"], io["acc2_out"]
    ta1 = p.dma("sp", cin2.h[0:4096].rearrange("(q n) -> q n", q=128),
                AFF.h[:, 0:32], deps=AFF.rdep() + cin2.wdep())
    AFF.read(ta1)
    cin2.wrote(ta1)
    tcc2 = collective(p, "AllReduce", ALU.add,
                      [cin2.h[0:4096].rearrange("(a q) -> a q", a=64)],
                      [cout2.h[0:4096].rearrange("(a q) -> a q", a=64)],
                      deps=[ta1] + cout2.wdep(),
                      sum_scratch=c.HT[2], sum_psum=c.PS[7],
                      flat_out=cout2.h[0:4096])
    cout2.wrote(tcc2)
    cin2.read(tcc2)
    ta2 = p.dma("sp", AFF.h[:, 0:32],
                cout2.h[0:4096].rearrange("(q n) -> q n", q=128),
                deps=[tcc2] + AFF.wdep())
    cout2.read(ta2)
    AFF.wrote(ta2)
    awt = io["aff_w"]
    psa = c.PS[4]
    mm = None
    afb = c.SH[0]
    tcv = p.emit("dve", "tensor_scalar_mul",
                 afb.h[:, 0:16].bitcast(BF16)[:, 0:32], AFF.h[:, 0:32],
                 1.0 / 4096.0, deps=AFF.rdep() + afb.wdep())
    afb.wrote(tcv)
    AFF.read(tcv)
    for dt in range(NDT):
        mm = p.emit("pe", "matmul", psa.h[0:8, 0:4],
                    awt.h[:, dt * 8:(dt + 1) * 8],
                    afb.h[:, 0:16].bitcast(BF16)[:, dt * 4:dt * 4 + 4],
                    start=(dt == 0), stop=(dt == NDT - 1),
                    deps=(psa.wdep() if dt == 0 else []) + awt.rdep()
                    + [tcv], inc=True)
        awt.read(mm)
        afb.read(mm)
    psa.wrote(mm)
    AFFE = io["affe"]
    vx = io["vecs"]
    tae = p.emit("dve", "tensor_scalar", AFFE.h[0:8, 0:4], psa.h[0:8, 0:4],
                 vx.h[0:8, io["vo_affb"]:io["vo_affb"] + 1], 0.3,
                 ALU.add, ALU.mult, deps=[mm] + vx.rdep() + AFFE.wdep())
    AFFE.wrote(tae)
    psa.read(tae)
    vx.read(tae)
    eu = io["eu"]
    t1 = p.emit("act", "activation", eu.h[0:8, 1:2], eu.h[0:8, 0:1],
                AF.Ln, bias=c.c128t(1e-8)[0][0:8, 0:1],
                deps=eu.rdep() + eu.wdep() + [c.c128t(1e-8)[1]])
    eu.wrote(t1)
    t2 = p.emit("dve", "tensor_mul", eu.h[0:8, 1:2], eu.h[0:8, 1:2],
                eu.h[0:8, 0:1], deps=[t1])
    eu.wrote(t2)
    psu = c.PS[5]
    tt6 = p.emit("pe", "transpose", psu.h[0:1, 0:8], eu.h[0:8, 1:2],
                 c.ident_f[0:8, 0:8], deps=[t2] + psu.wdep() + c.setup_tok)
    eu.read(tt6)
    psu.wrote(tt6)
    te6 = p.emit("dve", "tensor_copy", sca.h[0:1, 24:32], psu.h[0:1, 0:8],
                 deps=[tt6] + sca.wdep())
    psu.read(te6)
    sca.wrote(te6)
    tr7 = p.emit("dve", "tensor_reduce", sca.h[0:1, 7:8],
                 sca.h[0:1, 24:32], axis=AX.X, op=ALU.add, deps=[te6])
    sca.wrote(tr7)
    t3 = p.emit("dve", "tensor_scalar", sca.h[0:1, 7:8], sca.h[0:1, 7:8],
                2.0, 0.01, ALU.add, ALU.mult, deps=[tr7])
    sca.wrote(t3)
    t4 = p.emit("dve", "tensor_scalar_max", sca.h[0:1, 7:8],
                sca.h[0:1, 7:8], 1e-4, deps=[t3])
    sca.wrote(t4)
    return (g_row, sca, AFFE)


def l0_ffn_gates_experts(p, c, io, st, parts):
    g_row, sca, AFFE = parts
    vx = io["vecs"]

    def g1_evac(ht, tcI, t0, w, ps):
        hb = c.HT[2]
        ta = p.emit("act", "activation", hb.h[:, t0:t0 + w], ps.h[:, 0:w],
                    AF.Silu, bias=vx.h[:, io["vo_gb1"] + ht:
                                       io["vo_gb1"] + ht + 1],
                    deps=ps.rdep() + vx.rdep() + hb.wdep())
        hb.wrote(ta)
        ps.read(ta)
        vx.read(ta)
        if t0 + w == T:
            ts = p.dma("sp", io["hd"][ht], hb.h[:, 0:T], deps=hb.rdep(),
                       key="hspill0")
            hb.read(ts)
            io["hd_buf"].wrote(ts)

    gemm_ht(p, c, io["gw1_t"], NDT, NDT, c.XN, T, g1_evac)

    hb_buf = io["hd_buf"]
    m1r, r1r = c.ROW[0], c.ROW[1]
    for i in range(NDT):
        ab = c.HT[i % 2]
        tk = p.dma("pool", ab.h[:, 0:T], io["hd"][i],
                   deps=ab.wdep() + hb_buf.rdep())
        ab.wrote(tk)
        hb_buf.read(tk)
        sq = c.HT[2]
        tq = p.emit("act", "activation", sq.h[:, 0:T], ab.h[:, 0:T],
                    AF.Square, deps=[tk] + sq.wdep())
        sq.wrote(tq)
        ab.read(tq)
        for tcI in range(4):
            t0 = tcI * 512
            for stat in range(2):
                ps = c.PS[tcI * 2 + stat]
                rb = (ab, sq)[stat]
                mm = p.emit("pe", "matmul", ps.h[0:1, 0:512], c.ones_bf[:],
                            rb.h[:, t0:t0 + 512], start=(i == 0),
                            stop=(i == NDT - 1),
                            deps=(ps.wdep() + c.setup_tok if i == 0 else [])
                            + rb.rdep(), inc=True)
                rb.read(mm)
                ps.wrote(mm)
    for tcI in range(4):
        t0 = tcI * 512
        for stat in range(2):
            ps = c.PS[tcI * 2 + stat]
            row = (m1r, r1r)[stat]
            te = p.emit("dve", "tensor_copy", row.h[0:1, t0:t0 + 512],
                        ps.h[0:1, 0:512], deps=ps.rdep() + row.wdep())
            ps.read(te)
            row.wrote(te)
    tm = p.emit("dve", "tensor_scalar_mul", m1r.h[0:1, 0:T],
                m1r.h[0:1, 0:T], 1.0 / D, deps=m1r.wdep())
    m1r.wrote(tm)
    t2 = p.emit("dve", "tensor_scalar_mul", r1r.h[0:1, 0:T],
                r1r.h[0:1, 0:T], 1.0 / D, deps=r1r.wdep())
    r1r.wrote(t2)
    mm2r = c.ROW[2]
    t3 = p.emit("dve", "tensor_mul", mm2r.h[0:1, 0:T], m1r.h[0:1, 0:T],
                m1r.h[0:1, 0:T], deps=m1r.rdep() + mm2r.wdep())
    mm2r.wrote(t3)
    m1r.read(t3)
    t4 = p.emit("dve", "tensor_sub", r1r.h[0:1, 0:T], r1r.h[0:1, 0:T],
                mm2r.h[0:1, 0:T], deps=[t3] + r1r.wdep())
    r1r.wrote(t4)
    mm2r.read(t4)
    eps_ap, eps_tok = c.c11t(1e-5)
    t5 = p.emit("act", "activation", r1r.h[0:1, 0:T], r1r.h[0:1, 0:T],
                AF.Ln, bias=eps_ap, deps=[t4, eps_tok])
    r1r.wrote(t5)
    t6 = p.emit("act", "activation", r1r.h[0:1, 0:T], r1r.h[0:1, 0:T],
                AF.Exp, scale=-0.5, deps=[t5])
    r1r.wrote(t6)
    Mb1, Rb1 = c.SC[0], c.SC[1]
    bcast_row(p, c, m1r, 0, T, Mb1)
    bcast_row(p, c, r1r, 0, T, Rb1)
    for i in range(NDT):
        for tcI in range(4):
            t0 = tcI * 512
            tmp = c.XC[tcI % 4]
            tk = p.dma("pool", tmp.h[:, 0:512], io["hd"][i][:, t0:t0 + 512],
                       deps=tmp.wdep() + hb_buf.rdep())
            tmp.wrote(tk)
            hb_buf.read(tk)
            ta = p.emit("dve", "tensor_sub", tmp.h[:, 0:512],
                        tmp.h[:, 0:512], Mb1.h[:, t0:t0 + 512],
                        deps=[tk] + Mb1.rdep())
            tmp.wrote(ta)
            Mb1.read(ta)
            tb = p.emit("dve", "tensor_mul", tmp.h[:, 0:512],
                        tmp.h[:, 0:512], Rb1.h[:, t0:t0 + 512],
                        deps=[ta] + Rb1.rdep())
            tmp.wrote(tb)
            Rb1.read(tb)
            hq = c.HT[2]
            tc_ = p.emit("dve", "tensor_scalar", hq.h[:, t0:t0 + 512],
                         tmp.h[:, 0:512],
                         vx.h[:, io["vo_lns"] + i:io["vo_lns"] + i + 1],
                         vx.h[:, io["vo_lno"] + i:io["vo_lno"] + i + 1],
                         ALU.mult, ALU.add,
                         deps=[tb] + vx.rdep() + hq.wdep())
            hq.wrote(tc_)
            tmp.read(tc_)
            vx.read(tc_)
        ts = p.dma("sp", io["hd"][8 + i], c.HT[2].h[:, 0:T],
                   deps=c.HT[2].rdep(), key="hspill0")
        c.HT[2].read(ts)
        hb_buf.wrote(ts)

    def g2_flush(ot, t0, tw, ps_pair):
        for h in range(ceil_div(tw, 512)):
            a = t0 + h * 512
            w = min(512, tw - h * 512)
            ps = ps_pair[h]
            sh = c.SH[h]
            ta = p.emit("act", "activation",
                        sh.h[:, 0:256].bitcast(BF16)[:, 0:512][:, 0:w],
                        ps.h[:, 0:w], AF.Silu,
                        bias=vx.h[:, io["vo_gb2"] + ot:io["vo_gb2"] + ot + 1],
                        deps=ps.rdep() + vx.rdep() + sh.wdep())
            sh.wrote(ta)
            ps.read(ta)
            vx.read(ta)
            ts = p.dma("sp", io["hd"][16 + ot][:, a:a + w],
                       sh.h[:, 0:256].bitcast(BF16)[:, 0:512][:, 0:w],
                       deps=[ta], key="hspill0" + str(h))
            sh.read(ts)
            hb_buf.wrote(ts)

    gemm_consume(p, c, _apslice(io["hd"], 8, 8), 8, io["gw2_s"], 4, T,
                 g2_flush, h_buf=hb_buf)

    LOG = io["big8"]
    w3t = io["w3_t"]
    nsb = io["noise"]
    sc8b = io["sc8"]
    psn = c.PS[6]
    mbn = p.emit("pe", "matmul", psn.h[0:8, 0:1], c.ones_f1[0:1, 0:8],
                 sca.h[0:1, 7:8], start=True, stop=True,
                 deps=sca.rdep() + psn.wdep() + c.setup_tok, inc=True)
    sca.read(mbn)
    psn.wrote(mbn)
    tns = p.emit("dve", "tensor_copy", sc8b.h[0:8, 13:14], psn.h[0:8, 0:1],
                 deps=[mbn] + sc8b.wdep())
    psn.read(tns)
    sc8b.wrote(tns)
    for tcI in range(4):
        t0 = tcI * 512
        psl = c.PS[4 + tcI % 2]
        mm = None
        for dt in range(4):
            hq = c.HT[dt % 2]
            tk = p.dma("pool", hq.h[:, 0:512],
                       io["hd"][16 + dt][:, t0:t0 + 512],
                       deps=hq.wdep() + hb_buf.rdep())
            hq.wrote(tk)
            hb_buf.read(tk)
            mm = p.emit("pe", "matmul", psl.h[0:8, 0:512],
                        w3t.h[:, dt * 8:(dt + 1) * 8], hq.h[:, 0:512],
                        start=(dt == 0), stop=(dt == 3),
                        deps=(psl.wdep() if dt == 0 else []) + w3t.rdep()
                        + [tk], inc=True)
            w3t.read(mm)
            hq.read(mm)
        psl.wrote(mm)
        nz = c.SH[0]
        tn = p.dma("sp", nz.h[0:8, 0:512], nsb[0:8, t0:t0 + 512],
                   deps=nz.wdep(), key="noiseld")
        nz.wrote(tn)
        b_ = tcI
        tl1 = p.emit("dve", "tensor_scalar", LOG.h[0:8, t0:t0 + 512],
                     psl.h[0:8, 0:512],
                     vx.h[0:8, io["vo_gb3"]:io["vo_gb3"] + 1],
                     None, ALU.add, deps=[mm] + vx.rdep() + LOG.wdep())
        LOG.wrote(tl1)
        psl.read(tl1)
        vx.read(tl1)
        tl2 = p.emit("dve", "tensor_scalar", LOG.h[0:8, t0:t0 + 512],
                     LOG.h[0:8, t0:t0 + 512],
                     AFFE.h[0:8, b_:b_ + 1], None, ALU.add, deps=[tl1]
                     + AFFE.rdep())
        LOG.wrote(tl2)
        AFFE.read(tl2)
        tl3 = p.emit("dve", "tensor_scalar", nz.h[0:8, 0:512],
                     nz.h[0:8, 0:512], sc8b.h[0:8, 13:14], None, ALU.mult,
                     deps=[tn, tns] + sc8b.rdep())
        nz.wrote(tl3)
        sc8b.read(tl3)
        tl4 = p.emit("dve", "tensor_add", LOG.h[0:8, t0:t0 + 512],
                     LOG.h[0:8, t0:t0 + 512], nz.h[0:8, 0:512],
                     deps=[tl2, tl3])
        LOG.wrote(tl4)
        nz.read(tl4)
        tl5 = p.emit("dve", "tensor_scalar", LOG.h[0:8, t0:t0 + 512],
                     LOG.h[0:8, t0:t0 + 512], 1.0 / 1.5, None, ALU.mult,
                     deps=[tl4])
        LOG.wrote(tl5)

    for tb in range(16):
        pst = c.PS[4 + tb % 2]
        tt = p.emit("pe", "transpose", pst.h[0:128, 0:8],
                    LOG.h[0:8, tb * 128:(tb + 1) * 128],
                    c.ident_f[0:8, 0:8],
                    deps=LOG.rdep() + pst.wdep() + c.setup_tok)
        LOG.read(tt)
        pst.wrote(tt)
        lt = c.SH[0]
        te = p.emit("dve", "tensor_copy", lt.h[:, 0:8], pst.h[:, 0:8],
                    deps=[tt] + lt.wdep())
        pst.read(te)
        lt.wrote(te)
        t1 = p.emit("dve", "tensor_reduce", lt.h[:, 8:9], lt.h[:, 0:8],
                    axis=AX.X, op=ALU.max, deps=[te])
        lt.wrote(t1)
        t2 = p.emit("dve", "tensor_scalar", lt.h[:, 9:17], lt.h[:, 0:8],
                    lt.h[:, 8:9], None, ALU.is_ge, deps=[t1])
        lt.wrote(t2)
        t3 = p.emit("dve", "tensor_reduce", lt.h[:, 17:18], lt.h[:, 9:17],
                    axis=AX.X, op=ALU.add, deps=[t2])
        lt.wrote(t3)
        t4 = p.emit("dve", "tensor_scalar", lt.h[:, 18:26], lt.h[:, 9:17],
                    1e30, None, ALU.mult, deps=[t2])
        lt.wrote(t4)
        t5 = p.emit("dve", "tensor_sub", lt.h[:, 18:26], lt.h[:, 0:8],
                    lt.h[:, 18:26], deps=[t4])
        lt.wrote(t5)
        t6 = p.emit("dve", "tensor_reduce", lt.h[:, 26:27], lt.h[:, 18:26],
                    axis=AX.X, op=ALU.max, deps=[t5])
        lt.wrote(t6)
        t7 = p.emit("dve", "tensor_scalar", lt.h[:, 27:28], lt.h[:, 17:18],
                    1.5, None, ALU.is_gt, deps=[t3])
        lt.wrote(t7)
        t8 = p.emit("dve", "tensor_sub", lt.h[:, 28:29], lt.h[:, 8:9],
                    lt.h[:, 26:27], deps=[t6, t1])
        lt.wrote(t8)
        t9 = p.emit("dve", "tensor_mul", lt.h[:, 28:29], lt.h[:, 28:29],
                    lt.h[:, 27:28], deps=[t8, t7])
        lt.wrote(t9)
        ta = p.emit("dve", "tensor_add", lt.h[:, 26:27], lt.h[:, 26:27],
                    lt.h[:, 28:29], deps=[t9])
        lt.wrote(ta)
        tb1 = p.emit("dve", "tensor_scalar", lt.h[:, 9:17], lt.h[:, 0:8],
                     lt.h[:, 26:27], None, ALU.is_ge, deps=[ta])
        lt.wrote(tb1)
        tb2 = p.emit("dve", "tensor_scalar_mul", lt.h[:, 18:19],
                     lt.h[:, 8:9], -1.0, deps=lt.wdep())
        lt.wrote(tb2)
        tb3 = p.emit("act", "activation", lt.h[:, 0:8], lt.h[:, 0:8],
                     AF.Exp, bias=lt.h[:, 18:19], deps=[tb2])
        lt.wrote(tb3)
        tb4 = p.emit("dve", "tensor_mul", lt.h[:, 0:8], lt.h[:, 0:8],
                     lt.h[:, 9:17], deps=[tb3, tb1])
        lt.wrote(tb4)
        tb5 = p.emit("dve", "tensor_reduce", lt.h[:, 8:9], lt.h[:, 0:8],
                     axis=AX.X, op=ALU.add, deps=[tb4])
        lt.wrote(tb5)
        tb6 = p.emit("dve", "reciprocal", lt.h[:, 8:9], lt.h[:, 8:9],
                     deps=[tb5])
        lt.wrote(tb6)
        tb7 = p.emit("dve", "tensor_scalar_mul", lt.h[:, 0:8], lt.h[:, 0:8],
                     lt.h[:, 8:9], deps=[tb6])
        lt.wrote(tb7)
        pst2 = c.PS[6 + tb % 2]
        tt2 = p.emit("pe", "transpose", pst2.h[0:8, 0:128], lt.h[:, 0:8],
                     c.ident_f[:], deps=[tb7] + pst2.wdep() + c.setup_tok)
        lt.read(tt2)
        pst2.wrote(tt2)
        te2 = p.emit("dve", "tensor_copy",
                     LOG.h[0:8, 0:1024].bitcast(BF16)
                     [0:8, tb * 128:(tb + 1) * 128],
                     pst2.h[0:8, 0:128], deps=[tt2] + LOG.wdep())
        pst2.read(te2)
        LOG.wrote(te2)

    for i in range(NDT):
        for tcI in range(4):
            t0 = tcI * 512
            tx = p.emit("dve", "scalar_tensor_tensor",
                        c.X[i].h[:, t0:t0 + 512],
                        c.XN[i].h[:, t0:t0 + 512], 0.05,
                        c.X[i].h[:, t0:t0 + 512], ALU.mult, ALU.add,
                        deps=c.XN[i].rdep() + c.X[i].wdep())
            c.X[i].wrote(tx)
            c.XN[i].read(tx)

    PERM = [0, 3, 6, 1, 4, 7, 2, 5]
    ESEL = io["esel"]

    PB = c.SC[0]

    def expert_combine_flush(block):
        be_row = io["berow"]

        def bias_cb(ot, t0, w, ps):
            mm = p.emit("pe", "matmul", ps.h[:, 0:w],
                        be_row.h[0:1, ot * 128:ot * 128 + 128],
                        c.ones_row_bf[0:1, 0:w], start=False, stop=True,
                        deps=be_row.rdep() + c.setup_tok, inc=True)
            be_row.read(mm)
            return mm

        def flush_cb(ot, t0, tw, ps_pair):
            for h in range(ceil_div(tw, 512)):
                a = t0 + h * 512
                w = min(512, tw - h * 512)
                ps = ps_pair[h]
                tmp = c.SH[1]
                tf = p.emit("dve", "tensor_mul", tmp.h[:, 0:w],
                            ps.h[:, 0:w], PB.h[:, a:a + w],
                            deps=ps.rdep() + PB.rdep() + tmp.wdep())
                tmp.wrote(tf)
                ps.read(tf)
                PB.read(tf)
                tx = p.emit("pool", "tensor_tensor",
                            c.X[ot].h[:, a:a + w], c.X[ot].h[:, a:a + w],
                            tmp.h[:, 0:w], ALU.add,
                            deps=[tf] + c.X[ot].wdep())
                c.X[ot].wrote(tx)
                tmp.read(tx)
        return flush_cb, bias_cb

    def make_pb(block):
        g = PERM[block]
        tz = p.emit("pool", "memset", ESEL.h[0:8, 0:128], 1.0,
                    deps=ESEL.wdep())
        ESEL.wrote(tz)
        ts_ = p.emit("pool", "affine_select", ESEL.h[0:8, 0:128],
                     ESEL.h[0:8, 0:128], pattern=[[0, 128]],
                     compare_op=ALU.is_equal, fill=0.0, base=-g,
                     channel_multiplier=1, deps=[tz])
        ESEL.wrote(ts_)
        for tcI in range(4):
            t0 = tcI * 512
            psb = c.PS[4 + tcI % 2]
            mmb = p.emit("pe", "matmul", psb.h[:, 0:512],
                         ESEL.h[0:8, 0:128],
                         LOG.h[0:8, 0:1024].bitcast(BF16)
                         [0:8, t0:t0 + 512], start=True, stop=True,
                         deps=psb.wdep() + ESEL.rdep() + LOG.rdep(),
                         inc=True)
            ESEL.read(mmb)
            LOG.read(mmb)
            psb.wrote(mmb)
            te = p.emit("dve", "tensor_scalar_mul", PB.h[:, t0:t0 + 512],
                        psb.h[:, 0:512], 0.5, deps=[mmb] + PB.wdep())
            psb.read(te)
            PB.wrote(te)

    def eA_evac_silu(bias_off):
        def f(ht, tcI, t0, w, ps):
            hb = c.HT[2]
            ta = p.emit("act", "activation", hb.h[:, t0:t0 + w],
                        ps.h[:, 0:w], AF.Silu,
                        bias=vx.h[:, bias_off + ht:bias_off + ht + 1],
                        deps=ps.rdep() + vx.rdep() + hb.wdep())
            hb.wrote(ta)
            ps.read(ta)
            vx.read(ta)
            if t0 + w == T:
                ts = p.dma("sp", io["hd"][ht], hb.h[:, 0:T],
                           deps=hb.rdep(), key="hspill0")
                hb.read(ts)
                hb_buf.wrote(ts)
        return f

    def eA_evac_gelu(bias_off):
        def f(ht, tcI, t0, w, ps):
            hb = c.HT[2]
            ta = p.emit("act", "activation", hb.h[:, t0:t0 + w],
                        ps.h[:, 0:w], AF.Gelu,
                        bias=vx.h[:, bias_off + ht:bias_off + ht + 1],
                        deps=ps.rdep() + vx.rdep() + hb.wdep())
            hb.wrote(ta)
            ps.read(ta)
            vx.read(ta)
            if t0 + w == T:
                ts = p.dma("sp", io["hd"][ht], hb.h[:, 0:T],
                           deps=hb.rdep(), key="hspill0")
                hb.read(ts)
                hb_buf.wrote(ts)
        return f

    def eA_evac_silusig(bias_off):
        def f(ht, tcI, t0, w, ps):
            s1 = c.SH[0]
            ta = p.emit("act", "activation", s1.h[:, 0:w], ps.h[:, 0:w],
                        AF.Silu, bias=vx.h[:, bias_off + ht:
                                           bias_off + ht + 1],
                        deps=ps.rdep() + vx.rdep() + s1.wdep())
            s1.wrote(ta)
            ps.read(ta)
            vx.read(ta)
            s2 = c.SH[1]
            tb = p.emit("act", "activation", s2.h[:, 0:w], ps.h[:, 0:w],
                        AF.Sigmoid, bias=vx.h[:, bias_off + ht:
                                              bias_off + ht + 1],
                        deps=ps.rdep() + vx.rdep() + s2.wdep())
            s2.wrote(tb)
            ps.read(tb)
            vx.read(tb)
            hb = c.HT[2]
            tc_ = p.emit("dve", "tensor_mul", hb.h[:, t0:t0 + w],
                         s1.h[:, 0:w], s2.h[:, 0:w],
                         deps=[ta, tb] + hb.wdep())
            hb.wrote(tc_)
            s1.read(tc_)
            s2.read(tc_)
            if t0 + w == T:
                ts = p.dma("sp", io["hd"][ht], hb.h[:, 0:T],
                           deps=hb.rdep(), key="hspill0")
                hb.read(ts)
                hb_buf.wrote(ts)
        return f

    for k in range(3):
        make_pb(k)
        gemm_ht(p, c, _apslice(io["e0w1"], 32 * k, 32), NDT, 32, c.XN, T,
                eA_evac_gelu(io["vo_e0b1"] + 32 * k))
        fl, bc_ = expert_combine_flush(k)
        load_berow(p, c, io, k)
        gemm_consume(p, c, io["hd"], 32, _apslice(io["e0w2"], 2 * k, 2), 8,
                     T, fl, bias_cb=bc_, h_buf=hb_buf)
    for k in range(3):
        blk = 3 + k
        make_pb(blk)
        gemm_ht(p, c, _apslice(io["e1w1"], 16 * k, 16), NDT, 16, c.XN, T,
                eA_evac_silu(io["vo_e1b1"] + 16 * k))

        def mid_flush(ot, t0, tw, ps_pair, k=k):
            for h in range(ceil_div(tw, 512)):
                a = t0 + h * 512
                w = min(512, tw - h * 512)
                ps = ps_pair[h]
                sh = c.SH[h]
                ta = p.emit("act", "activation",
                            sh.h[:, 0:256].bitcast(BF16)[:, 0:512][:, 0:w],
                            ps.h[:, 0:w], AF.Silu,
                            bias=vx.h[:, io["vo_e1b2"] + 16 * k + ot:
                                      io["vo_e1b2"] + 16 * k + ot + 1],
                            deps=ps.rdep() + vx.rdep() + sh.wdep())
                sh.wrote(ta)
                ps.read(ta)
                vx.read(ta)
                ts = p.dma("sp", io["hd"][16 + ot][:, a:a + w],
                           sh.h[:, 0:256].bitcast(BF16)[:, 0:512][:, 0:w],
                           deps=[ta], key="hspill0" + str(h))
                sh.read(ts)
                hb_buf.wrote(ts)

        gemm_consume(p, c, io["hd"], 16, _apslice(io["e1w2"], 4 * k, 4), 16,
                     T, mid_flush, h_buf=hb_buf)
        fl, bc_ = expert_combine_flush(blk)
        load_berow(p, c, io, blk)
        gemm_consume(p, c, _apslice(io["hd"], 16, 16), 16,
                     _apslice(io["e1w3"], 2 * k, 2), 8, T, fl, bias_cb=bc_,
                     h_buf=hb_buf)
    for k in range(2):
        blk = 6 + k
        make_pb(blk)
        gemm_ht(p, c, _apslice(io["e2w1"], 24 * k, 24), NDT, 24, c.XN, T,
                eA_evac_silusig(io["vo_e2b1"] + 24 * k))
        fl, bc_ = expert_combine_flush(blk)
        load_berow(p, c, io, blk)
        gemm_consume(p, c, io["hd"], 24, _apslice(io["e2w2"], 2 * k, 2), 8,
                     T, fl, bias_cb=bc_, h_buf=hb_buf)


def _apslice(ap, start, n):
    return ap[start:start + n]


def load_berow(p, c, io, block):
    be = io["berow"]
    tk = p.dma("pool", be.h[0:1, 0:1024],
               io["berows_in"][block:block + 1, :], deps=be.wdep(),
               key=be.h.name)
    be.wrote(tk)


def declare_l0_ffn_io(nc, p, c, tag, pref="L0f"):
    io = {}
    q = pref
    io["gw1_t"] = nc.dram_tensor(f"{q}_gw1_t", [8, 128, 1024], F32,
                                 kind="ExternalInput").ap()
    io["gw2_s"] = nc.dram_tensor(f"{q}_gw2_s", [1, 1, 128, 4096], F32,
                                 kind="ExternalInput").ap()
    io["w3_in"] = nc.dram_tensor(f"{q}_w3", [128, 32], F32,
                                 kind="ExternalInput").ap()
    io["affw_in"] = nc.dram_tensor(f"{q}_affw", [128, 64], F32,
                                   kind="ExternalInput").ap()
    io["e0w1"] = nc.dram_tensor(f"{q}_e0w1", [96, 128, 1024], F32,
                                kind="ExternalInput").ap()
    io["e0w2"] = nc.dram_tensor(f"{q}_e0w2", [6, 4, 128, 4096], F32,
                                kind="ExternalInput").ap()
    io["e1w1"] = nc.dram_tensor(f"{q}_e1w1", [48, 128, 1024], F32,
                                kind="ExternalInput").ap()
    io["e1w2"] = nc.dram_tensor(f"{q}_e1w2", [12, 2, 128, 4096], F32,
                                kind="ExternalInput").ap()
    io["e1w3"] = nc.dram_tensor(f"{q}_e1w3", [6, 2, 128, 4096], F32,
                                kind="ExternalInput").ap()
    io["e2w1"] = nc.dram_tensor(f"{q}_e2w1", [48, 128, 1024], F32,
                                kind="ExternalInput").ap()
    io["e2w2"] = nc.dram_tensor(f"{q}_e2w2", [4, 3, 128, 4096], F32,
                                kind="ExternalInput").ap()
    io["vecs_in"] = nc.dram_tensor(f"{q}_vecs", [128, 270], F32,
                                   kind="ExternalInput").ap()
    io["berows_in"] = nc.dram_tensor(f"{q}_berows", [8, 1024], F32,
                                     kind="ExternalInput").ap()
    io["noise"] = nc.dram_tensor(f"{q}_noise", [8, T], F32,
                                 kind="ExternalInput").ap()
    io["eu_in"] = nc.dram_tensor(f"{q}_eu", [8, 1], F32,
                                 kind="ExternalInput").ap()
    io["hd"] = nc.dram_tensor(f"{q}_hd{tag}", [32, 128, T], BF16,
                              kind="Internal").ap()
    io["hd_buf"] = Buf(io["hd"])
    io["icc_in"] = Buf(nc.dram_tensor(f"{q}_iccin{tag}", [T], F32,
                                      kind="Internal").ap())
    io["icc_out"] = Buf(nc.dram_tensor(f"{q}_iccout{tag}", [NCORES * T],
                                       F32, kind="Internal",
                                       addr_space="Shared").ap())
    io["acc2_in"] = Buf(nc.dram_tensor(f"{q}_acc2in{tag}", [4096], F32,
                                       kind="Internal").ap())
    io["acc2_out"] = Buf(nc.dram_tensor(f"{q}_acc2out{tag}", [4096], F32,
                                        kind="Internal",
                                        addr_space="Shared").ap())
    io["big8"] = c.scratch([8, T], F32)
    io["sc8"] = c.scratch([8, 16], F32)
    io["sca"] = c.scratch([1, 32], F32)
    io["s128"] = _subbuf(c.XC[3], 0, 128)
    io["sq128"] = _subbuf(c.XC[2], 0, 128)
    _mb = Buf(_ColShift(c.XC[2].h.bitcast(BF16), 256))
    _mb.w = c.XC[2].w
    _mb.r = c.XC[2].r
    io["mask128"] = _mb
    io["bis"] = _subbuf(c.XC[2], 192, 8)
    io["aff"] = c.scratch([128, 32], F32)
    io["affe"] = c.scratch([8, 8], F32)
    io["esel"] = c.scratch([8, 128], BF16)
    io["berow"] = c.scratch([1, 1024], BF16)
    vx = c.scratch([128, 270], F32)
    tk = p.dma("sp", vx.h[:, 0:270], io["vecs_in"], deps=vx.wdep(),
               key=vx.h.name)
    vx.wrote(tk)
    io["vecs"] = vx
    io["vo_gb1"] = 0
    io["vo_lns"] = 8
    io["vo_lno"] = 16
    io["vo_gb2"] = 24
    io["vo_gb3"] = 28
    io["vo_affb"] = 29
    io["vo_e0b1"] = 30
    io["vo_e1b1"] = 126
    io["vo_e1b2"] = 174
    io["vo_e2b1"] = 222
    w3t = c.scratch([128, 32], BF16)
    tk2 = p.dma("pool", w3t.h[:, 0:32], io["w3_in"], deps=w3t.wdep(),
                key=w3t.h.name)
    w3t.wrote(tk2)
    io["w3_t"] = w3t
    awt = c.scratch([128, 64], BF16)
    tk3 = p.dma("pool", awt.h[:, 0:64], io["affw_in"], deps=awt.wdep(),
                key=awt.h.name)
    awt.wrote(tk3)
    io["aff_w"] = awt
    eu = c.scratch([8, 2], F32)
    tk4 = p.dma("sp", eu.h[0:8, 0:1], io["eu_in"], deps=eu.wdep(),
                key=eu.h.name)
    eu.wrote(tk4)
    io["eu"] = eu
    return io


def l0_ffn_full(p, c, io, st, tag):
    parts = l0_ffn(p, c, io, st, tag)
    l0_ffn_gates_experts(p, c, io, st, parts)


def stage_l0_ffn(inputs, noise_full, pref="L0f"):
    def tile_do(w):
        Din, Hn = w.shape
        nd, nh = Din // 128, Hn // 128
        t = w.reshape(nd, 128, nh, 128).transpose(2, 1, 0, 3)
        return np.ascontiguousarray(t.reshape(nh, 128, nd * 128),
                                    dtype=np.float32)

    def tile_slab(w, n_ot):
        Hn, Dn = w.shape
        n_g, n_ob = Hn // 1024, n_ot // 4
        t = w.reshape(n_g, 8, 128, n_ob, 4, 128).transpose(3, 0, 2, 1, 4, 5)
        return np.ascontiguousarray(t.reshape(n_ob, n_g, 128, 4096),
                                    dtype=np.float32)

    q = pref
    vecs = np.zeros((128, 270), np.float32)
    vecs[:, 0:8] = inputs["L0_ffn_gate_b1"].reshape(8, 128).T
    vecs[:, 8:16] = inputs["L0_ffn_ln_scale"].reshape(8, 128).T
    vecs[:, 16:24] = inputs["L0_ffn_ln_offset"].reshape(8, 128).T
    vecs[:, 24:28] = inputs["L0_ffn_gate_b2"].reshape(4, 128).T
    vecs[0:8, 28] = inputs["L0_ffn_gate_b3"]
    vecs[0:8, 29] = inputs["L0_ffn_aff_b"]
    vecs[:, 30:126] = inputs["L0_ffn_e0_b1"].reshape(3 * 32, 128).T
    vecs[:, 126:174] = inputs["L0_ffn_e1_b1"].reshape(3 * 16, 128).T
    vecs[:, 174:222] = inputs["L0_ffn_e1_b2"].reshape(3 * 16, 128).T
    vecs[:, 222:270] = inputs["L0_ffn_e2_b1"].reshape(2 * 24, 128).T
    berows = np.zeros((8, 1024), np.float32)
    berows[0:3] = inputs["L0_ffn_e0_b2"]
    berows[3:6] = inputs["L0_ffn_e1_b3"]
    berows[6:8] = inputs["L0_ffn_e2_b2"]
    w3 = np.zeros((128, 32), np.float32)
    w3v = inputs["L0_ffn_gate_w3"]
    for dt in range(4):
        w3[:, dt * 8:(dt + 1) * 8] = w3v[dt * 128:(dt + 1) * 128, :]
    affw = np.zeros((128, 64), np.float32)
    aw = inputs["L0_ffn_aff_w"]
    for dt in range(8):
        affw[:, dt * 8:(dt + 1) * 8] = aw[dt * 128:(dt + 1) * 128, :]
    base = {
        f"{q}_gw1_t": tile_do(inputs["L0_ffn_gate_w1"]),
        f"{q}_gw2_s": tile_slab(inputs["L0_ffn_gate_w2"], 4),
        f"{q}_w3": w3,
        f"{q}_affw": affw,
        f"{q}_e0w1": np.concatenate([tile_do(inputs["L0_ffn_e0_w1"][k])
                                     for k in range(3)]),
        f"{q}_e0w2": np.concatenate([tile_slab(inputs["L0_ffn_e0_w2"][k], 8)
                                     for k in range(3)]),
        f"{q}_e1w1": np.concatenate([tile_do(inputs["L0_ffn_e1_w1"][k])
                                     for k in range(3)]),
        f"{q}_e1w2": np.concatenate([tile_slab(inputs["L0_ffn_e1_w2"][k], 16)
                                     for k in range(3)]),
        f"{q}_e1w3": np.concatenate([tile_slab(inputs["L0_ffn_e1_w3"][k], 8)
                                     for k in range(3)]),
        f"{q}_e2w1": np.concatenate([tile_do(inputs["L0_ffn_e2_w1"][k])
                                     for k in range(2)]),
        f"{q}_e2w2": np.concatenate([tile_slab(inputs["L0_ffn_e2_w2"][k], 8)
                                     for k in range(2)]),
        f"{q}_vecs": vecs,
        f"{q}_berows": berows,
        f"{q}_eu": inputs["L0_ffn_expert_usage"].reshape(8, 1).astype(np.float32),
    }
    per_core = []
    for cc in range(NCORES):
        nz = np.ascontiguousarray(
            noise_full[:, cc * SLOC:(cc + 1) * SLOC, :]
            .transpose(2, 0, 1).reshape(8, T), dtype=np.float32)
        per_core.append({f"{q}_noise": nz, **base})
    return per_core
